# version 23
# baseline (speedup 1.0000x reference)
import numpy as np
import ml_dtypes
import concourse.bass as bass
import concourse.mybir as mybir
from concourse.bass_utils import run_bass_kernel_spmd
from concourse.alu_op_type import AluOpType
from contextlib import ExitStack, contextmanager

F32 = mybir.dt.float32
BF16 = mybir.dt.bfloat16
AF = mybir.ActivationFunctionType
AX = mybir.AxisListType
Alu = AluOpType

D = 1024
HALO = 256
NSPEC = 4
EX = HALO + NSPEC
DH = 64
WINDOW = 256
SDIM = 128
KLEN = 4096
EPS = 1e-5
LN2EPS = 1e-5


def _blockify_lhs(W):
    K, M = W.shape
    assert K % 128 == 0 and M % 128 == 0
    return np.ascontiguousarray(
        W.reshape(K // 128, 128, M // 128, 128).transpose(2, 1, 0, 3)
    ).reshape(M // 128, 128, (K // 128) * 128)


def _np(x):
    return np.ascontiguousarray(np.asarray(x), dtype=np.float32)


def blk_list(ncol, NB):
    out = []
    off = 0
    while off < ncol:
        n = min(NB, ncol - off)
        rem = ncol - off - n
        if 0 < rem < 64:
            tot = n + rem
            n1 = (tot // 2 + 1) // 2 * 2
            out.append((off, n1))
            out.append((off + n1, tot - n1))
            off = ncol
            continue
        out.append((off, n))
        off += n
    return out



class Sem:
    __slots__ = ("h", "v")
    def __init__(self, h):
        self.h = h
        self.v = 0
    def inc(self, instr, n=1):
        instr.then_inc(self.h, n)
        self.v += n
        return instr
    def snap(self):
        return (self.h, self.v)


class B:
    def __init__(self, cfg):
        self.cfg = cfg
        self.nc = bass.Bass("TRN2", target_bir_lowering=False, detect_race_conditions=False)
        self.stack = ExitStack()
        self._names = {}
        self.sems = {}
        self.block = None
        self._consts = {}
        self._stage = None

    def _uname(self, base):
        i = self._names.get(base, 0)
        self._names[base] = i + 1
        return f"{base}_{i}" if i else base

    def sem(self, name):
        if name not in self.sems:
            self.sems[name] = Sem(self.stack.enter_context(
                self.nc.semaphore(self._uname("sm_" + name))))
        return self.sems[name]

    def sbuf(self, name, shape, dtype, scope=None, persist=False):
        if persist:
            return self.stack.enter_context(
                self.nc.sbuf_tensor(self._uname(name), list(shape), dtype))
        tgt = scope or self._stage or self.stack
        kw = {"side": "right"} if tgt is not self.stack else {}
        return tgt.enter_context(
            self.nc.sbuf_tensor(self._uname(name), list(shape), dtype, **kw))

    def psum(self, name, shape, dtype=F32, scope=None):
        shape = list(shape)
        elt = 4 if dtype == F32 else 2
        per_bank = 2048 // elt
        if shape[-1] % per_bank:
            shape[-1] = ((shape[-1] // per_bank) + 1) * per_bank
        t = (scope or self._stage or self.stack).enter_context(
            self.nc.psum_tensor(self._uname(name), shape, dtype))
        return t

    @contextmanager
    def stage(self):
        prev = self._stage
        es = ExitStack()
        self._stage = es
        try:
            yield es
        finally:
            self._stage = prev
            es.close()

    def dram_in(self, name, shape, dtype=F32):
        return self.nc.dram_tensor(name, list(shape), dtype, kind="ExternalInput").ap()

    def dram_out(self, name, shape, dtype=F32):
        return self.nc.dram_tensor(name, list(shape), dtype, kind="ExternalOutput").ap()

    def dram_tmp(self, name, shape, dtype=F32):
        return self.nc.dram_tensor(self._uname(name), list(shape), dtype).ap()

    def start(self):
        self.block = self.stack.enter_context(self.nc.Block())
        return self

    def sec(self, eng, fn):
        getattr(self.block, eng)(fn)

    def _fence_dummy(self):
        if "fd" not in self._consts:
            t = self.sbuf("fence_dummy", (128, 8), F32, persist=True)
            sm = self.sem("v")
            def f(v):
                sm.inc(v.memset(t[:, :], 0.0), 1)
                v.wait_ge(sm.h, sm.v)
            self.sec("vector", f)
            self._consts["fd"] = t
            self._consts["fd_init"] = sm.snap()
        return self._consts["fd"]

    def fence(self):
        fd = self._fence_dummy()
        init = self._consts["fd_init"]
        fs = {e: self.sem("fence_" + e) for e in ("vector", "scalar", "gpsimd")}
        def fv(v):
            fs["vector"].inc(v.tensor_copy(fd[:, 0:1], fd[:, 1:2]), 1)
        self.sec("vector", fv)
        def fsc(sc):
            sc.wait_ge(init[0], init[1])
            fs["scalar"].inc(sc.copy(fd[:, 2:3], fd[:, 3:4]), 1)
        self.sec("scalar", fsc)
        def fg(g):
            g.wait_ge(init[0], init[1])
            fs["gpsimd"].inc(g.tensor_copy(fd[:, 4:5], fd[:, 5:6]), 1)
        self.sec("gpsimd", fg)
        targets = [s.snap() for s in fs.values()]
        for dn in ("sdma", "gdma", "t"):
            if dn in self.sems:
                targets.append(self.sems[dn].snap())
        for e in ("tensor", "vector", "scalar", "gpsimd", "sync"):
            def mk(targets):
                def f(eng):
                    for h, v in targets:
                        if v:
                            eng.wait_ge(h, v)
                return f
            self.sec(e, mk(targets))

    def finish(self):
        self.stack.close()
        return self.nc

    def const_col(self, value):
        key = ("col", float(value))
        if key not in self._consts:
            t = self.sbuf(f"c_col", (128, 1), F32, persist=True)
            sm = self.sem("v")
            def f(v):
                sm.inc(v.memset(t[:, :], float(value)), 1)
                v.wait_ge(sm.h, sm.v)
            self.sec("vector", f)
            self._consts[key] = t
        return self._consts[key]

    def const_ones(self, p, n, dtype=BF16):
        key = ("ones", p, n, dtype)
        if key not in self._consts:
            t = self.sbuf(f"c_ones_{p}x{n}", (p, n), dtype, persist=True)
            sm = self.sem("v")
            def f(v):
                sm.inc(v.memset(t[:, :], 1.0), 1)
                v.wait_ge(sm.h, sm.v)
            self.sec("vector", f)
            self._consts[key] = t
        return self._consts[key]



class LnScratch:
    def __init__(self, b, NB):
        self.xb = b.sbuf("ln_xb", (128, 8 * NB), BF16)
        self.mu = b.sbuf("ln_mu", (128, NB), F32)
        self.rstd = b.sbuf("ln_rstd", (128, NB), F32)
        self.ps_s = b.psum("ln_ps_s", (128, NB))
        self.ps_q = b.psum("ln_ps_q", (128, NB))
        self.guard = None


def emit_ln(b, xT_ap_fn, ncol, xnT, sc, gain_col=None, bias_col=None):
    NB = b.cfg.NB
    ones = b.const_ones(128, 128)
    sv, ss, st = b.sem("v"), b.sem("s"), b.sem("t")
    nblk = (ncol + NB - 1) // NB
    for ib in range(nblk):
        off = ib * NB
        n = min(NB, ncol - off)
        guard = sc.guard
        def f(s, off=off, n=n):
            for cc in range(8):
                i = s.copy(sc.xb[:, cc * NB:cc * NB + n], xT_ap_fn(cc, off, n))
            ss.inc(i, 1)
        b.sec("scalar", f)
        def f(t, n=n, g=guard, ssv=ss.v):
            t.wait_ge(ss.h, ssv)
            if g:
                t.wait_ge(g[0], g[1])
            for cc in range(8):
                i = t.matmul(sc.ps_s[:, 0:n], ones[:, :], sc.xb[:, cc * NB:cc * NB + n],
                             start=(cc == 0), stop=(cc == 7))
            st.inc(i, 1)
        b.sec("tensor", f)
        def f(s, off=off, n=n, stv=st.v):
            s.wait_ge(st.h, stv)
            for cc in range(8):
                i = s.activation(sc.xb[:, cc * NB:cc * NB + n], xT_ap_fn(cc, off, n),
                                 AF.Square)
            ss.inc(i, 1)
        b.sec("scalar", f)
        def f(t, n=n, ssv=ss.v):
            t.wait_ge(ss.h, ssv)
            for cc in range(8):
                i = t.matmul(sc.ps_q[:, 0:n], ones[:, :], sc.xb[:, cc * NB:cc * NB + n],
                             start=(cc == 0), stop=(cc == 7))
            st.inc(i, 1)
        b.sec("tensor", f)
        def f(v, n=n, stv=st.v):
            v.wait_ge(st.h, stv)
            v.tensor_scalar(sc.mu[:, 0:n], sc.ps_s[:, 0:n], 1.0 / D, None, Alu.mult)
            v.tensor_tensor(sc.rstd[:, 0:n], sc.mu[:, 0:n], sc.mu[:, 0:n], Alu.mult)
            sv.inc(v.scalar_tensor_tensor(sc.rstd[:, 0:n], sc.ps_q[:, 0:n], 1.0 / D,
                                          sc.rstd[:, 0:n], Alu.mult, Alu.subtract), 1)
        b.sec("vector", f)
        def f(s, n=n, svv=sv.v):
            s.wait_ge(sv.h, svv)
            s.activation(sc.rstd[:, 0:n], sc.rstd[:, 0:n], AF.Ln, bias=b.const_col(LN2EPS)[:, :])
            ss.inc(s.activation(sc.rstd[:, 0:n], sc.rstd[:, 0:n], AF.Exp, scale=-0.5), 1)
        b.sec("scalar", f)
        def f(v, off=off, n=n, ssv=ss.v):
            v.wait_ge(ss.h, ssv)
            for cc in range(8):
                xo = xnT[:, cc * ncol + off: cc * ncol + off + n]
                v.tensor_tensor(sc.xb[:, cc * NB:cc * NB + n], xT_ap_fn(cc, off, n),
                                sc.mu[:, 0:n], Alu.subtract)
                if gain_col is None:
                    i = v.tensor_tensor(xo, sc.xb[:, cc * NB:cc * NB + n],
                                        sc.rstd[:, 0:n], Alu.mult)
                else:
                    v.tensor_tensor(sc.xb[:, cc * NB:cc * NB + n],
                                    sc.xb[:, cc * NB:cc * NB + n],
                                    sc.rstd[:, 0:n], Alu.mult)
                    i = v.tensor_scalar(xo, sc.xb[:, cc * NB:cc * NB + n],
                                        gain_col[:, cc:cc + 1], bias_col[:, cc:cc + 1],
                                        Alu.mult, Alu.add)
            sv.inc(i, 1)
        b.sec("vector", f)
        sc.guard = sv.snap()



def emit_proj(b, wblob, nK, nM, rhs_fn, ncol, out_cb, bias_row=None, wname="w"):
    NB = b.cfg.NB
    nkc = nK // 128
    nmt = nM // 128
    wsb = b.sbuf(f"w_{wname}", (128, 2 * nkc * 128), BF16)
    sg, st = b.sem("gdma"), b.sem("t")
    brow = None
    if bias_row is not None:
        brow = b.sbuf(f"wb_{wname}", (1, nM), BF16)
        def f(g):
            sg.inc(g.dma_start(out=brow[:, :], in_=bias_row), 16)
        b.sec("gpsimd", f)
    ones1 = b.const_ones(1, NB)
    nblk = (ncol + NB - 1) // NB
    wguard = [None, None]
    psguard = [None, None]
    pst = [b.psum(f"ps_{wname}", (128, NB)) for _ in range(2)]
    for mi in range(nmt):
        slot = mi % 2
        def f(g, mi=mi, slot=slot, wg=wguard[slot]):
            if wg:
                g.wait_ge(wg[0], wg[1])
            sg.inc(g.dma_start(out=wsb[:, slot * nkc * 128:(slot + 1) * nkc * 128],
                               in_=wblob[mi:mi + 1, :]), 16)
        b.sec("gpsimd", f)
        for ib in range(nblk):
            off = ib * NB
            n = min(NB, ncol - off)
            pi = (mi * nblk + ib) % 2
            ps = pst[pi]
            def f(t, mi=mi, slot=slot, off=off, n=n, ps=ps,
                  pg=psguard[pi], sgv=sg.v):
                t.wait_ge(sg.h, sgv)
                if pg:
                    t.wait_ge(pg[0], pg[1])
                for kc in range(nkc):
                    i = t.matmul(ps[:, 0:n],
                                 wsb[:, slot * nkc * 128 + kc * 128:
                                     slot * nkc * 128 + (kc + 1) * 128],
                                 rhs_fn(kc, off, n),
                                 start=(kc == 0),
                                 stop=(kc == nkc - 1 and brow is None))
                if brow is not None:
                    i = t.matmul(ps[:, 0:n], brow[:, mi * 128:(mi + 1) * 128],
                                 ones1[:, 0:n], start=False, stop=True)
                st.inc(i, 1)
            b.sec("tensor", f)
            psguard[pi] = out_cb(mi, off, n, ps)
        wguard[slot] = (st.h, st.v)



def emit_la1(b, x0T, hT_out, dbg_stop=None):
    cfg = b.cfg
    Tb = cfg.Tb
    NB2 = 512
    nblk = Tb // NB2
    sv, ss, st = b.sem("v"), b.sem("s"), b.sem("t")
    sd, sg = b.sem("sdma"), b.sem("gdma")
    ones128 = b.const_ones(128, 128)
    ones128f = b.const_ones(128, 128, F32)

    igw = b.dram_in("ig_w", (8, 128, 128))
    agw = b.dram_in("ag_w", (8, 128, 128))
    igb = b.dram_in("ig_bias_row", (1, D))
    agb = b.dram_in("ag_bias_row", (1, D))
    apar = b.dram_in("a_param_col", (128, 8))

    onesf = b.sbuf("onesf", (128, Tb), BF16)
    m8sp = b.sbuf("m8sp", (128, 8), F32)
    gwsb = b.sbuf("gwsb", (128, 16 * 128), BF16)
    gbsb = b.sbuf("gbsb", (1, 2 * D), BF16)
    mean_row = b.sbuf("mean_row", (1, Tb), BF16)
    rstd_row = b.sbuf("rstd_row", (1, Tb), BF16)
    rowtmp = b.sbuf("rowtmp", (1, Tb), BF16)
    sp_t = b.sbuf("sp_t", (128, 8), F32)
    sp_r = b.sbuf("sp_r", (128, 8), F32)
    scope1 = ExitStack()
    xblk = b.sbuf("xblk", (128, 8 * 512), F32, scope=scope1)
    xbb = b.sbuf("xbb", (128, 8 * 512), BF16, scope=scope1)
    ps_s = b.psum("ps_stat", (128, 512), scope=scope1)
    ps_q = b.psum("ps_stat2", (128, 512), scope=scope1)
    ev_s = b.sbuf("ev_stat", (128, 512), F32, scope=scope1)
    ev_q = b.sbuf("ev_stat2", (128, 512), F32, scope=scope1)

    def f(g):
        for cc in range(8):
            sg.inc(g.dma_start(out=gwsb[:, cc * 128:(cc + 1) * 128], in_=igw[cc]), 16)
            sg.inc(g.dma_start(out=gwsb[:, (8 + cc) * 128:(9 + cc) * 128], in_=agw[cc]), 16)
        sg.inc(g.dma_start(out=gbsb[:, 0:D], in_=igb), 16)
        sg.inc(g.dma_start(out=gbsb[:, D:2 * D], in_=agb), 16)
    b.sec("gpsimd", f)
    def f(sy):
        sd.inc(sy.dma_start(out=m8sp[:, :], in_=apar), 16)
    b.sec("sync", f)
    def f(v):
        v.wait_ge(sd.h, sd.v)
        v.memset(onesf[:, :], 1.0)
        sv.inc(v.tensor_scalar(sp_r[:, :], m8sp[:, :], 0.0, None, Alu.max), 1)
    b.sec("vector", f)
    def f(s, svv=sv.v):
        s.wait_ge(sv.h, svv)
        s.activation(sp_t[:, :], m8sp[:, :], AF.Abs)
        s.activation(sp_t[:, :], sp_t[:, :], AF.Exp, scale=-1.0)
        ss.inc(s.activation(sp_t[:, :], sp_t[:, :], AF.Ln, bias=b.const_col(1.0)[:, :]), 1)
    b.sec("scalar", f)
    def f(v, ssv=ss.v):
        v.wait_ge(ss.h, ssv)
        v.tensor_tensor(sp_t[:, :], sp_t[:, :], sp_r[:, :], Alu.add)
        sv.inc(v.tensor_scalar(m8sp[:, :], sp_t[:, :], -8.0, None, Alu.mult), 1)
    b.sec("vector", f)
    b.fence()
    if dbg_stop == 0:
        def f(g):
            sg.inc(g.dma_start(out=hT_out[0:128, :], in_=onesf[:, :]), 16)
            g.wait_ge(sg.h, sg.v)
        b.sec("gpsimd", f)
        return

    _n_iters = 1 if dbg_stop == 13 else Tb // 512
    SD1, SS1, ST1, SV1 = sd.v, ss.v, st.v, sv.v
    def f(sy, SS1=SS1, ssv=ss.v):
        sy.wait_ge(ss.h, ssv)
        for ib in range(_n_iters):
            off = ib * 512
            if ib >= 1:
                sy.wait_ge(ss.h, SS1 + 2 * ib)
            for cc in range(8):
                sd.inc(sy.dma_start(out=xblk[:, cc * 512:(cc + 1) * 512],
                                    in_=x0T[cc * 128:(cc + 1) * 128, off:off + 512]), 16)
    b.sec("sync", f)
    def f(s, SD1=SD1, ST1=ST1, stv=st.v):
        s.wait_ge(st.h, stv)
        for ib in range(_n_iters):
            s.wait_ge(sd.h, SD1 + 128 * (ib + 1))
            if ib >= 1:
                s.wait_ge(st.h, ST1 + 2 * ib)
            s.copy(xbb[:, :], xblk[:, :])
            ss.inc(s.activation(xblk[:, :], xblk[:, :], AF.Square), 1)
            s.wait_ge(st.h, ST1 + 2 * ib + 1)
            ss.inc(s.copy(xbb[:, :], xblk[:, :]), 1)
    b.sec("scalar", f)
    def f(t, SS1=SS1, SV1=SV1, svv=sv.v):
        t.wait_ge(sv.h, svv)
        for ib in range(_n_iters):
            t.wait_ge(ss.h, SS1 + 2 * ib + 1)
            if ib >= 1:
                t.wait_ge(sv.h, SV1 + ib)
            for cc in range(8):
                i = t.matmul(ps_s[:, :], ones128[:, :], xbb[:, cc * 512:(cc + 1) * 512],
                             start=(cc == 0), stop=(cc == 7))
            st.inc(i, 1)
            t.wait_ge(ss.h, SS1 + 2 * ib + 2)
            for cc in range(8):
                i = t.matmul(ps_q[:, :], ones128[:, :], xbb[:, cc * 512:(cc + 1) * 512],
                             start=(cc == 0), stop=(cc == 7))
            st.inc(i, 1)
    b.sec("tensor", f)
    def f(v, ST1=ST1):
        for ib in range(_n_iters):
            off = ib * 512
            v.wait_ge(st.h, ST1 + 2 * ib + 2)
            v.tensor_copy(ev_s[:, :], ps_s[:, :])
            v.tensor_copy(ev_q[:, :], ps_q[:, :])
            v.tensor_copy(mean_row[0:1, off:off + 512], ev_s[0:1, :])
            sv.inc(v.tensor_copy(rstd_row[0:1, off:off + 512], ev_q[0:1, :]), 1)
    b.sec("vector", f)
    b.fence()
    def f(v):
        with b.nc.allow_low_precision(reason="bf16 LN stat rows"):
            sv.inc(v.tensor_scalar(mean_row[0:1, :], mean_row[0:1, :], 1.0 / D, None,
                                   Alu.mult), 1)
    b.sec("vector", f)
    def f(s, svv=sv.v):
        s.wait_ge(sv.h, svv)
        ss.inc(s.activation(rowtmp[0:1, :], mean_row[0:1, :], AF.Square), 1)
    b.sec("scalar", f)
    def f(v, ssv=ss.v):
        v.wait_ge(ss.h, ssv)
        with b.nc.allow_low_precision(reason="bf16 LN stat rows"):
            sv.inc(v.scalar_tensor_tensor(rstd_row[0:1, :], rstd_row[0:1, :], 1.0 / D,
                                          rowtmp[0:1, :], Alu.mult, Alu.subtract), 1)
    b.sec("vector", f)
    def f(s, svv=sv.v):
        s.wait_ge(sv.h, svv)
        s.activation(rstd_row[0:1, :], rstd_row[0:1, :], AF.Ln, bias=b.const_col(EPS)[0:1, :])
        ss.inc(s.activation(rstd_row[0:1, :], rstd_row[0:1, :], AF.Exp, scale=-0.5), 1)
    b.sec("scalar", f)
    b.fence()
    if dbg_stop == 1 or (dbg_stop or 0) >= 10:
        def f(g):
            g.wait_ge(sg.h, sg.v)
        b.sec("gpsimd", f)
        scope1.close()
        return

    scope1.close()
    bA = b.sbuf("bufA", (128, Tb), F32)
    bX = b.sbuf("bufX", (128, Tb), F32)
    bP = b.sbuf("bufP", (128, Tb), F32)
    xnb = [b.sbuf("p2_xnb", (128, NB2), BF16) for _ in range(2)]
    gx = [b.sbuf("p2_gx", (128, NB2), F32) for _ in range(2)]
    tt1 = [b.sbuf("p2_t1", (128, NB2), F32) for _ in range(2)]
    tt2 = [b.sbuf("p2_t2", (128, NB2), F32) for _ in range(2)]
    xin = [b.sbuf("p2_x", (128, NB2), F32) for _ in range(2)]
    ps_r = [b.psum("ps_rep", (128, NB2)) for _ in range(2)]
    ps_r2 = [b.psum("ps_rep2", (128, NB2)) for _ in range(2)]
    ps_g = [b.psum("ps_g", (128, NB2)) for _ in range(2)]
    ps_g2 = [b.psum("ps_g2", (128, NB2)) for _ in range(2)]
    onesb = b.const_ones(1, NB2)
    g_f1 = [None, None]
    g_sig = [None, None]
    g_fin = [None, None]
    for cc in range(8):
        for pr in range(nblk // 2):
            offs = [(2 * pr) * NB2, (2 * pr + 1) * NB2]
            def f(sy, cc=cc, offs=offs, g0=g_f1[0], g1=g_f1[1]):
                for p, (off, g) in enumerate(zip(offs, (g0, g1))):
                    if g:
                        sy.wait_ge(g[0], g[1])
                    sd.inc(sy.dma_start(out=xin[p][:, :],
                                        in_=x0T[cc * 128:(cc + 1) * 128,
                                                off:off + NB2]), 16)
            b.sec("sync", f)
            def f(t, offs=offs, g0=g_f1[0], g1=g_f1[1]):
                for p, (off, g) in enumerate(zip(offs, (g0, g1))):
                    if g:
                        t.wait_ge(g[0], g[1])
                    t.matmul(ps_r[p][:, :], ones128[0:1, :],
                             mean_row[0:1, off:off + NB2], start=True, stop=True)
                    i = t.matmul(ps_r2[p][:, :], ones128[0:1, :],
                                 rstd_row[0:1, off:off + NB2], start=True, stop=True)
                    st.inc(i, 1)
            b.sec("tensor", f)
            def f(v, sdv=sd.v, stv=st.v):
                v.wait_ge(sd.h, sdv)
                v.wait_ge(st.h, stv)
                for p in range(2):
                    v.tensor_tensor(tt1[p][:, :], xin[p][:, :], ps_r[p][:, :],
                                    Alu.subtract)
                    i = v.tensor_tensor(xnb[p][:, :], tt1[p][:, :], ps_r2[p][:, :],
                                        Alu.mult)
                    sv.inc(i, 1)
            b.sec("vector", f)
            g_f1[0] = (sv.h, sv.v - 1)
            g_f1[1] = sv.snap()
            def f(t, cc=cc, svv=sv.v, g0=g_sig[0], g1=g_sig[1]):
                t.wait_ge(sv.h, svv)
                for p, g in enumerate((g0, g1)):
                    if g:
                        t.wait_ge(g[0], g[1])
                    t.matmul(ps_g[p][:, :], gwsb[:, cc * 128:(cc + 1) * 128],
                             xnb[p][:, :], start=True, stop=False)
                    t.matmul(ps_g[p][:, :], gbsb[:, cc * 128:(cc + 1) * 128],
                             onesb[:, 0:NB2], start=False, stop=True)
                    t.matmul(ps_g2[p][:, :], gwsb[:, (8 + cc) * 128:(9 + cc) * 128],
                             xnb[p][:, :], start=True, stop=False)
                    i = t.matmul(ps_g2[p][:, :], gbsb[:, D + cc * 128:D + (cc + 1) * 128],
                                 onesb[:, 0:NB2], start=False, stop=True)
                    st.inc(i, 1)
            b.sec("tensor", f)
            st_g = st.v
            ss_b = ss.v
            sv_b = sv.v
            def fsc(s, stg=st_g, g0=g_fin[0], g1=g_fin[1], offs=offs, svb=sv_b):
                for p, g in enumerate((g0, g1)):
                    s.wait_ge(st.h, stg - 1 + p)
                    if g:
                        s.wait_ge(g[0], g[1])
                    s.activation(gx[p][:, :], ps_g[p][:, :], AF.Sigmoid)
                    ss.inc(s.activation(tt1[p][:, :], ps_g2[p][:, :], AF.Sigmoid), 1)
                for p in range(2):
                    s.wait_ge(sv.h, svb + 1 + p)
                    ss.inc(s.activation(tt2[p][:, :], bA[:, offs[p]:offs[p] + NB2],
                                        AF.Exp, scale=2.0), 1)
                for p in range(2):
                    s.wait_ge(sv.h, svb + 3 + p)
                    s.activation(tt2[p][:, :], tt2[p][:, :], AF.Ln)
                    ss.inc(s.activation(tt2[p][:, :], tt2[p][:, :], AF.Exp, scale=0.5), 1)
            b.sec("scalar", fsc)
            g_sig[0] = (ss.h, ss_b + 1)
            g_sig[1] = (ss.h, ss_b + 2)
            def fv1(v, cc=cc, offs=offs, ssb=ss_b):
                for p in range(2):
                    v.wait_ge(ss.h, ssb + 1 + p)
                    sv.inc(v.tensor_scalar(bA[:, offs[p]:offs[p] + NB2], tt1[p][:, :],
                                           m8sp[:, cc:cc + 1], None, Alu.mult), 1)
            b.sec("vector", fv1)
            def fv2(v, ssb=ss_b):
                for p in range(2):
                    v.wait_ge(ss.h, ssb + 3 + p)
                    v.tensor_scalar(tt2[p][:, :], tt2[p][:, :], -1.0, 1.0,
                                    Alu.mult, Alu.add)
                    sv.inc(v.tensor_scalar(tt2[p][:, :], tt2[p][:, :], 1e-7, None,
                                           Alu.max), 1)
            b.sec("vector", fv2)
            def fv3(v, offs=offs, ssb=ss_b):
                for p in range(2):
                    v.wait_ge(ss.h, ssb + 5 + p)
                    v.tensor_tensor(tt1[p][:, :], xnb[p][:, :], gx[p][:, :], Alu.mult)
                    sv.inc(v.tensor_tensor(bX[:, offs[p]:offs[p] + NB2], tt1[p][:, :],
                                           tt2[p][:, :], Alu.mult), 1)
            b.sec("vector", fv3)
            g_fin[0] = (sv.h, sv.v - 1)
            g_fin[1] = sv.snap()
        if dbg_stop == 2:
            def f(v, cc=cc, svv=sv.v):
                v.wait_ge(sv.h, svv)
                sv.inc(v.tensor_copy(bP[:, :], bX[:, :]), 1)
            b.sec("vector", f)
            def f(g, cc=cc, svv=sv.v):
                g.wait_ge(sv.h, svv)
                sg.inc(g.dma_start(out=hT_out[cc * 128:(cc + 1) * 128, :], in_=bP[:, :]), 16)
            b.sec("gpsimd", f)
            continue
        def f(v, cc=cc, svv=sv.v):
            v.wait_ge(sv.h, svv)
            v.tensor_tensor_scan(bP[:, :], onesf[:, :], bA[:, :], 0.0,
                                 Alu.mult, Alu.add)
            sv.inc(v.tensor_scalar(bA[:, 1:Tb], bP[:, 0:Tb - 1], -16.11809565095832,
                                   None, Alu.max), 1)
            sv.inc(v.memset(bA[:, 0:1], 0.0), 1)
        b.sec("vector", f)
        def f(s, svv=sv.v):
            s.wait_ge(sv.h, svv)
            ss.inc(s.activation(bP[:, :], bA[:, :], AF.Exp, scale=-1.0), 1)
        b.sec("scalar", f)
        def f(v, ssv=ss.v):
            v.wait_ge(ss.h, ssv)
            v.tensor_tensor(bP[:, :], bX[:, :], bP[:, :], Alu.mult)
            sv.inc(v.tensor_tensor_scan(bX[:, :], onesf[:, :], bP[:, :], 0.0,
                                        Alu.mult, Alu.add), 1)
        b.sec("vector", f)
        def f(s, svv=sv.v):
            s.wait_ge(sv.h, svv)
            ss.inc(s.activation(bA[:, :], bA[:, :], AF.Exp), 1)
        b.sec("scalar", f)
        def f(v, ssv=ss.v):
            v.wait_ge(ss.h, ssv)
            sv.inc(v.tensor_tensor(bP[:, :], bA[:, :], bX[:, :], Alu.mult), 1)
        b.sec("vector", f)
        def f(g, cc=cc, svv=sv.v):
            g.wait_ge(sv.h, svv)
            sg.inc(g.dma_start(out=hT_out[cc * 128:(cc + 1) * 128, :], in_=bP[:, :]), 16)
        b.sec("gpsimd", f)
    def f(g):
        g.wait_ge(sg.h, sg.v)
    b.sec("gpsimd", f)
    b.fence()





def emit_resadd_hbm(b, dst, srcA, srcB, ncol):
    scope = ExitStack()
    sv, sd, sg = b.sem("v"), b.sem("sdma"), b.sem("gdma")
    ta = [b.sbuf("ra_a", (128, ncol), F32, scope=scope) for _ in range(2)]
    tb_ = [b.sbuf("ra_b", (128, ncol), F32, scope=scope) for _ in range(2)]
    SD0, SV0, SG0 = sd.v, sv.v, sg.v
    def fl(sy, svv=sv.v, sgv=sg.v, SG0=SG0):
        sy.wait_ge(sv.h, svv)
        sy.wait_ge(sg.h, sgv)
        for cc in range(8):
            p = cc % 2
            if cc >= 2:
                sy.wait_ge(sg.h, SG0 + 16 * (cc - 1))
            sd.inc(sy.dma_start(out=ta[p][:, :], in_=srcA[cc * 128:(cc + 1) * 128, :]), 16)
            sd.inc(sy.dma_start(out=tb_[p][:, :], in_=srcB[cc * 128:(cc + 1) * 128, :]), 16)
    b.sec("sync", fl)
    def fv(v, SD0=SD0):
        for cc in range(8):
            p = cc % 2
            v.wait_ge(sd.h, SD0 + 32 * (cc + 1))
            sv.inc(v.tensor_tensor(ta[p][:, :], ta[p][:, :], tb_[p][:, :], Alu.add), 1)
    b.sec("vector", fv)
    def fg(g, SV0=SV0):
        for cc in range(8):
            p = cc % 2
            g.wait_ge(sv.h, SV0 + cc + 1)
            sg.inc(g.dma_start(out=dst[cc * 128:(cc + 1) * 128, :], in_=ta[p][:, :]), 16)
    b.sec("gpsimd", fg)
    b.fence()
    scope.close()


class LnStream:
    def __init__(self, b, NB, nbuf=2):
        self.b = b
        self.NB = NB
        self.nbuf = nbuf
        self.x_nbs = [b.sbuf("lns_x", (128, 8 * NB), F32) for _ in range(nbuf)]
        self.xbs = [b.sbuf("lns_xb", (128, 8 * NB), BF16) for _ in range(nbuf)]
        self.xns = [b.sbuf("lns_xn", (128, 8 * NB), BF16) for _ in range(nbuf)]
        self.mus = [b.sbuf("lns_mu", (128, NB), F32) for _ in range(nbuf)]
        self.rstds = [b.sbuf("lns_rstd", (128, NB), F32) for _ in range(nbuf)]
        self.ps_sq = [b.psum("lns_ps_sq", (128, 2 * NB)) for _ in range(nbuf)]
        self.x_nb, self.xb, self.xn = self.x_nbs[0], self.xbs[0], self.xns[0]
        self.guard = None
        self.guards = [None] * nbuf
        self.stats_done = [None] * nbuf
        self.ps_free = [None] * nbuf
        self.xnb_free = [None] * nbuf
        self.blkcnt = 0
        self.last_p = None

    def block(self, xcur, ncol, off, n, gain_col=None, bias_col=None, xn_dst=None):
        b, NB = self.b, self.NB
        ones = b.const_ones(128, 128)
        sv, ss, st, sd = b.sem("v"), b.sem("s"), b.sem("t"), b.sem("sdma")
        p = self.blkcnt % self.nbuf
        self.blkcnt += 1
        if self.last_p is not None:
            self.guards[self.last_p] = self.guard
        self.last_p = p
        self.x_nb, self.xb, self.xn = self.x_nbs[p], self.xbs[p], self.xns[p]
        mu, rstd = self.mus[p], self.rstds[p]
        self.mu, self.rstd = mu, rstd
        ps_s, ps_q = self.ps_sq[p][:, 0:NB], self.ps_sq[p][:, NB:2 * NB]
        g0 = self.guards[p]
        gnb = self.xnb_free[p]
        def fl(sy, off=off, n=n, g=g0, gnb=gnb):
            if g:
                sy.wait_ge(g[0], g[1])
            if gnb:
                sy.wait_ge(gnb[0], gnb[1])
            for cc in range(8):
                sd.inc(sy.dma_start(out=self.x_nb[:, cc * NB:cc * NB + n],
                                    in_=xcur[cc * 128:(cc + 1) * 128, off:off + n]), 16)
        b.sec("sync", fl)
        def fs(s, n=n, sdv=sd.v, xb=self.xb, x_nb=self.x_nb, gst=self.stats_done[p],
               gnb=gnb, g0=g0):
            s.wait_ge(sd.h, sdv)
            if gst:
                s.wait_ge(gst[0], gst[1])
            if gnb:
                s.wait_ge(gnb[0], gnb[1])
            if g0:
                s.wait_ge(g0[0], g0[1])
            for cc in range(8):
                i = s.copy(xb[:, cc * NB:cc * NB + n], x_nb[:, cc * NB:cc * NB + n])
            ss.inc(i, 1)
        b.sec("scalar", fs)
        def ft(t, n=n, ssv=ss.v, xb=self.xb, ps_s=ps_s, gpf=self.ps_free[p]):
            t.wait_ge(ss.h, ssv)
            if gpf:
                t.wait_ge(gpf[0], gpf[1])
            for cc in range(8):
                i = t.matmul(ps_s[:, 0:n], ones[:, :], xb[:, cc * NB:cc * NB + n],
                             start=(cc == 0), stop=(cc == 7))
            st.inc(i, 1)
        b.sec("tensor", ft)
        def fs(s, n=n, stv=st.v, xb=self.xb, x_nb=self.x_nb):
            s.wait_ge(st.h, stv)
            for cc in range(8):
                i = s.activation(xb[:, cc * NB:cc * NB + n],
                                 x_nb[:, cc * NB:cc * NB + n], AF.Square)
            ss.inc(i, 1)
        b.sec("scalar", fs)
        def ft(t, n=n, ssv=ss.v, xb=self.xb, ps_q=ps_q):
            t.wait_ge(ss.h, ssv)
            for cc in range(8):
                i = t.matmul(ps_q[:, 0:n], ones[:, :], xb[:, cc * NB:cc * NB + n],
                             start=(cc == 0), stop=(cc == 7))
            st.inc(i, 1)
        b.sec("tensor", ft)
        self.stats_done[p] = (st.h, st.v)
        def fv(v, n=n, stv=st.v, mu=mu, rstd=rstd, ps_s=ps_s, ps_q=ps_q):
            v.wait_ge(st.h, stv)
            v.tensor_scalar(mu[:, 0:n], ps_s[:, 0:n], 1.0 / D, None, Alu.mult)
            v.tensor_tensor(rstd[:, 0:n], mu[:, 0:n], mu[:, 0:n], Alu.mult)
            sv.inc(v.scalar_tensor_tensor(rstd[:, 0:n], ps_q[:, 0:n], 1.0 / D,
                                          rstd[:, 0:n], Alu.mult, Alu.subtract), 1)
        b.sec("vector", fv)
        self.ps_free[p] = (sv.h, sv.v)
        def fs(s, n=n, svv=sv.v, rstd=rstd):
            s.wait_ge(sv.h, svv)
            s.activation(rstd[:, 0:n], rstd[:, 0:n], AF.Ln,
                         bias=b.const_col(EPS)[:, :])
            ss.inc(s.activation(rstd[:, 0:n], rstd[:, 0:n], AF.Exp, scale=-0.5), 1)
        b.sec("scalar", fs)
        def fv(v, off=off, n=n, ssv=ss.v, gain_col=gain_col, bias_col=bias_col,
               xn_dst=xn_dst, xb=self.xb, x_nb=self.x_nb, xn=self.xn,
               mu=mu, rstd=rstd):
            v.wait_ge(ss.h, ssv)
            for cc in range(8):
                if xn_dst is None:
                    xo = xn[:, cc * NB:cc * NB + n]
                else:
                    xo = xn_dst[0][:, cc * xn_dst[1] + off:cc * xn_dst[1] + off + n]
                v.tensor_tensor(xb[:, cc * NB:cc * NB + n],
                                x_nb[:, cc * NB:cc * NB + n],
                                mu[:, 0:n], Alu.subtract)
                if gain_col is None:
                    i = v.tensor_tensor(xo, xb[:, cc * NB:cc * NB + n],
                                        rstd[:, 0:n], Alu.mult)
                else:
                    v.tensor_tensor(xb[:, cc * NB:cc * NB + n],
                                    xb[:, cc * NB:cc * NB + n],
                                    rstd[:, 0:n], Alu.mult)
                    i = v.tensor_scalar(xo, xb[:, cc * NB:cc * NB + n],
                                        gain_col[:, cc:cc + 1], bias_col[:, cc:cc + 1],
                                        Alu.mult, Alu.add)
            sv.inc(i, 1)
        b.sec("vector", fv)
        self.xnb_free[p] = (sv.h, sv.v)


def emit_ffn0_stage(b, xsrc, xdst, cpblob, w1blob, w2blob, ncol):
    NB = b.cfg.NB
    sv, ss, st, sg, sd = b.sem("v"), b.sem("s"), b.sem("t"), b.sem("gdma"), b.sem("sdma")
    scope = ExitStack()
    xn_hbm = b.dram_tmp("f0_xnh", (D, ncol), BF16)
    blks = blk_list(ncol, NB)
    lnscope = ExitStack()
    prev = b._stage
    b._stage = lnscope
    lns = LnStream(b, NB)
    b._stage = prev
    for (off, n) in blks:
        lns.block(xsrc, ncol, off, n)
        def fxo(g, off=off, n=n, svv=sv.v):
            g.wait_ge(sv.h, svv)
            for cc in range(8):
                sg.inc(g.dma_start(out=xn_hbm[cc * 128:(cc + 1) * 128, off:off + n],
                                   in_=lns.xn[:, cc * NB:cc * NB + n]), 16)
        b.sec("gpsimd", fxo)
        lns.guard = sg.snap()
    b.fence()
    lnscope.close()
    w = {
        "w1sb": b.sbuf("f0_w1", (128, 32 * 128), BF16, scope=scope),
        "w2sb": b.sbuf("f0_w2", (128, 128 * 128), BF16, scope=scope),
        "cpsb": [b.sbuf("f0_cp", (128, 128 * 128), BF16, scope=scope) for _ in range(2)],
        "ysb": b.sbuf("f0_y", (128, 32 * NB), BF16, scope=scope),
        "hsb": b.sbuf("f0_h", (128, 128 * NB), BF16, scope=scope),
        "x_t": b.sbuf("f0_x", (128, 8 * NB), F32, scope=scope),
        "xnb": b.sbuf("f0_xnb", (128, 8 * NB), BF16, scope=scope),
        "ps_y": [b.psum("f0_psy", (128, NB), scope=scope) for _ in range(2)],
        "ps_z": [b.psum("f0_psz", (128, 4 * NB), scope=scope) for _ in range(2)],
        "ps_o": b.psum("f0_pso", (128, NB), scope=scope),
    }
    def fl(g):
        sg.inc(g.dma_start(out=w["w1sb"][:, :], in_=w1blob), 16)
        sg.inc(g.dma_start(out=w["w2sb"][:, :], in_=w2blob), 16)
        sg.inc(g.dma_start(out=w["cpsb"][0][:, :], in_=cpblob[0]), 16)
    b.sec("gpsimd", fl)
    gy = None
    gz = [None, None]
    go = None
    gcp = [None, None]
    gx = None
    cpi = 1
    for ib, (off, n) in enumerate(blks):
        def fx(sy, off=off, n=n, g=gx, gyv=gy):
            if g:
                sy.wait_ge(g[0], g[1])
            if gyv:
                sy.wait_ge(gyv[0], gyv[1])
            for mi in range(8):
                sd.inc(sy.dma_start(out=w["x_t"][:, mi * NB:mi * NB + n],
                                    in_=xsrc[mi * 128:(mi + 1) * 128, off:off + n]), 16)
                sd.inc(sy.dma_start(out=w["xnb"][:, mi * NB:mi * NB + n],
                                    in_=xn_hbm[mi * 128:(mi + 1) * 128, off:off + n]), 16)
        b.sec("sync", fx)
        STY0 = st.v
        SVY0 = sv.v
        def fty(t, n=n, g=gy, sgv=sg.v, sdv=sd.v, SVY0=SVY0):
            t.wait_ge(sg.h, sgv)
            t.wait_ge(sd.h, sdv)
            if g:
                t.wait_ge(g[0], g[1])
            for yt in range(32):
                if yt >= 2:
                    t.wait_ge(sv.h, SVY0 + yt - 1)
                cc = yt % 8
                st.inc(t.matmul(w["ps_y"][yt % 2][:, 0:n],
                                w["w1sb"][:, yt * 128:(yt + 1) * 128],
                                w["xnb"][:, cc * NB:cc * NB + n],
                                start=True, stop=True), 1)
        b.sec("tensor", fty)
        def fvy(v, n=n, STY0=STY0):
            for yt in range(32):
                v.wait_ge(st.h, STY0 + yt + 1)
                sv.inc(v.tensor_copy(w["ysb"][:, yt * NB:yt * NB + n],
                                     w["ps_y"][yt % 2][:, 0:n]), 1)
        b.sec("vector", fvy)
        gy = sv.snap()
        ss_base = ss.v
        gel_jobs = []
        for grp in range(32):
            pi = grp % 2
            g = (ss.h, ss_base + grp - 1) if grp >= 2 else gz[pi]
            def ft(t, grp=grp, n=n, pi=pi, g=g, svv=sv.v):
                t.wait_ge(sv.h, svv)
                if g:
                    t.wait_ge(g[0], g[1])
                for a in range(4):
                    kc = grp * 4 + a
                    hq, cc = kc // 32, (kc % 32) // 4
                    yt = hq * 8 + cc
                    i = t.matmul(w["ps_z"][pi][:, a * NB:a * NB + n],
                                 w["w2sb"][:, kc * 128:(kc + 1) * 128],
                                 w["ysb"][:, yt * NB:yt * NB + n], start=True, stop=True)
                st.inc(i, 1)
            b.sec("tensor", ft)
            gel_jobs.append((st.v, grp, pi))
        def fs_all(s, jobs=gel_jobs, n=n):
            for (stv, grp, pi) in jobs:
                s.wait_ge(st.h, stv)
                if n == NB:
                    i = s.activation(w["hsb"][:, grp * 4 * NB:(grp + 1) * 4 * NB],
                                     w["ps_z"][pi][:, :], AF.Gelu)
                else:
                    for a in range(4):
                        kc = grp * 4 + a
                        i = s.activation(w["hsb"][:, kc * NB:kc * NB + n],
                                         w["ps_z"][pi][:, a * NB:a * NB + n], AF.Gelu)
                ss.inc(i, 1)
        b.sec("scalar", fs_all)
        gz[0] = (ss.h, ss_base + 31)
        gz[1] = (ss.h, ss_base + 32)
        for mi in range(8):
            use = mi % 2
            sg_use = sg.v
            pf = 1 - use
            def fg(g, mi=mi, pf=pf, wg=gcp[pf]):
                if wg:
                    g.wait_ge(wg[0], wg[1])
                sg.inc(g.dma_start(out=w["cpsb"][pf][:, :], in_=cpblob[(mi + 1) % 8]), 16)
            b.sec("gpsimd", fg)
            def ft(t, mi=mi, n=n, use=use, g=go, sgv=sg_use, ssv=ss.v, ssb=ss_base):
                t.wait_ge(sg.h, sgv)
                if mi != 0:
                    t.wait_ge(ss.h, ssv)
                if g:
                    t.wait_ge(g[0], g[1])
                for kc in range(128):
                    if mi == 0 and kc % 4 == 0:
                        t.wait_ge(ss.h, ssb + kc // 4 + 1)
                    i = t.matmul(w["ps_o"][:, 0:n],
                                 w["cpsb"][use][:, kc * 128:(kc + 1) * 128],
                                 w["hsb"][:, kc * NB:kc * NB + n],
                                 start=(kc == 0), stop=(kc == 127))
                st.inc(i, 1)
            b.sec("tensor", ft)
            gcp[use] = (st.h, st.v)
            def fv(v, mi=mi, n=n, stv=st.v, sdv=sd.v):
                v.wait_ge(st.h, stv)
                v.wait_ge(sd.h, sdv)
                sv.inc(v.tensor_tensor(w["x_t"][:, mi * NB:mi * NB + n],
                                       w["x_t"][:, mi * NB:mi * NB + n],
                                       w["ps_o"][:, 0:n], Alu.add), 1)
            b.sec("vector", fv)
            go = sv.snap()
            def fg(g, mi=mi, off=off, n=n, svv=sv.v):
                g.wait_ge(sv.h, svv)
                sg.inc(g.dma_start(out=xdst[mi * 128:(mi + 1) * 128, off:off + n],
                                   in_=w["x_t"][:, mi * NB:mi * NB + n]), 16)
            b.sec("gpsimd", fg)
        gx = sg.snap()
    b.fence()
    scope.close()


def emit_ffn0_block(b, lns, st_weights, xcur, ncol, off, n, dbgfence=0, xsrc=None):
    w = st_weights
    sv, ss, st, sg, sd = b.sem("v"), b.sem("s"), b.sem("t"), b.sem("gdma"), b.sem("sdma")
    NB = b.cfg.NB
    for yt in range(32):
        def ft(t, yt=yt, n=n, g=w["gy"][0], sgv=sg.v, svv=sv.v):
            t.wait_ge(sg.h, sgv)
            t.wait_ge(sv.h, svv)
            if g:
                t.wait_ge(g[0], g[1])
            cc = yt % 8
            st.inc(t.matmul(w["ps_y"][0][:, 0:n], w["w1sb"][:, yt * 128:(yt + 1) * 128],
                            lns.xn[:, cc * NB:cc * NB + n], start=True, stop=True), 1)
        b.sec("tensor", ft)
        def fv(v, yt=yt, n=n, stv=st.v):
            v.wait_ge(st.h, stv)
            sv.inc(v.tensor_copy(w["ysb"][:, yt * NB:yt * NB + n], w["ps_y"][0][:, 0:n]), 1)
        b.sec("vector", fv)
        w["gy"][0] = sv.snap()
    if dbgfence & 1:
        b.fence()
    for grp in range(32):
        pi = grp % 2
        def ft(t, grp=grp, n=n, pi=pi, g=w["gz"][pi], svv=sv.v):
            t.wait_ge(sv.h, svv)
            if g:
                t.wait_ge(g[0], g[1])
            for a in range(4):
                kc = grp * 4 + a
                hq, cc = kc // 32, (kc % 32) // 4
                yt = hq * 8 + cc
                i = t.matmul(w["ps_z"][pi][:, a * NB:a * NB + n],
                             w["w2sb"][:, kc * 128:(kc + 1) * 128],
                             w["ysb"][:, yt * NB:yt * NB + n], start=True, stop=True)
            st.inc(i, 1)
        b.sec("tensor", ft)
        def fs(s, grp=grp, n=n, pi=pi, stv=st.v):
            s.wait_ge(st.h, stv)
            if n == NB:
                i = s.activation(w["hsb"][:, grp * 4 * NB:(grp + 1) * 4 * NB],
                                 w["ps_z"][pi][:, :], AF.Gelu)
            else:
                for a in range(4):
                    kc = grp * 4 + a
                    i = s.activation(w["hsb"][:, kc * NB:kc * NB + n],
                                     w["ps_z"][pi][:, a * NB:a * NB + n], AF.Gelu)
            ss.inc(i, 1)
        b.sec("scalar", fs)
        w["gz"][pi] = ss.snap()
    if dbgfence & 2:
        b.fence()
    for mi in range(8):
        slot = w["cpi"]
        w["cpi"] = 1 - slot
        def fg(g, mi=mi, slot=slot, wg=w["gcp"][slot]):
            if wg:
                g.wait_ge(wg[0], wg[1])
            sg.inc(g.dma_start(out=w["cpsb"][slot][:, :], in_=w["cpblob"][mi]), 16)
        b.sec("gpsimd", fg)
        def ft(t, mi=mi, n=n, slot=slot, g=w["go"][0], sgv=sg.v, ssv=ss.v):
            t.wait_ge(sg.h, sgv)
            t.wait_ge(ss.h, ssv)
            if g:
                t.wait_ge(g[0], g[1])
            for kc in range(128):
                i = t.matmul(w["ps_o"][0][:, 0:n],
                             w["cpsb"][slot][:, kc * 128:(kc + 1) * 128],
                             w["hsb"][:, kc * NB:kc * NB + n],
                             start=(kc == 0), stop=(kc == 127))
            st.inc(i, 1)
        b.sec("tensor", ft)
        w["gcp"][slot] = (st.h, st.v)
        def fv(v, mi=mi, n=n, stv=st.v):
            v.wait_ge(st.h, stv)
            sv.inc(v.tensor_tensor(lns.x_nb[:, mi * NB:mi * NB + n],
                                   lns.x_nb[:, mi * NB:mi * NB + n],
                                   w["ps_o"][0][:, 0:n], Alu.add), 1)
        b.sec("vector", fv)
        w["go"][0] = sv.snap()
        def fg(g, mi=mi, off=off, n=n, svv=sv.v):
            g.wait_ge(sv.h, svv)
            sg.inc(g.dma_start(out=xcur[mi * 128:(mi + 1) * 128, off:off + n],
                               in_=lns.x_nb[:, mi * NB:mi * NB + n]), 16)
        b.sec("gpsimd", fg)


def mk_ffn0_state(b, cpblob, w1blob, w2blob, scope):
    NB = b.cfg.NB
    w = {
        "w1sb": b.sbuf("f0_w1", (128, 32 * 128), BF16, scope=scope),
        "w2sb": b.sbuf("f0_w2", (128, 128 * 128), BF16, scope=scope),
        "cpsb": [b.sbuf("f0_cp", (128, 128 * 128), BF16, scope=scope) for _ in range(2)],
        "ysb": b.sbuf("f0_y", (128, 32 * NB), BF16, scope=scope),
        "hsb": b.sbuf("f0_h", (128, 128 * NB), BF16, scope=scope),
        "ps_y": [b.psum("f0_psy", (128, NB), scope=scope)],
        "ps_z": [b.psum("f0_psz", (128, 4 * NB), scope=scope) for _ in range(2)],
        "ps_o": [b.psum("f0_pso", (128, NB), scope=scope)],
        "cpblob": cpblob,
        "gy": [None, None], "gz": [None, None], "go": [None, None],
        "gcp": [None, None], "cpi": 0,
    }
    sg = b.sem("gdma")
    def fl(g):
        sg.inc(g.dma_start(out=w["w1sb"][:, :], in_=w1blob[:, :]), 16)
        for half in range(4):
            sg.inc(g.dma_start(
                out=w["w2sb"][:, half * 32 * 128:(half + 1) * 32 * 128],
                in_=w2blob[:, half * 32 * 128:(half + 1) * 32 * 128]), 16)
    b.sec("gpsimd", fl)
    return w


def emit_qkv_block(b, lns, qw, off, n, tpc, ncol, do_q=True):
    sv, ss, st, sg, sd = b.sem("v"), b.sem("s"), b.sem("t"), b.sem("gdma"), b.sem("sdma")
    NB = b.cfg.NB
    ones1 = b.const_ones(1, NB)
    ones1x128 = b.const_ones(1, 128)
    for which in ("q", "k"):
        if which == "q" and not do_q:
            continue
        if which == "k" and off >= tpc:
            continue
        nn = n if which == "q" else min(n, tpc - off)
        wsb = qw["wqsb"] if which == "q" else qw["wksb"]
        bsb = qw["bqsb"] if which == "q" else qw["bksb"]
        pi0 = qw["pi"]
        qw["pi"] = (pi0 + 8) % 2
        STA0 = st.v
        SVB0 = sv.v
        def ftA(t, nn=nn, pi0=pi0, sgv=sg.v, wsb=wsb, bsb=bsb, svv=sv.v,
                gqk=tuple(qw["gqk"]), SVB0=SVB0):
            t.wait_ge(sg.h, sgv)
            t.wait_ge(sv.h, svv)
            for mi in range(8):
                pi = (pi0 + mi) % 2
                if mi >= 2:
                    t.wait_ge(sv.h, SVB0 + mi - 1)
                elif gqk[pi]:
                    t.wait_ge(gqk[pi][0], gqk[pi][1])
                ps = qw["ps_qk"][pi]
                for kc in range(8):
                    t.matmul(ps[:, 0:nn], wsb[:, (mi * 8 + kc) * 128:(mi * 8 + kc + 1) * 128],
                             lns.xn[:, kc * NB:kc * NB + nn], start=(kc == 0), stop=False)
                st.inc(t.matmul(ps[:, 0:nn], bsb[:, mi * 128:(mi + 1) * 128],
                                ones1[:, 0:nn], start=False, stop=True), 1)
        b.sec("tensor", ftA)
        def fvB(v, nn=nn, pi0=pi0, STA0=STA0, graw=qw["graw_t"]):
            if graw:
                v.wait_ge(graw[0], graw[1])
            for mi in range(8):
                v.wait_ge(st.h, STA0 + mi + 1)
                sv.inc(v.tensor_copy(qw["qraw_all"][:, mi * NB:mi * NB + nn],
                                     qw["ps_qk"][(pi0 + mi) % 2][:, 0:nn]), 1)
        b.sec("vector", fvB)
        qw["gqk"][(pi0 + 6) % 2] = (sv.h, SVB0 + 7)
        qw["gqk"][(pi0 + 7) % 2] = (sv.h, SVB0 + 8)
        STC0 = st.v
        def ftC(t, nn=nn, SVB0=SVB0, gpp=tuple(qw["gpp2"])):
            for mi in range(8):
                p2 = mi % 2
                t.wait_ge(sv.h, SVB0 + mi + 1)
                if mi >= 2:
                    t.wait_ge(sv.h, SVB0 + 8 + mi - 1)
                elif gpp[p2]:
                    t.wait_ge(gpp[p2][0], gpp[p2][1])
                st.inc(t.matmul(qw["ps_p"][p2][:, 0:nn], qw["permsb"][:, :],
                                qw["qraw_all"][:, mi * NB:mi * NB + nn],
                                start=True, stop=True), 1)
        b.sec("tensor", ftC)
        qw["graw_t"] = (st.h, st.v)
        def fvD(v, off=off, nn=nn, STC0=STC0, gout=qw["gout_t"]):
            if gout:
                v.wait_ge(gout[0], gout[1])
            for mi in range(8):
                p2 = mi % 2
                v.wait_ge(st.h, STC0 + mi + 1)
                v.tensor_tensor(qw["t1s"][:, 0:nn],
                                qw["qraw_all"][:, mi * NB:mi * NB + nn],
                                qw["cossb"][:, off:off + nn], Alu.mult)
                v.tensor_tensor(qw["qraw_all"][:, mi * NB:mi * NB + nn],
                                qw["ps_p"][p2][:, 0:nn],
                                qw["sinsb"][:, off:off + nn], Alu.mult)
                sv.inc(v.tensor_tensor(qw["orow_all"][:, mi * NB:mi * NB + nn],
                                       qw["t1s"][:, 0:nn],
                                       qw["qraw_all"][:, mi * NB:mi * NB + nn],
                                       Alu.add), 1)
        b.sec("vector", fvD)
        qw["gpp2"][0] = (sv.h, SVB0 + 8 + 7)
        qw["gpp2"][1] = (sv.h, SVB0 + 8 + 8)
        def fgE(g, off=off, nn=nn, which=which, SVD0=SVB0 + 8):
            for mi in range(8):
                g.wait_ge(sv.h, SVD0 + mi + 1)
                if which == "q":
                    sg.inc(g.dma_start(out=qw["qT_out"][mi * 128:(mi + 1) * 128, off:off + nn],
                                       in_=qw["orow_all"][:, mi * NB:mi * NB + nn]), 16)
                else:
                    sg.inc(g.dma_start(out=qw["khp_out"][mi, :, off:off + nn],
                                       in_=qw["orow_all"][:, mi * NB:mi * NB + nn]), 16)
        b.sec("gpsimd", fgE)
        qw["gout_t"] = sg.snap()
    if off < tpc:
        nn = min(n, tpc - off)
        assert nn % 128 == 0 or nn == n
        for to in range(0, nn, 128):
            tt = (off + to) // 128
            tp = tt % 2
            for vb in range(2):
                def ft(t, to=to, vb=vb, tp=tp, sgv=sg.v, svv=sv.v, g=qw["gvaug2"][tp]):
                    t.wait_ge(sg.h, sgv)
                    t.wait_ge(sv.h, svv)
                    if g:
                        t.wait_ge(g[0], g[1])
                    for kc in range(8):
                        t.matmul(qw["ps_v"][vb][:, :],
                                 lns.xn[:, kc * NB + to:kc * NB + to + 128],
                                 qw["wvsb"][:, kc * 1024 + vb * 512:kc * 1024 + (vb + 1) * 512],
                                 start=(kc == 0), stop=False)
                    st.inc(t.matmul(qw["ps_v"][vb][:, :], ones1x128[:, :],
                                    qw["bvsb"][:, vb * 512:(vb + 1) * 512],
                                    start=False, stop=True), 1)
                b.sec("tensor", ft)
                def fv(v, vb=vb, tp=tp, stv=st.v, g=qw["gvaug2"][tp]):
                    v.wait_ge(st.h, stv)
                    if g:
                        v.wait_ge(g[0], g[1])
                    sv.inc(v.tensor_copy(
                        qw["vaug"][tp][:, :].rearrange("p (h c) -> p h c", h=16)[:, vb * 8:(vb + 1) * 8, 0:64],
                        qw["ps_v"][vb][:, :].rearrange("p (h c) -> p h c", h=8)), 1)
                b.sec("vector", fv)
            def fg(g, tt=tt, tp=tp, svv=sv.v):
                g.wait_ge(sv.h, svv)
                for hg in range(4):
                    sg.inc(g.dma_start(out=qw["vhg_out"][hg, :, tt, :],
                                       in_=qw["vaug"][tp][:, hg * 260:(hg + 1) * 260]), 16)
            b.sec("gpsimd", fg)
            qw["gvaug2"][tp] = sg.snap()


def mk_qkv_state(b, scope, wq, wk, wv, bq, bk, bv, cosin, perm, qT_out, khp_out,
                 vhg_out, ncol):
    NB = b.cfg.NB
    sg, sv = b.sem("gdma"), b.sem("v")
    qw = {
        "wqsb": b.sbuf("qk_wq", (128, 64 * 128), BF16, scope=scope),
        "wksb": b.sbuf("qk_wk", (128, 64 * 128), BF16, scope=scope),
        "wvsb": b.sbuf("qk_wv", (128, 8 * 1024), BF16, scope=scope),
        "bqsb": b.sbuf("qk_bq", (1, D), BF16, scope=scope),
        "bksb": b.sbuf("qk_bk", (1, D), BF16, scope=scope),
        "bvsb": b.sbuf("qk_bv", (1, D), BF16, scope=scope),
        "permsb": b.sbuf("qk_perm", (128, 128), BF16, scope=scope),
        "cossb": b.sbuf("qk_cos", (128, ncol), BF16, scope=scope),
        "sinsb": b.sbuf("qk_sin", (128, ncol), BF16, scope=scope),
        "qraw_all": b.sbuf("qk_raw", (128, 8 * NB), BF16, scope=scope),
        "t1s": b.sbuf("qk_t1", (128, NB), BF16, scope=scope),
        "orow_all": b.sbuf("qk_orow", (128, 8 * NB), BF16, scope=scope),
        "vaug": [b.sbuf("qk_vaug", (128, 1040), BF16, scope=scope) for _ in range(2)],
        "ps_qk": [b.psum("qk_psqk", (128, NB), scope=scope) for _ in range(2)],
        "ps_p": [b.psum("qk_psp", (128, NB), scope=scope) for _ in range(2)],
        "ps_v": [b.psum("qk_psv", (128, 512), scope=scope) for _ in range(2)],
        "qT_out": qT_out, "khp_out": khp_out, "vhg_out": vhg_out,
        "gqk": [None, None], "graw_t": None, "gpp2": [None, None],
        "gout_t": None, "gvaug2": [None, None],
        "pi": 0,
    }
    def fl(g):
        for mi in range(8):
            sg.inc(g.dma_start(out=qw["wqsb"][:, mi * 8 * 128:(mi + 1) * 8 * 128],
                               in_=wq[mi]), 16)
            sg.inc(g.dma_start(out=qw["wksb"][:, mi * 8 * 128:(mi + 1) * 8 * 128],
                               in_=wk[mi]), 16)
        for kc in range(8):
            sg.inc(g.dma_start(out=qw["wvsb"][:, kc * 1024:(kc + 1) * 1024],
                               in_=wv[kc * 128:(kc + 1) * 128, :]), 16)
        sg.inc(g.dma_start(out=qw["bqsb"][:, :], in_=bq), 16)
        sg.inc(g.dma_start(out=qw["bksb"][:, :], in_=bk), 16)
        sg.inc(g.dma_start(out=qw["bvsb"][:, :], in_=bv), 16)
        sg.inc(g.dma_start(out=qw["permsb"][:, :], in_=perm[:, :]), 16)
        sg.inc(g.dma_start(out=qw["cossb"][:, :], in_=cosin[0]), 16)
        sg.inc(g.dma_start(out=qw["sinsb"][:, :], in_=cosin[1]), 16)
    b.sec("gpsimd", fl)
    def fv(v):
        v.memset(qw["vaug"][0][:, :], 1.0)
        sv.inc(v.memset(qw["vaug"][1][:, :], 1.0), 1)
    b.sec("vector", fv)
    return qw


def build_la2_program(cfg, dbg=None):
    b = B(cfg)
    NCOL, TPC, NB = cfg.NCOL, cfg.TPC, cfg.NB
    x0c = b.dram_in("x0T_cols", (D, NCOL))
    hc = b.dram_in("hT_cols", (D, NCOL))
    w1blob = b.dram_in("w1blob", (128, 32 * 128))
    w2blob = b.dram_in("w2blob", (128, 128 * 128))
    cpblob = b.dram_in("cpblob", (8, 128, 128 * 128))
    wq = b.dram_in("wq_blob", (8, 128, 8 * 128))
    wk = b.dram_in("wk_blob", (8, 128, 8 * 128))
    wv = b.dram_in("wv_nat", (D, D))
    bq = b.dram_in("bq_row", (1, D))
    bk = b.dram_in("bk_row", (1, D))
    bv = b.dram_in("bv_row", (1, D))
    cosin = b.dram_in("cosin", (2, 128, NCOL))
    perm = b.dram_in("rope_perm", (128, 128))
    x2c = b.dram_out("x2T_cols", (D, NCOL))
    qTo = b.dram_out("qT_cols", (D, NCOL), BF16)
    khp = b.dram_out("khp_own", (8, 128, TPC), BF16)
    vhg = b.dram_out("vhg_own", (4, 128, TPC // 128, 260), BF16)
    b.start()
    emit_resadd_hbm(b, x2c, x0c, hc, NCOL)
    b.fence()
    if dbg != "noffn":
        emit_ffn0_stage(b, x2c, x2c, cpblob, w1blob, w2blob, NCOL)
    lns = LnStream(b, NB)
    blks = blk_list(NCOL, NB)
    if dbg == "noqkv":
        def fo(g):
            sg = b.sem("gdma")
            g.wait_ge(sg.h, sg.v)
            sd = b.sem("sdma")
            g.wait_ge(sd.h, sd.v)
        b.sec("gpsimd", fo)
        return b.finish()
    scope = ExitStack()
    qw = mk_qkv_state(b, scope, wq, wk, wv, bq, bk, bv, cosin, perm, qTo, khp, vhg, NCOL)
    b.fence()
    lns.guard = None
    for ib, (off, n) in enumerate(blks):
        lns.block(x2c, NCOL, off, n)
        emit_qkv_block(b, lns, qw, off, n, TPC, NCOL)
        lns.guard = b.sem("gdma").snap()
    b.fence()
    scope.close()
    def fo(g):
        sg = b.sem("gdma")
        g.wait_ge(sg.h, sg.v)
        sd = b.sem("sdma")
        g.wait_ge(sd.h, sd.v)
    b.sec("gpsimd", fo)
    return b.finish()


def _rope_tables(cfg):
    inv = 1.0 / (10000.0 ** (np.arange(0, DH, 2, dtype=np.float64) / DH))
    pos = np.arange(cfg.Tb, dtype=np.float64)
    fr = pos[:, None] * inv[None, :]
    emb = np.concatenate([fr, fr], axis=1)
    cos = np.cos(emb).astype(np.float32).T
    sin = np.sin(emb).astype(np.float32).T
    cos2 = np.concatenate([cos, cos], axis=0)
    sin2 = np.concatenate([sin, sin], axis=0)
    perm = np.zeros((128, 128), np.float32)
    for base in (0, 64):
        for dd in range(32):
            perm[base + dd + 32, base + dd] = -1.0
            perm[base + dd, base + dd + 32] = 1.0
    return cos2, sin2, perm


def _col_index(cfg, core):
    q = core % 4
    q0 = q * cfg.TPC
    own = np.arange(q0, q0 + cfg.TPC)
    if q > 0:
        haloidx = np.arange(q0 - HALO, q0)
    else:
        haloidx = np.zeros(HALO, np.int64)
    spec = np.arange(4) * cfg.TPC
    return np.concatenate([own, haloidx, spec])


def prep_la2_inputs(inputs, cfg, core):
    rm = _np(inputs["L0_ffn_rail_maker"])
    ru = _np(inputs["L0_ffn_rail_up_proj"])
    w1 = np.zeros((4, 8, 128, 128), np.float32)
    for h in range(4):
        for cc in range(8):
            for rl in range(8):
                w1[h, cc, rl * 16:(rl + 1) * 16, rl * 16:(rl + 1) * 16] = rm[h, cc * 8 + rl]
    w1blob = np.ascontiguousarray(
        w1.reshape(32, 128, 128).transpose(1, 0, 2)).reshape(128, 32 * 128)
    w2 = np.zeros((4, 8, 4, 128, 128), np.float32)
    for h in range(4):
        for cc in range(8):
            for a in range(4):
                for rl in range(2):
                    r = cc * 8 + 2 * a + rl
                    w2[h, cc, a, (2 * a + rl) * 16:(2 * a + rl + 1) * 16,
                       rl * 64:(rl + 1) * 64] = ru[h, r].T
    w2blob = np.ascontiguousarray(
        w2.reshape(128, 128, 128).transpose(1, 0, 2)).reshape(128, 128 * 128)
    cpblob = _blockify_lhs(np.ascontiguousarray(_np(inputs["L0_ffn_w_cproj"]).T))
    wqkv = _np(inputs["L0_attn_b_W_qkv"])
    bqkv = _np(inputs["L0_attn_b_b_qkv"])
    cos2, sin2, perm = _rope_tables(cfg)
    cols = _col_index(cfg, core)
    cosin = np.stack([np.ascontiguousarray(cos2[:, cols]),
                      np.ascontiguousarray(sin2[:, cols])])
    return {
        "w1blob": w1blob, "w2blob": w2blob, "cpblob": cpblob,
        "wq_blob": _blockify_lhs(wqkv[:, 0:D]),
        "wk_blob": _blockify_lhs(wqkv[:, D:2 * D]),
        "wv_nat": np.ascontiguousarray(wqkv[:, 2 * D:3 * D]),
        "bq_row": bqkv[0:D].reshape(1, D).copy(),
        "bk_row": bqkv[D:2 * D].reshape(1, D).copy(),
        "bv_row": bqkv[2 * D:3 * D].reshape(1, D).copy(),
        "cosin": cosin, "rope_perm": perm,
    }





def emit_l0_attention(b, qT_hbm, khp, vhg, woutblob, bout_row, x2_hbm, x3_hbm,
                      ncol, Tb, dbg=False):
    cfg = b.cfg
    sv, ss, st, sg, sd = b.sem("v"), b.sem("s"), b.sem("t"), b.sem("gdma"), b.sem("sdma")
    scope = ExitStack()
    nkt = Tb // 128
    attnT_hbm = (b.dram_out("attn_dbg", (D, ncol), BF16) if dbg else b.dram_tmp("attnT_scratch", (D, ncol), BF16))
    qh = [b.sbuf("at_qh", (128, ncol), BF16, scope=scope) for _ in range(2)]
    kt = [b.sbuf("at_kt", (128, Tb), BF16, scope=scope) for _ in range(2)]
    vg_sb = b.sbuf("at_vg", (128, nkt * 260), BF16, scope=scope)
    P2 = [b.sbuf("at_p2", (128, 2 * 512), BF16, scope=scope) for _ in range(2)]
    rep_sb = b.sbuf("at_rep", (64, 512), F32, scope=scope)
    den_sb = b.sbuf("at_den", (1, 512), F32, scope=scope)
    ao_sb = b.sbuf("at_ao", (64, 512), BF16, scope=scope)
    ps_S2 = [b.psum("at_psS2", (128, 2 * 512), scope=scope) for _ in range(2)]
    ps_av = [b.psum("at_psav", (128, 512), scope=scope) for _ in range(2)]
    ps_rep = b.psum("at_psrep", (64, 512), scope=scope)
    ones64f = b.const_ones(1, 64, F32)
    qblks = blk_list(ncol, 512)
    g_q = None
    g_kt = [None, None]
    g_vg = None
    g_av = None
    g_S2 = [None, None]
    g_P2 = [None, None]
    for hg in range(4):
        def fgv(g, hg=hg, gv=g_vg):
            if gv:
                g.wait_ge(gv[0], gv[1])
            sg.inc(g.dma_start(out=vg_sb[:, :], in_=vhg[hg]), 16)
        b.sec("gpsimd", fgv)
        for hp in range(2):
            hgp = hg * 2 + hp
            hhA, hhB = 2 * hp, 2 * hp + 1
            pp = hgp % 2
            def fk(sy, hgp=hgp, pp=pp, gk=g_kt[hgp % 2]):
                if gk:
                    sy.wait_ge(gk[0], gk[1])
                sd.inc(sy.dma_start(out=kt[pp][:, :], in_=khp[hgp]), 16)
                sd.inc(sy.dma_start(out=qh[pp][:, :],
                                    in_=qT_hbm[hgp * 128:(hgp + 1) * 128, :]), 16)
            b.sec("sync", fk)
            for (qo, nq) in qblks:
                ng = nkt
                ss_base = ss.v
                exp_jobs = []
                for g2 in range(ng + 1):
                    pi = g2 % 2
                    if g2 < ng:
                        gS = (ss.h, ss_base + g2 - 1) if g2 >= 2 else g_S2[pi]
                        def ftS(t, qo=qo, nq=nq, g2=g2, pi=pi,
                                sdv=sd.v, gS=gS):
                            t.wait_ge(sd.h, sdv)
                            if gS:
                                t.wait_ge(gS[0], gS[1])
                            for par in range(2):
                                i = t.matmul(ps_S2[pi][:, par * 512:par * 512 + nq],
                                             kt[pp][par * 64:par * 64 + 64,
                                                g2 * 128:(g2 + 1) * 128],
                                             qh[pp][par * 64:par * 64 + 64, qo:qo + nq],
                                             start=True, stop=True)
                            st.inc(i, 1)
                        b.sec("tensor", ftS)
                        exp_jobs.append((st.v, g_P2[pi], pi))
                    if g2 >= 1:
                        kc = g2 - 1
                        pa = kc % 2
                        def ftA(t, hhA=hhA, hhB=hhB, nq=nq, kc=kc, pa=pa,
                                ssv=ss_base + kc + 1,
                                sgv=sg.v, ga=g_av):
                            t.wait_ge(ss.h, ssv)
                            t.wait_ge(sg.h, sgv)
                            if kc == 0 and ga:
                                t.wait_ge(ga[0], ga[1])
                            for pj, hx in ((0, hhA), (1, hhB)):
                                i = t.matmul(ps_av[pj][0:65, 0:nq],
                                             vg_sb[:, kc * 260 + hx * 65:
                                                   kc * 260 + (hx + 1) * 65],
                                             P2[pa][:, pj * 512:pj * 512 + nq],
                                             start=(kc == 0), stop=(kc == nkt - 1))
                            st.inc(i, 1)
                        b.sec("tensor", ftA)
                        g_P2[pa] = (st.h, st.v)
                def fsE_all(s, jobs=exp_jobs, nq=nq):
                    for (stv, gP, pi) in jobs:
                        s.wait_ge(st.h, stv)
                        if gP:
                            s.wait_ge(gP[0], gP[1])
                        if nq == 512:
                            i = s.activation(P2[pi][:, :], ps_S2[pi][:, :],
                                             AF.Exp, scale=0.125)
                        else:
                            for j in range(2):
                                i = s.activation(P2[pi][:, j * 512:j * 512 + nq],
                                                 ps_S2[pi][:, j * 512:j * 512 + nq],
                                                 AF.Exp, scale=0.125)
                        ss.inc(i, 1)
                b.sec("scalar", fsE_all)
                g_S2[0] = (ss.h, ss_base + ng - 1)
                g_S2[1] = (ss.h, ss_base + ng)
                for pj, par in ((0, 0), (1, 1)):
                    def fvD(v, pj=pj, nq=nq, stv=st.v):
                        v.wait_ge(st.h, stv)
                        v.tensor_copy(den_sb[0:1, 0:nq], ps_av[pj][64:65, 0:nq])
                        sv.inc(v.reciprocal(den_sb[0:1, 0:nq], den_sb[0:1, 0:nq]), 1)
                    b.sec("vector", fvD)
                    def ftR(t, nq=nq, svv=sv.v):
                        t.wait_ge(sv.h, svv)
                        st.inc(t.matmul(ps_rep[:, 0:nq], ones64f[:, :],
                                        den_sb[0:1, 0:nq], start=True, stop=True), 1)
                    b.sec("tensor", ftR)
                    def fvA(v, pj=pj, nq=nq, stv=st.v, sgv=sg.v):
                        v.wait_ge(st.h, stv)
                        v.wait_ge(sg.h, sgv)
                        v.tensor_copy(rep_sb[:, 0:nq], ps_rep[:, 0:nq])
                        sv.inc(v.tensor_tensor(ao_sb[:, 0:nq], ps_av[pj][0:64, 0:nq],
                                               rep_sb[:, 0:nq], Alu.mult), 1)
                    b.sec("vector", fvA)
                    def fgo(g, hgp=hgp, par=par, qo=qo, nq=nq, svv=sv.v):
                        g.wait_ge(sv.h, svv)
                        sg.inc(g.dma_start(
                            out=attnT_hbm[hgp * 128 + par * 64:
                                          hgp * 128 + par * 64 + 64, qo:qo + nq],
                            in_=ao_sb[:, 0:nq]), 16)
                    b.sec("gpsimd", fgo)
                g_av = sv.snap()
            g_kt[hgp % 2] = (st.h, st.v)
        g_vg = (st.h, st.v)
    b.fence()
    scope.close()
    scope = ExitStack()
    wsb = b.sbuf("wo_w", (128, 64 * 128), BF16, scope=scope)
    bsb = b.sbuf("wo_b", (1, D), BF16, scope=scope)
    a_nb = b.sbuf("wo_a", (128, 8 * 512), BF16, scope=scope)
    x_nb = b.sbuf("wo_x", (128, 8 * 512), F32, scope=scope)
    ps_o = [b.psum("wo_ps", (128, 512), scope=scope) for _ in range(2)]
    ones1 = b.const_ones(1, 512)
    def fl(g):
        sg.inc(g.dma_start(out=bsb[:, :], in_=bout_row), 16)
        for mi in range(8):
            sg.inc(g.dma_start(out=wsb[:, mi * 1024:(mi + 1) * 1024],
                               in_=woutblob[mi]), 16)
    b.sec("gpsimd", fl)
    g_a = None
    g_x = None
    g_o = [None, None]
    for (qo, nq) in qblks:
        SD0 = sd.v
        def fla(sy, qo=qo, nq=nq, ga=g_a, gx=g_x):
            if ga:
                sy.wait_ge(ga[0], ga[1])
            if gx:
                sy.wait_ge(gx[0], gx[1])
            for kc in range(8):
                sd.inc(sy.dma_start(out=a_nb[:, kc * 512:kc * 512 + nq],
                                    in_=attnT_hbm[kc * 128:(kc + 1) * 128, qo:qo + nq]), 16)
            for mi in range(8):
                sd.inc(sy.dma_start(out=x_nb[:, mi * 512:mi * 512 + nq],
                                    in_=x2_hbm[mi * 128:(mi + 1) * 128, qo:qo + nq]), 16)
        b.sec("sync", fla)
        ST0 = st.v
        SV0 = sv.v
        def ft(t, nq=nq, sdv=sd.v, sgv=sg.v, gO=tuple(g_o), SV0=SV0):
            t.wait_ge(sd.h, sdv)
            t.wait_ge(sg.h, sgv)
            for mi in range(8):
                pi = mi % 2
                if mi >= 2:
                    t.wait_ge(sv.h, SV0 + mi - 1)
                elif gO[pi]:
                    t.wait_ge(gO[pi][0], gO[pi][1])
                for kc in range(8):
                    t.matmul(ps_o[pi][:, 0:nq], wsb[:, mi * 1024 + kc * 128:mi * 1024 + (kc + 1) * 128],
                             a_nb[:, kc * 512:kc * 512 + nq],
                             start=(kc == 0), stop=False)
                st.inc(t.matmul(ps_o[pi][:, 0:nq], bsb[:, mi * 128:(mi + 1) * 128],
                                ones1[:, 0:nq], start=False, stop=True), 1)
        b.sec("tensor", ft)
        def fv(v, nq=nq, ST0=ST0):
            for mi in range(8):
                v.wait_ge(st.h, ST0 + mi + 1)
                sv.inc(v.tensor_tensor(x_nb[:, mi * 512:mi * 512 + nq],
                                       x_nb[:, mi * 512:mi * 512 + nq],
                                       ps_o[mi % 2][:, 0:nq], Alu.add), 1)
        b.sec("vector", fv)
        g_o[0] = (sv.h, SV0 + 7)
        g_o[1] = (sv.h, SV0 + 8)
        def fgo(g, qo=qo, nq=nq, SV0=SV0):
            for mi in range(8):
                g.wait_ge(sv.h, SV0 + mi + 1)
                sg.inc(g.dma_start(out=x3_hbm[mi * 128:(mi + 1) * 128, qo:qo + nq],
                                   in_=x_nb[:, mi * 512:mi * 512 + nq]), 16)
        b.sec("gpsimd", fgo)
        g_x = sg.snap()
        g_a = (st.h, st.v)
    b.fence()
    scope.close()


def build_la4_program(cfg):
    b = B(cfg)
    NCOL, NB = cfg.NCOL, cfg.NB
    x3c = b.dram_in("x3T_cols", (D, NCOL))
    w1blob = b.dram_in("w1blob", (128, 32 * 128))
    w2blob = b.dram_in("w2blob", (128, 128 * 128))
    cpblob = b.dram_in("cpblob", (8, 128, 128 * 128))
    x4c = b.dram_out("x4T_cols", (D, NCOL))
    b.start()
    emit_ffn0_stage(b, x3c, x4c, cpblob, w1blob, w2blob, NCOL)
    def fo(g):
        for dn in ("gdma", "sdma"):
            sm = b.sem(dn)
            g.wait_ge(sm.h, sm.v)
    b.sec("gpsimd", fo)
    return b.finish()


def build_la3_program(cfg, dbg=False):
    b = B(cfg)
    NCOL, Tb = cfg.NCOL, cfg.Tb
    qT = b.dram_in("qT_cols", (D, NCOL), BF16)
    khp = b.dram_in("khp_full", (8, 128, Tb), BF16)
    vhg = b.dram_in("vhg_full", (4, 128, Tb // 128, 260), BF16)
    wout = b.dram_in("wout_blob", (8, 128, 8 * 128))
    bout = b.dram_in("bout_row", (1, D))
    x2c = b.dram_in("x2T_cols", (D, NCOL))
    x3c = b.dram_out("x3T_cols", (D, NCOL))
    b.start()
    emit_l0_attention(b, qT, khp, vhg, wout, bout, x2c, x3c, NCOL, Tb, dbg=dbg)
    def fo(g):
        for dn in ("gdma", "sdma"):
            sm = b.sem(dn)
            g.wait_ge(sm.h, sm.v)
    b.sec("gpsimd", fo)
    return b.finish()


def prep_la3_inputs(inputs, cfg):
    return {
        "wout_blob": _blockify_lhs(_np(inputs["L0_attn_b_W_out"])),
        "bout_row": _np(inputs["L0_attn_b_b_out"]).reshape(1, D),
    }





def emit_l1_attn_a(b, x4_hbm, x5_hbm, kT_in, v_in, ncol, tpc, dbg=False):
    cfg = b.cfg
    NB = cfg.NB
    sv, ss, st, sg, sd = b.sem("v"), b.sem("s"), b.sem("t"), b.sem("gdma"), b.sem("sdma")
    scope = ExitStack()
    lns = LnStream(b, NB, nbuf=1)
    kt_sb = b.sbuf("aa_kt", (128, 8 * KLEN), BF16, scope=scope)
    v_sb = b.sbuf("aa_v", (128, 32 * 1024), BF16, scope=scope)
    S_sb = b.sbuf("aa_S", (128, KLEN), F32, scope=scope)
    P_sb = b.sbuf("aa_P", (128, KLEN), BF16, scope=scope)
    PT_sb = P_sb
    srep = b.sbuf("aa_srep", (128, KLEN), BF16, scope=scope)
    ident = b.sbuf("aa_id", (128, 128), BF16, scope=scope)
    nmx = b.sbuf("aa_nmx", (128, 1), F32, scope=scope)
    den = b.sbuf("aa_den", (128, 1), F32, scope=scope)
    x_t = b.sbuf("aa_x", (128, 8 * 128), F32, scope=scope)
    ps_S = [b.psum("aa_psS", (128, 512), scope=scope) for _ in range(2)]
    ps_T = [b.psum("aa_psT", (128, 1024), BF16, scope=scope) for _ in range(2)]
    ps_o = [b.psum("aa_pso", (128, 128), scope=scope) for _ in range(2)]
    scale4 = b.dram_tmp("aa_scale4", (4, D))
    ones128 = b.const_ones(128, 128)

    def fl(g):
        for kc in range(8):
            sg.inc(g.dma_start(out=kt_sb[:, kc * KLEN:(kc + 1) * KLEN],
                               in_=kT_in[kc * 128:(kc + 1) * 128, :]), 16)
        sg.inc(g.dma_start(out=v_sb[:, :].rearrange("p (t v) -> p t v", t=32),
                           in_=v_in[:, :].rearrange("(t p) v -> p t v", p=128)), 16)
        g.memset(ident[:, :], 1.0)
        b.sem("gc").inc(g.affine_select(ident[:, :], ident[:, :], [[-1, 128]],
                                        Alu.is_equal, 0.0, base=0, channel_multiplier=1), 1)
    b.sec("gpsimd", fl)
    sgc = b.sem("gc")

    blks = blk_list(ncol, NB)
    (off_l, n_l) = blks[-1]
    lns.block(x4_hbm, ncol, off_l, n_l)
    def fgs(g, off=off_l, n=n_l, svv=sv.v):
        g.wait_ge(sv.h, svv)
        for cc in range(8):
            sg.inc(g.dma_start(
                out=scale4[:, cc * 128:(cc + 1) * 128].rearrange("j p -> p j"),
                in_=lns.xn[:, cc * NB + n - 4:cc * NB + n],
                allow_slow_non_contiguous=True), 16)
    b.sec("gpsimd", fgs)
    iscope = ExitStack()
    srow = b.sbuf("aa_srow", (1, KLEN), BF16, scope=iscope)
    ps_r = ps_S[0]
    def fls(g2, sgv=sg.v):
        g2.wait_ge(sg.h, sgv)
        sg.inc(g2.dma_start(out=srow[:, :],
                            in_=scale4[:, :].rearrange("a b -> (a b)")[None, :]), 16)
    b.sec("gpsimd", fls)
    for j in range(8):
        def ft(t, j=j, svv=sv.v, sgv=sg.v):
            t.wait_ge(sg.h, sgv)
            t.wait_ge(sv.h, svv)
            st.inc(t.matmul(ps_r[:, :], ones128[0:1, :], srow[0:1, j * 512:(j + 1) * 512],
                            start=True, stop=True), 1)
        b.sec("tensor", ft)
        def fv(v, j=j, stv=st.v):
            v.wait_ge(st.h, stv)
            sv.inc(v.tensor_copy(srep[:, j * 512:(j + 1) * 512], ps_r[:, :]), 1)
        b.sec("vector", fv)
    b.fence()
    iscope.close()

    lns.guard = None
    g_S = [None, None]
    g_T = [None, None]
    g_o = [None, None]
    g_soft = None
    g_xnb = None
    for (off, n) in blks:
        lns.block(x4_hbm, ncol, off, n)
        subs = [(0, n)] if n <= 128 else ([(i * 128, 128) for i in range(n // 128)]
                                          if n % 128 == 0 else [(0, (n + 1) // 2), ((n + 1) // 2, n - (n + 1) // 2)])
        for (so, sn) in subs:
            STS0 = st.v
            SVS0 = sv.v
            def ftS(t, off=off, so=so, sn=sn, svv=sv.v, gS=tuple(g_S), SVS0=SVS0):
                t.wait_ge(sv.h, svv)
                for kb in range(8):
                    pi = kb % 2
                    if kb >= 2:
                        t.wait_ge(sv.h, SVS0 + kb - 1)
                    elif gS[pi]:
                        t.wait_ge(gS[pi][0], gS[pi][1])
                    for kc in range(8):
                        i = t.matmul(ps_S[pi][0:sn, 0:512],
                                     lns.xn[:, kc * NB + so:kc * NB + so + sn],
                                     kt_sb[:, kc * KLEN + kb * 512:kc * KLEN + (kb + 1) * 512],
                                     start=(kc == 0), stop=(kc == 7))
                    st.inc(i, 1)
            b.sec("tensor", ftS)
            def fvS(v, sn=sn, STS0=STS0, gs=g_soft):
                if gs:
                    v.wait_ge(gs[0], gs[1])
                for kb in range(8):
                    v.wait_ge(st.h, STS0 + kb + 1)
                    sv.inc(v.tensor_tensor(S_sb[0:sn, kb * 512:(kb + 1) * 512],
                                           ps_S[kb % 2][0:sn, 0:512],
                                           srep[0:sn, kb * 512:(kb + 1) * 512], Alu.mult), 1)
            b.sec("vector", fvS)
            g_S[0] = (sv.h, SVS0 + 7)
            g_S[1] = (sv.h, SVS0 + 8)
            def fvm(v, sn=sn, ssv=ss.v):
                v.wait_ge(ss.h, ssv)
                sv.inc(v.reduce_max(nmx[0:sn, :], S_sb[0:sn, :], axis=AX.X, negate=True), 1)
            b.sec("vector", fvm)
            def fse(s, sn=sn, svv=sv.v):
                s.wait_ge(sv.h, svv)
                ss.inc(s.activation(P_sb[0:sn, :], S_sb[0:sn, :], AF.Exp,
                                    bias=nmx[0:sn, :], accum_out=den[0:sn, :]), 1)
            b.sec("scalar", fse)
            def fvd(v, sn=sn, ssv=ss.v):
                v.wait_ge(ss.h, ssv)
                v.reciprocal(den[0:sn, :], den[0:sn, :])
                sv.inc(v.tensor_scalar(P_sb[0:sn, :], P_sb[0:sn, :], den[0:sn, :],
                                       None, Alu.mult), 1)
            b.sec("vector", fvd)
            g_soft = sv.snap()
            if dbg and off == 0 and so == 0:
                b.fence()
                sdbg = b.dram_out("S_dbg", (128, KLEN))
                pdbg = b.dram_out("P_dbg", (128, KLEN), BF16)
                ddbg = b.dram_out("den_dbg", (128, 1))
                srdbg = b.dram_out("srep_dbg", (128, KLEN), BF16)
                def fdbg(g, svv=sv.v):
                    g.wait_ge(sv.h, svv)
                    sg.inc(g.dma_start(out=sdbg[:, :], in_=S_sb[:, :]), 16)
                    sg.inc(g.dma_start(out=pdbg[:, :], in_=P_sb[:, :]), 16)
                    sg.inc(g.dma_start(out=ddbg[:, :], in_=den[:, :]), 16)
                    sg.inc(g.dma_start(out=srdbg[:, :], in_=srep[:, :]), 16)
                    g.wait_ge(sg.h, sg.v)
                b.sec("gpsimd", fdbg)
                b.fence()
            STT0 = st.v
            SVT0 = sv.v
            def ftT(t, sn=sn, svv=sv.v, gT=tuple(g_T), sgcv=sgc.v, SVT0=SVT0):
                t.wait_ge(sv.h, svv)
                t.wait_ge(sgc.h, sgcv)
                for kc in range(32):
                    pi = kc % 2
                    if kc >= 2:
                        t.wait_ge(sv.h, SVT0 + kc - 1)
                    elif gT[pi]:
                        t.wait_ge(gT[pi][0], gT[pi][1])
                    st.inc(t.transpose(ps_T[pi][0:128, 0:sn],
                                       P_sb[0:sn, kc * 128:(kc + 1) * 128],
                                       ident[0:sn, 0:sn]), 1)
            b.sec("tensor", ftT)
            def fvT(v, sn=sn, STT0=STT0):
                for kc in range(32):
                    v.wait_ge(st.h, STT0 + kc + 1)
                    sv.inc(v.tensor_copy(PT_sb[:, kc * 128:kc * 128 + sn],
                                         ps_T[kc % 2][:, 0:sn]), 1)
            b.sec("vector", fvT)
            g_T[0] = (sv.h, SVT0 + 31)
            g_T[1] = (sv.h, SVT0 + 32)
            STA0 = st.v
            SVA0 = SVT0 + 32
            SD0 = sd.v
            def ftA(t, sn=sn, SVA0=SVA0, gO=tuple(g_o)):
                t.wait_ge(sv.h, SVA0)
                for mi in range(8):
                    pi = mi % 2
                    if mi >= 2:
                        t.wait_ge(sv.h, SVA0 + mi - 1)
                    elif gO[pi]:
                        t.wait_ge(gO[pi][0], gO[pi][1])
                    for kc in range(32):
                        i = t.matmul(ps_o[pi][:, 0:sn],
                                     v_sb[:, kc * 1024 + mi * 128:kc * 1024 + (mi + 1) * 128],
                                     PT_sb[:, kc * 128:kc * 128 + sn],
                                     start=(kc == 0), stop=(kc == 31))
                    st.inc(i, 1)
            b.sec("tensor", ftA)
            def fx(sy, off=off, so=so, sn=sn, gx=g_xnb):
                if gx:
                    sy.wait_ge(gx[0], gx[1])
                for mi in range(8):
                    sd.inc(sy.dma_start(out=x_t[:, mi * 128:mi * 128 + sn],
                                        in_=x4_hbm[mi * 128:(mi + 1) * 128,
                                                   off + so:off + so + sn]), 16)
            b.sec("sync", fx)
            def fvA(v, sn=sn, STA0=STA0, SD0=SD0):
                for mi in range(8):
                    v.wait_ge(st.h, STA0 + mi + 1)
                    v.wait_ge(sd.h, SD0 + 16 * (mi + 1))
                    sv.inc(v.tensor_tensor(x_t[:, mi * 128:mi * 128 + sn],
                                           x_t[:, mi * 128:mi * 128 + sn],
                                           ps_o[mi % 2][:, 0:sn], Alu.add), 1)
            b.sec("vector", fvA)
            g_o[0] = (sv.h, SVA0 + 7)
            g_o[1] = (sv.h, SVA0 + 8)
            def fgo(g, off=off, so=so, sn=sn, SVA0=SVA0):
                for mi in range(8):
                    g.wait_ge(sv.h, SVA0 + mi + 1)
                    sg.inc(g.dma_start(out=x5_hbm[mi * 128:(mi + 1) * 128,
                                                  off + so:off + so + sn],
                                       in_=x_t[:, mi * 128:mi * 128 + sn]), 16)
            b.sec("gpsimd", fgo)
            g_xnb = sg.snap()
        lns.guard = (st.h, st.v)
    b.fence()
    scope.close()


def build_la5_program(cfg, dbg=False):
    b = B(cfg)
    NCOL, TPC = cfg.NCOL, cfg.TPC
    x4c = b.dram_in("x4T_cols", (D, NCOL))
    kT = b.dram_in("l1a_kT", (D, KLEN))
    vn = b.dram_in("l1a_v", (KLEN, D))
    x5c = b.dram_out("x5T_cols", (D, NCOL))
    b.start()
    emit_l1_attn_a(b, x4c, x5c, kT, vn, NCOL, TPC, dbg=dbg)
    def fo(g):
        for dn in ("gdma", "sdma"):
            sm = b.sem(dn)
            g.wait_ge(sm.h, sm.v)
    b.sec("gpsimd", fo)
    return b.finish()





def emit_l1_ffn(b, xsrc, xdst, selblob, selb_rows, compblob, compb_rows, ncol):
    cfg = b.cfg
    NB = cfg.NB
    sv, ss, st, sg, sd = b.sem("v"), b.sem("s"), b.sem("t"), b.sem("gdma"), b.sem("sdma")
    scope = ExitStack()
    lns = LnStream(b, NB, nbuf=1)
    xn_full = b.sbuf("lf_xn", (128, 8 * ncol), BF16, scope=scope)
    Lbuf = b.sbuf("lf_L", (128, 8 * ncol), F32, scope=scope)
    mbuf = b.sbuf("lf_m", (128, ncol), F32, scope=scope)
    msk = b.sbuf("lf_msk", (128, ncol), F32, scope=scope)
    acc = b.sbuf("lf_acc", (128, ncol), F32, scope=scope)
    x_t = b.sbuf("lf_x", (128, ncol), F32, scope=scope)
    wsb = b.sbuf("lf_w", (128, 2 * 1024), BF16, scope=scope)
    brow = b.sbuf("lf_b", (1, 2 * 128), BF16, scope=scope)
    ps = [b.psum("lf_ps", (128, 512), scope=scope) for _ in range(2)]
    ones1 = b.const_ones(1, 512)
    blks = blk_list(ncol, NB)
    for (off, n) in blks:
        lns.block(xsrc, ncol, off, n)
        def fv(v, off=off, n=n, g=lns.guard):
            for cc in range(8):
                i = v.tensor_copy(xn_full[:, cc * ncol + off:cc * ncol + off + n],
                                  lns.xn[:, cc * NB:cc * NB + n])
            sv.inc(i, 1)
        b.sec("vector", fv)
        lns.guard = sv.snap()
    b.fence()
    nbl2 = blk_list(ncol, 512)
    g_w = [None, None]
    g_ps = [None, None]
    g_sel = None
    for oc in range(8):
        for n_e in range(8):
            slot = n_e % 2
            def fg(g, oc=oc, n_e=n_e, slot=slot, gw=g_w[slot]):
                if gw:
                    g.wait_ge(gw[0], gw[1])
                sg.inc(g.dma_start(out=wsb[:, slot * 1024:(slot + 1) * 1024],
                                   in_=selblob[n_e, oc]), 16)
                sg.inc(g.dma_start(out=brow[:, slot * 128:(slot + 1) * 128],
                                   in_=selb_rows[n_e, 0:1, oc * 128:(oc + 1) * 128]), 16)
            b.sec("gpsimd", fg)
            for (qo, nq) in nbl2:
                pi = (n_e * len(nbl2) + len(nbl2) + nbl2.index((qo, nq))) % 2
                def ft(t, oc=oc, n_e=n_e, slot=slot, qo=qo, nq=nq, pi=pi,
                       sgv=sg.v, gp=g_ps[pi]):
                    t.wait_ge(sg.h, sgv)
                    if gp:
                        t.wait_ge(gp[0], gp[1])
                    for kc in range(8):
                        t.matmul(ps[pi][:, 0:nq],
                                 wsb[:, slot * 1024 + kc * 128:slot * 1024 + (kc + 1) * 128],
                                 xn_full[:, kc * ncol + qo:kc * ncol + qo + nq],
                                 start=(kc == 0), stop=False)
                    st.inc(t.matmul(ps[pi][:, 0:nq], brow[:, slot * 128:slot * 128 + 128],
                                    ones1[:, 0:nq], start=False, stop=True), 1)
                b.sec("tensor", ft)
                def fs(s, n_e=n_e, qo=qo, nq=nq, pi=pi, stv=st.v, gsl=g_sel):
                    s.wait_ge(st.h, stv)
                    if gsl:
                        s.wait_ge(gsl[0], gsl[1])
                    ss.inc(s.copy(Lbuf[:, n_e * ncol + qo:n_e * ncol + qo + nq],
                                  ps[pi][:, 0:nq]), 1)
                b.sec("scalar", fs)
                def fv(v, n_e=n_e, qo=qo, nq=nq, ssv=ss.v):
                    v.wait_ge(ss.h, ssv)
                    if n_e == 0:
                        i = v.tensor_copy(mbuf[:, qo:qo + nq],
                                          Lbuf[:, qo:qo + nq])
                    else:
                        i = v.tensor_tensor(mbuf[:, qo:qo + nq],
                                            mbuf[:, qo:qo + nq],
                                            Lbuf[:, n_e * ncol + qo:n_e * ncol + qo + nq],
                                            Alu.max)
                    sv.inc(i, 1)
                b.sec("vector", fv)
                g_ps[pi] = ss.snap()
            g_w[slot] = (st.h, st.v)
        for n_e in range(8):
            slot = n_e % 2
            def fg(g, oc=oc, n_e=n_e, slot=slot, gw=g_w[slot]):
                if gw:
                    g.wait_ge(gw[0], gw[1])
                sg.inc(g.dma_start(out=wsb[:, slot * 1024:(slot + 1) * 1024],
                                   in_=compblob[n_e, oc]), 16)
                sg.inc(g.dma_start(out=brow[:, slot * 128:(slot + 1) * 128],
                                   in_=compb_rows[n_e, 0:1, oc * 128:(oc + 1) * 128]), 16)
            b.sec("gpsimd", fg)
            for (qo, nq) in nbl2:
                pi = (n_e * len(nbl2) + nbl2.index((qo, nq))) % 2
                def ft(t, n_e=n_e, slot=slot, qo=qo, nq=nq, pi=pi, sgv=sg.v, gp=g_ps[pi]):
                    t.wait_ge(sg.h, sgv)
                    if gp:
                        t.wait_ge(gp[0], gp[1])
                    for kc in range(8):
                        t.matmul(ps[pi][:, 0:nq],
                                 wsb[:, slot * 1024 + kc * 128:slot * 1024 + (kc + 1) * 128],
                                 xn_full[:, kc * ncol + qo:kc * ncol + qo + nq],
                                 start=(kc == 0), stop=False)
                    st.inc(t.matmul(ps[pi][:, 0:nq], brow[:, slot * 128:slot * 128 + 128],
                                    ones1[:, 0:nq], start=False, stop=True), 1)
                b.sec("tensor", ft)
                def fv(v, n_e=n_e, qo=qo, nq=nq, pi=pi, stv=st.v, svv=sv.v):
                    v.wait_ge(st.h, stv)
                    v.tensor_tensor(msk[:, qo:qo + nq],
                                    Lbuf[:, n_e * ncol + qo:n_e * ncol + qo + nq],
                                    mbuf[:, qo:qo + nq], Alu.is_ge)
                    if n_e == 0:
                        i = v.tensor_tensor(acc[:, qo:qo + nq], msk[:, qo:qo + nq],
                                            ps[pi][:, 0:nq], Alu.mult)
                    else:
                        v.tensor_tensor(msk[:, qo:qo + nq], msk[:, qo:qo + nq],
                                        ps[pi][:, 0:nq], Alu.mult)
                        i = v.tensor_tensor(acc[:, qo:qo + nq], acc[:, qo:qo + nq],
                                            msk[:, qo:qo + nq], Alu.add)
                    sv.inc(i, 1)
                b.sec("vector", fv)
                g_ps[pi] = sv.snap()
            g_w[slot] = (st.h, st.v)
        def fx(sy, oc=oc, sgv=sg.v):
            sy.wait_ge(sg.h, sgv)
            sd.inc(sy.dma_start(out=x_t[:, :],
                                in_=xsrc[oc * 128:(oc + 1) * 128, :]), 16)
        b.sec("sync", fx)
        def fv(v, oc=oc, sdv=sd.v, svv=sv.v):
            v.wait_ge(sd.h, sdv)
            sv.inc(v.tensor_tensor(x_t[:, :], x_t[:, :], acc[:, :], Alu.add), 1)
        b.sec("vector", fv)
        def fgo(g, oc=oc, svv=sv.v):
            g.wait_ge(sv.h, svv)
            sg.inc(g.dma_start(out=xdst[oc * 128:(oc + 1) * 128, :], in_=x_t[:, :]), 16)
        b.sec("gpsimd", fgo)
        g_sel = sv.snap()
    b.fence()
    scope.close()


def build_la6_program(cfg, inname="x5T_cols", outname="x6T_cols"):
    b = B(cfg)
    NCOL = cfg.NCOL
    xsrc = b.dram_in(inname, (D, NCOL))
    selblob = b.dram_in("sel_blob", (8, 8, 128, 1024))
    selb = b.dram_in("selb_rows", (8, 1, D))
    compblob = b.dram_in("comp_blob", (8, 8, 128, 1024))
    compb = b.dram_in("compb_rows", (8, 1, D))
    xdst = b.dram_out(outname, (D, NCOL))
    b.start()
    emit_l1_ffn(b, xsrc, xdst, selblob, selb, compblob, compb, NCOL)
    def fo(g):
        for dn in ("gdma", "sdma"):
            sm = b.sem(dn)
            g.wait_ge(sm.h, sm.v)
    b.sec("gpsimd", fo)
    return b.finish()


def prep_l1ffn_inputs(inputs, cfg):
    sel_w = _np(inputs["L1_ffn_sel_w"])
    sel_b = _np(inputs["L1_ffn_sel_b"])
    comp_w = _np(inputs["L1_ffn_comp_w"])
    comp_b = _np(inputs["L1_ffn_comp_b"])
    selT = sel_w.reshape(1024, 8, 1024).transpose(1, 2, 0)
    selblob = np.zeros((8, 8, 128, 1024), np.float32)
    compblob = np.zeros((8, 8, 128, 1024), np.float32)
    for n_e in range(8):
        bb = _blockify_lhs(np.ascontiguousarray(selT[n_e]))
        selblob[n_e] = bb
        cb = _blockify_lhs(np.ascontiguousarray(comp_w[n_e].T))
        compblob[n_e] = cb
    selb_rows = np.ascontiguousarray(sel_b.reshape(1024, 8).T).reshape(8, 1, D)
    compb_rows = comp_b.reshape(8, 1, D).copy()
    return {"sel_blob": selblob, "selb_rows": selb_rows,
            "comp_blob": compblob, "compb_rows": compb_rows}





def emit_l1_attn_b(b, x6_hbm, x7_hbm, W, ncol, tpc):
    cfg = b.cfg
    NB = cfg.NB
    sv, ss, st, sg, sd = b.sem("v"), b.sem("s"), b.sem("t"), b.sem("gdma"), b.sem("sdma")
    sgc = b.sem("gc")
    nwin = tpc // WINDOW
    y0_hbm = b.dram_tmp("lb_y0", (D, ncol))
    xin1_hbm = b.dram_tmp("lb_xin1", (D, tpc))
    lns = LnStream(b, NB, nbuf=1)
    attnT_hbm = b.dram_tmp("lb_attnT", (D, tpc), BF16)
    ln1g = b.sbuf("lb_ln1g", (128, 8), F32)
    ln1b = b.sbuf("lb_ln1b", (128, 8), F32)
    ln2g = b.sbuf("lb_ln2g", (128, 8), F32)
    ln2b = b.sbuf("lb_ln2b", (128, 8), F32)
    acol = b.sbuf("lb_acol", (128, 1), F32)
    hgate = b.sbuf("lb_hgate", (128, HALO), F32)
    scopeCD = ExitStack()
    g_sb = b.sbuf("lb_g", (128, 8 * tpc), BF16, scope=scopeCD)
    states = b.sbuf("lb_states", (128, HALO + tpc), F32, scope=scopeCD)
    states_bf = b.sbuf("lb_statesbf", (128, tpc), BF16, scope=scopeCD)
    def fl(sy):
        for t, src in ((ln1g, W["ln1g_col"]), (ln1b, W["ln1b_col"]),
                       (ln2g, W["ln2g_col"]), (ln2b, W["ln2b_col"]),
                       (acol, W["A_col"]), (hgate, W["halo_gate"])):
            sd.inc(sy.dma_start(out=t[:, :], in_=src), 16)
    b.sec("sync", fl)
    b.fence()

    scopeA = ExitStack()
    xn_full = b.sbuf("lb_xn", (128, 8 * ncol), BF16, scope=scopeA)
    blks = blk_list(ncol, NB)
    for (off, n) in blks:
        lns.block(x6_hbm, ncol, off, n)
        def fv(v, off=off, n=n):
            for cc in range(8):
                v.tensor_tensor(lns.xb[:, cc * NB:cc * NB + n],
                                lns.x_nb[:, cc * NB:cc * NB + n],
                                lns.mu[:, 0:n], Alu.subtract)
                v.tensor_tensor(lns.xb[:, cc * NB:cc * NB + n],
                                lns.xb[:, cc * NB:cc * NB + n],
                                lns.rstd[:, 0:n], Alu.mult)
                i = v.tensor_copy(lns.x_nb[:, cc * NB:cc * NB + n],
                                  lns.xb[:, cc * NB:cc * NB + n])
            sv.inc(i, 1)
        b.sec("vector", fv)
        def fv2(v, off=off, n=n, svv=sv.v):
            for cc in range(8):
                i = v.tensor_scalar(xn_full[:, cc * ncol + off:cc * ncol + off + n],
                                    lns.x_nb[:, cc * NB:cc * NB + n],
                                    ln1g[:, cc:cc + 1], ln1b[:, cc:cc + 1],
                                    Alu.mult, Alu.add)
            sv.inc(i, 1)
        b.sec("vector", fv2)
        def fgo(g, off=off, n=n, svv=sv.v):
            g.wait_ge(sv.h, svv)
            for cc in range(8):
                sg.inc(g.dma_start(out=y0_hbm[cc * 128:(cc + 1) * 128, off:off + n],
                                   in_=lns.x_nb[:, cc * NB:cc * NB + n]), 16)
        b.sec("gpsimd", fgo)
        lns.guard = sg.snap()
    b.fence()

    scopeB = ExitStack()
    wqsb = b.sbuf("lb_wq", (128, 64 * 128), BF16, scope=scopeB)
    wksb = b.sbuf("lb_wk", (128, 64 * 128), BF16, scope=scopeB)
    wvsb = b.sbuf("lb_wv", (128, 8 * 1024), BF16, scope=scopeB)
    bqkvsb = b.sbuf("lb_bqkv", (1, 3 * D), BF16, scope=scopeB)
    masksb = b.sbuf("lb_mask", (128, 2 * 4 * 256), BF16, scope=scopeB)
    qwin = b.sbuf("lb_qwin", (128, 8 * 256), BF16, scope=scopeB)
    kwin = b.sbuf("lb_kwin", (128, 2 * 8 * 256), BF16, scope=scopeB)
    vwin = b.sbuf("lb_vwin", (128, 2 * 2 * 1040), BF16, scope=scopeB)
    P2 = [b.sbuf("lb_P2", (128, 1024), BF16, scope=scopeB) for _ in range(2)]
    den2 = [b.sbuf("lb_den2", (1, 256), F32, scope=scopeB) for _ in range(2)]
    rep2 = [b.sbuf("lb_rep2", (64, 256), F32, scope=scopeB) for _ in range(2)]
    ao2_sb = [b.sbuf("lb_ao2", (64, 256), BF16, scope=scopeB) for _ in range(2)]
    ps_qkv = [b.psum("lb_psqkv", (128, 512), scope=scopeB) for _ in range(2)]
    ps_S2 = [b.psum("lb_psS2", (128, 1024), scope=scopeB)]
    ps_av2 = [b.psum("lb_psav2", (128, 256), scope=scopeB) for _ in range(2)]
    ps_rep2 = ps_qkv[0]
    ones1 = b.const_ones(1, 512)
    ones64f = b.const_ones(1, 64, F32)
    def flw(g):
        for mi in range(8):
            sg.inc(g.dma_start(out=wqsb[:, mi * 8 * 128:(mi + 1) * 8 * 128],
                               in_=W["l1_wq"][mi]), 16)
            sg.inc(g.dma_start(out=wksb[:, mi * 8 * 128:(mi + 1) * 8 * 128],
                               in_=W["l1_wk"][mi]), 16)
        for kc in range(8):
            sg.inc(g.dma_start(out=wvsb[:, kc * 1024:(kc + 1) * 1024],
                               in_=W["l1_wv"][kc * 128:(kc + 1) * 128, :]), 16)
        sg.inc(g.dma_start(out=bqkvsb[:, :], in_=W["l1_bqkv"]), 16)
        wm = W["wmask"].rearrange("f k p c -> (f k) p c")
        for a in range(8):
            sg.inc(g.dma_start(out=masksb[:, a * 256:(a + 1) * 256], in_=wm[a]), 16)
    b.sec("gpsimd", flw)
    def fvv(v):
        sv.inc(v.memset(vwin[:, :], 1.0), 1)
    b.sec("vector", fvv)
    b.fence()

    gQKV = [None, None]
    g_headT = [None]
    g_headV = [None]
    qkv_done = [None]

    def qkv_window(col0, n, par, do_q):
        parts = [("k", wksb, D, kwin, par)]
        if do_q:
            parts.append(("q", wqsb, 0, qwin, par))
        for (which, wsb, boff, dst, par_) in parts:
            ST0 = st.v
            SV0 = sv.v
            def ft(t, col0=col0, n=n, wsb=wsb, boff=boff, SV0=SV0,
                   gq=tuple(gQKV), gh=g_headT[0], ghv=g_headV[0]):
                if gh:
                    t.wait_ge(gh[0], gh[1])
                if ghv:
                    t.wait_ge(ghv[0], ghv[1])
                for mi in range(8):
                    pi = mi % 2
                    if mi >= 2:
                        t.wait_ge(sv.h, SV0 + mi - 1)
                    elif gq[pi]:
                        t.wait_ge(gq[pi][0], gq[pi][1])
                    for kc in range(8):
                        t.matmul(ps_qkv[pi][:, 0:n],
                                 wsb[:, (mi * 8 + kc) * 128:(mi * 8 + kc + 1) * 128],
                                 xn_full[:, kc * ncol + col0:kc * ncol + col0 + n],
                                 start=(kc == 0), stop=False)
                    st.inc(t.matmul(ps_qkv[pi][:, 0:n],
                                    bqkvsb[:, boff + mi * 128:boff + (mi + 1) * 128],
                                    ones1[:, 0:n], start=False, stop=True), 1)
            b.sec("tensor", ft)
            def fv(v, n=n, dst=dst, par_=par_, ST0=ST0, iskw=(which == "k")):
                for mi in range(8):
                    o = (par_ * 8 + mi) * 256 if iskw else mi * 256
                    v.wait_ge(st.h, ST0 + mi + 1)
                    sv.inc(v.tensor_copy(dst[:, o:o + n],
                                         ps_qkv[mi % 2][:, 0:n]), 1)
            b.sec("vector", fv)
            gQKV[0] = (sv.h, SV0 + 7)
            gQKV[1] = (sv.h, SV0 + 8)
        ST0 = st.v
        SV0 = sv.v
        combos = [(tt, vb) for tt in range(2) for vb in range(2)]
        def ftv(t, col0=col0, SV0=SV0, gq=tuple(gQKV)):
            for j, (tt, vb) in enumerate(combos):
                pi = vb % 2
                if j >= 2:
                    t.wait_ge(sv.h, SV0 + j - 1)
                elif gq[pi]:
                    t.wait_ge(gq[pi][0], gq[pi][1])
                for kc in range(8):
                    t.matmul(ps_qkv[pi][:, :],
                             xn_full[:, kc * ncol + col0 + tt * 128:
                                     kc * ncol + col0 + (tt + 1) * 128],
                             wvsb[:, kc * 1024 + vb * 512:kc * 1024 + (vb + 1) * 512],
                             start=(kc == 0), stop=False)
                st.inc(t.matmul(ps_qkv[pi][:, :], ones1[0:1, 0:128],
                                bqkvsb[:, 2 * D + vb * 512:2 * D + (vb + 1) * 512],
                                start=False, stop=True), 1)
        b.sec("tensor", ftv)
        def fvv(v, par=par, ST0=ST0):
            for j, (tt, vb) in enumerate(combos):
                v.wait_ge(st.h, ST0 + j + 1)
                sv.inc(v.tensor_copy(
                    vwin[:, (par * 2 + tt) * 1040:(par * 2 + tt + 1) * 1040]
                    .rearrange("p (h c) -> p h c", h=16)[:, vb * 8:(vb + 1) * 8, 0:64],
                    ps_qkv[vb % 2][:, :].rearrange("p (h c) -> p h c", h=8)), 1)
        b.sec("vector", fvv)
        gQKV[0] = (sv.h, SV0 + 3)
        gQKV[1] = (sv.h, SV0 + 4)
        qkv_done[0] = sv.snap()

    qkv_window(tpc, HALO, 0, False)
    gS2W = None
    gAoW = None
    for w in range(nwin):
        par_cur = (w + 1) % 2
        par_prev = w % 2
        qkv_window(w * WINDOW, WINDOW, par_cur, True)
        first = (w == 0)
        moff = (4 * 256) if first else 0
        STB0 = st.v
        SSE0 = ss.v
        SVB0 = sv.v
        SGB0 = sg.v
        def ftH(t, par_cur=par_cur, par_prev=par_prev, qd=qkv_done[0],
                gS2W=gS2W, STB0=STB0, SSE0=SSE0, SVB0=SVB0):
            t.wait_ge(qd[0], qd[1])
            if gS2W:
                t.wait_ge(gS2W[0], gS2W[1])
            for h in range(17):
                if h < 16:
                    hp, hpar = h // 2, h % 2
                    if h >= 1:
                        t.wait_ge(ss.h, SSE0 + h)
                    for kc in range(4):
                        par = par_prev if kc < 2 else par_cur
                        koff = (par * 8 + hp) * 256 + (kc % 2) * 128
                        i = t.matmul(ps_S2[0][:, kc * 256:(kc + 1) * 256],
                                     kwin[hpar * 64:hpar * 64 + 64, koff:koff + 128],
                                     qwin[hpar * 64:hpar * 64 + 64,
                                          hp * 256:(hp + 1) * 256],
                                     start=True, stop=True)
                    st.inc(i, 1)
                if h >= 1:
                    j = h - 1
                    t.wait_ge(sv.h, SVB0 + 3 * j + 1)
                    for kc in range(4):
                        par = par_prev if kc < 2 else par_cur
                        vt = par * 2 + (kc % 2)
                        i = t.matmul(ps_av2[j % 2][0:65, 0:256],
                                     vwin[:, vt * 1040 + j * 65:vt * 1040 + j * 65 + 65],
                                     P2[j % 2][:, kc * 256:(kc + 1) * 256],
                                     start=(kc == 0), stop=(kc == 3))
                    st.inc(i, 1)
                    t.wait_ge(sv.h, SVB0 + 3 * j + 2)
                    st.inc(t.matmul(ps_rep2[0:64, 0:256], ones64f[0:1, 0:64],
                                    den2[j % 2][0:1, :], start=True, stop=True), 1)
        b.sec("tensor", ftH)
        def fsH(s, STB0=STB0):
            for h in range(16):
                s.wait_ge(st.h, STB0 + (1 if h == 0 else 3 * h - 1))
                ss.inc(s.activation(P2[h % 2][:, :], ps_S2[0][:, :],
                                    AF.Exp, scale=0.125), 1)
        b.sec("scalar", fsH)
        gS2W = (ss.h, SSE0 + 16)
        def fvH(v, moff=moff, STB0=STB0, SSE0=SSE0, SGB0=SGB0, gAoW=gAoW):
            for h in range(16):
                pa = h % 2
                v.wait_ge(ss.h, SSE0 + h + 1)
                for kc in range(4):
                    i = v.tensor_tensor(P2[pa][:, kc * 256:(kc + 1) * 256],
                                        P2[pa][:, kc * 256:(kc + 1) * 256],
                                        masksb[:, moff + kc * 256:moff + (kc + 1) * 256],
                                        Alu.mult)
                sv.inc(i, 1)
                v.wait_ge(st.h, STB0 + (3 * h + 3 if h <= 14 else 47))
                v.tensor_copy(den2[pa][0:1, :], ps_av2[pa][64:65, 0:256])
                sv.inc(v.reciprocal(den2[pa][0:1, :], den2[pa][0:1, :]), 1)
                v.wait_ge(st.h, STB0 + (3 * h + 4 if h <= 14 else 48))
                if h >= 2:
                    v.wait_ge(sg.h, SGB0 + 16 * (h - 1))
                elif gAoW:
                    v.wait_ge(gAoW[0], gAoW[1])
                v.tensor_copy(rep2[pa][:, :], ps_rep2[0:64, 0:256])
                sv.inc(v.tensor_tensor(ao2_sb[pa][:, :], ps_av2[pa][0:64, 0:256],
                                       rep2[pa][:, :], Alu.mult), 1)
        b.sec("vector", fvH)
        def fgH(g, w=w, SVB0=SVB0):
            for h in range(16):
                hp, hpar = h // 2, h % 2
                g.wait_ge(sv.h, SVB0 + 3 * h + 3)
                sg.inc(g.dma_start(
                    out=attnT_hbm[hp * 128 + hpar * 64:hp * 128 + hpar * 64 + 64,
                                  w * WINDOW:(w + 1) * WINDOW],
                    in_=ao2_sb[hpar][:, :]), 16)
        b.sec("gpsimd", fgH)
        gAoW = (sg.h, SGB0 + 16 * 16)
        g_headT[0] = (st.h, st.v)
        g_headV[0] = (sv.h, SVB0 + 3 * 15 + 3)
    b.fence()
    scopeB.close()

    scopeC = ExitStack()
    wgsb = b.sbuf("lb_wg", (128, 64 * 128), BF16, scope=scopeC)
    wdsb = b.sbuf("lb_wd", (128, 8 * 128), BF16, scope=scopeC)
    wd2sb = b.sbuf("lb_wd2", (128, 8 * 128), BF16, scope=scopeC)
    bgrow = b.sbuf("lb_bg", (1, D), BF16, scope=scopeC)
    burow = b.sbuf("lb_bu", (1, SDIM), BF16, scope=scopeC)
    drive = b.sbuf("lb_drive", (128, HALO + tpc), F32, scope=scopeC)
    onesbig = b.sbuf("lb_onesbig", (128, HALO + tpc), F32, scope=scopeC)
    ps_g = [b.psum("lb_psg", (128, 512), scope=scopeC) for _ in range(2)]
    def flc(g):
        for mi in range(8):
            sg.inc(g.dma_start(out=wgsb[:, mi * 8 * 128:(mi + 1) * 8 * 128],
                               in_=W["l1_wg"][mi]), 16)
        sg.inc(g.dma_start(out=wdsb[:, :], in_=W["l1_wd"][0]), 16)
        sg.inc(g.dma_start(out=wd2sb[:, :], in_=W["l1_wd2"][0]), 16)
        sg.inc(g.dma_start(out=bgrow[:, :], in_=W["l1_bg"]), 16)
        sg.inc(g.dma_start(out=burow[:, :], in_=W["l1_bu"]), 16)
    b.sec("gpsimd", flc)
    def fvc(v, sgv=sg.v):
        v.wait_ge(sg.h, sgv)
        sv.inc(v.memset(onesbig[:, :], 1.0), 1)
        i = v.tensor_tensor(wdsb[:, :], wdsb[:, :], wd2sb[:, :], Alu.add)
        sv.inc(i, 1)
    b.sec("vector", fvc)
    b.fence()
    nbl2 = blk_list(tpc, 512)
    combosG = [(mi, qo, nq) for mi in range(8) for (qo, nq) in nbl2]
    STC0 = st.v
    SSC0 = ss.v
    def ftG(t, combosG=combosG, SSC0=SSC0, ssv=ss.v):
        t.wait_ge(ss.h, ssv)
        for j, (mi, qo, nq) in enumerate(combosG):
            pi = j % 2
            if j >= 2:
                t.wait_ge(ss.h, SSC0 + j - 1)
            for kc in range(8):
                t.matmul(ps_g[pi][:, 0:nq],
                         wgsb[:, (mi * 8 + kc) * 128:(mi * 8 + kc + 1) * 128],
                         xn_full[:, kc * ncol + qo:kc * ncol + qo + nq],
                         start=(kc == 0), stop=False)
            st.inc(t.matmul(ps_g[pi][:, 0:nq], bgrow[:, mi * 128:(mi + 1) * 128],
                            ones1[:, 0:nq], start=False, stop=True), 1)
    b.sec("tensor", ftG)
    def fsG(s, combosG=combosG, STC0=STC0):
        for j, (mi, qo, nq) in enumerate(combosG):
            s.wait_ge(st.h, STC0 + j + 1)
            ss.inc(s.activation(g_sb[:, mi * tpc + qo:mi * tpc + qo + nq],
                                ps_g[j % 2][:, 0:nq], AF.Exp, scale=-1.0), 1)
    b.sec("scalar", fsG)
    def fvG(v, combosG=combosG, SSC0=SSC0):
        for j, (mi, qo, nq) in enumerate(combosG):
            v.wait_ge(ss.h, SSC0 + j + 1)
            sl = g_sb[:, mi * tpc + qo:mi * tpc + qo + nq]
            v.tensor_scalar(lns.xb[:, 0:nq], sl, 1.0, None, Alu.add)
            with b.nc.allow_low_precision(reason="sigmoid gate stored bf16"):
                sv.inc(v.reciprocal(sl, lns.xb[:, 0:nq]), 1)
    b.sec("vector", fvG)
    combosD = [(tpc, HALO, 0)] + [(w * 512, min(512, tpc - w * 512), HALO + w * 512)
                                  for w in range((tpc + 511) // 512)]
    STD0 = st.v
    SVD0 = sv.v
    nG = len(combosG)
    def ftD(t, combosD=combosD, SSC0=SSC0, nG=nG, SVD0=SVD0):
        t.wait_ge(ss.h, SSC0 + nG)
        for j, (c0, n, dof) in enumerate(combosD):
            pi = j % 2
            if j >= 2:
                t.wait_ge(sv.h, SVD0 + j - 1)
            for kc in range(8):
                t.matmul(ps_g[pi][:, 0:n],
                         wdsb[:, kc * 128:(kc + 1) * 128],
                         xn_full[:, kc * ncol + c0:kc * ncol + c0 + n],
                         start=(kc == 0), stop=False)
            st.inc(t.matmul(ps_g[pi][:, 0:n], burow[:, 0:SDIM],
                            ones1[:, 0:n], start=False, stop=True), 1)
    b.sec("tensor", ftD)
    def fvD2(v, combosD=combosD, STD0=STD0):
        for j, (c0, n, dof) in enumerate(combosD):
            v.wait_ge(st.h, STD0 + j + 1)
            sv.inc(v.tensor_copy(drive[:, dof:dof + n], ps_g[j % 2][:, 0:n]), 1)
    b.sec("vector", fvD2)
    def fvs(v, svv=sv.v):
        v.tensor_tensor(drive[:, 0:HALO], drive[:, 0:HALO], hgate[:, :], Alu.mult)
        v.tensor_scalar(onesbig[:, :], onesbig[:, :], acol[:, :], None, Alu.mult)
        sv.inc(v.tensor_tensor_scan(states[:, :], onesbig[:, :], drive[:, :], 0.0,
                                    Alu.mult, Alu.add), 1)
    b.sec("vector", fvs)
    states_bf = b.sbuf("lb_statesbf", (128, tpc), BF16, scope=scopeC)
    def fsc(s, svv=sv.v):
        s.wait_ge(sv.h, svv)
        ss.inc(s.copy(states_bf[:, :], states[:, HALO:]), 1)
    b.sec("scalar", fsc)
    b.fence()
    scopeC.close()
    scopeA.close()

    scopeD = ExitStack()
    wosb = b.sbuf("lb_wo", (128, 8 * 8 * 128), BF16, scope=scopeD)
    cwsb = b.sbuf("lb_cw", (128, D), BF16, scope=scopeD)
    borow = b.sbuf("lb_bo", (1, D), BF16, scope=scopeD)
    ao_sb = b.sbuf("lb_ao", (128, 512), F32, scope=scopeD)
    a_nb = b.sbuf("lb_anb", (128, 8 * 512), BF16, scope=scopeD)
    y0_nb = b.sbuf("lb_y0nb", (128, 8 * 512), F32, scope=scopeD)
    t2_sb = b.sbuf("lb_t2", (128, 512), F32, scope=scopeD)
    ps_o = [b.psum("lb_pso", (128, 512), scope=scopeD) for _ in range(2)]
    ps_y = [b.psum("lb_psy", (128, 512), scope=scopeD) for _ in range(2)]
    def fld(g):
        sg.inc(g.dma_start(out=cwsb[:, :], in_=W["l1_cw"]), 16)
        sg.inc(g.dma_start(out=borow[:, :], in_=W["l1_bo"]), 16)
        for mi in range(8):
            sg.inc(g.dma_start(out=wosb[:, mi * 1024:(mi + 1) * 1024],
                               in_=W["l1_wo"][mi]), 16)
    b.sec("gpsimd", fld)
    b.fence()
    g_buf = None
    g_anb = None
    for (qo, nq) in nbl2:
        SD0 = sd.v
        def fla(sy, qo=qo, nq=nq, ga=g_anb, gb=g_buf):
            if ga:
                sy.wait_ge(ga[0], ga[1])
            if gb:
                sy.wait_ge(gb[0], gb[1])
            for kc in range(8):
                sd.inc(sy.dma_start(out=a_nb[:, kc * 512:kc * 512 + nq],
                                    in_=attnT_hbm[kc * 128:(kc + 1) * 128, qo:qo + nq]), 16)
            for mi in range(8):
                sd.inc(sy.dma_start(out=y0_nb[:, mi * 512:mi * 512 + nq],
                                    in_=y0_hbm[mi * 128:(mi + 1) * 128, qo:qo + nq]), 16)
        b.sec("sync", fla)
        STD = st.v
        SVD = sv.v
        def ftDD(t, qo=qo, nq=nq, sgv=sg.v, sdv=sd.v, ssv=ss.v, SVD=SVD,
                 gps=(g_anb is not None)):
            t.wait_ge(sg.h, sgv)
            t.wait_ge(sd.h, sdv)
            t.wait_ge(ss.h, ssv)
            for mi in range(8):
                pi = mi % 2
                if mi >= 2:
                    t.wait_ge(sv.h, SVD + mi - 1)
                for kc in range(8):
                    t.matmul(ps_o[pi][:, 0:nq],
                             wosb[:, mi * 1024 + kc * 128:mi * 1024 + (kc + 1) * 128],
                             a_nb[:, kc * 512:kc * 512 + nq],
                             start=(kc == 0), stop=False)
                t.matmul(ps_o[pi][:, 0:nq], borow[:, mi * 128:(mi + 1) * 128],
                         ones1[:, 0:nq], start=False, stop=True)
                st.inc(t.matmul(ps_y[pi][:, 0:nq], cwsb[:, mi * 128:(mi + 1) * 128],
                                states_bf[:, qo:qo + nq], start=True, stop=True), 1)
        b.sec("tensor", ftDD)
        def fvDD(v, qo=qo, nq=nq, STD=STD):
            for mi in range(8):
                pi = mi % 2
                v.wait_ge(st.h, STD + mi + 1)
                v.tensor_copy(ao_sb[:, 0:nq], ps_o[pi][:, 0:nq])
                v.tensor_tensor(t2_sb[:, 0:nq], ao_sb[:, 0:nq], ps_y[pi][:, 0:nq],
                                Alu.subtract)
                v.tensor_tensor(t2_sb[:, 0:nq], t2_sb[:, 0:nq],
                                g_sb[:, mi * tpc + qo:mi * tpc + qo + nq], Alu.mult)
                v.tensor_tensor(y0_nb[:, mi * 512:mi * 512 + nq],
                                y0_nb[:, mi * 512:mi * 512 + nq], ps_y[pi][:, 0:nq],
                                Alu.add)
                sv.inc(v.tensor_tensor(y0_nb[:, mi * 512:mi * 512 + nq],
                                       y0_nb[:, mi * 512:mi * 512 + nq], t2_sb[:, 0:nq],
                                       Alu.add), 1)
        b.sec("vector", fvDD)
        def fgo(g, qo=qo, nq=nq, SVD=SVD):
            for mi in range(8):
                g.wait_ge(sv.h, SVD + mi + 1)
                sg.inc(g.dma_start(out=xin1_hbm[mi * 128:(mi + 1) * 128, qo:qo + nq],
                                   in_=y0_nb[:, mi * 512:mi * 512 + nq]), 16)
        b.sec("gpsimd", fgo)
        g_buf = sg.snap()
        g_anb = (st.h, st.v)
    b.fence()
    scopeD.close()
    scopeCD.close()

    scopeE = ExitStack()
    w1sb = b.sbuf("lb_w1", (128, 32 * 1024), BF16, scope=scopeE)
    w2sb = b.sbuf("lb_w2", (128, 8 * 4096), BF16, scope=scopeE)
    b1row = b.sbuf("lb_b1", (1, 4096), BF16, scope=scopeE)
    b2row = b.sbuf("lb_b2", (1, D), BF16, scope=scopeE)
    h1_sb = b.sbuf("lb_h1", (128, 32 * NB), BF16, scope=scopeE)
    x6_nb = b.sbuf("lb_x6nb", (128, NB), F32, scope=scopeE)
    ps_m = [b.psum("lb_psm", (128, NB), scope=scopeE) for _ in range(2)]
    def fle(g):
        sg.inc(g.dma_start(out=b1row[:, :], in_=W["l1_b1"]), 16)
        sg.inc(g.dma_start(out=b2row[:, :], in_=W["l1_b2"]), 16)
        for m1 in range(32):
            sg.inc(g.dma_start(out=w1sb[:, m1 * 1024:(m1 + 1) * 1024],
                               in_=W["l1_w1"][m1]), 16)
        for mi in range(8):
            sg.inc(g.dma_start(out=w2sb[:, mi * 4096:(mi + 1) * 4096],
                               in_=W["l1_w2"][mi]), 16)
    b.sec("gpsimd", fle)
    lns.guard = None
    g_h1 = [None, None]
    g_m = [None, None]
    g_x6 = None
    for (off, n) in blk_list(tpc, NB):
        lns.block(xin1_hbm, tpc, off, n, gain_col=ln2g, bias_col=ln2b)
        for m1 in range(32):
            pi = m1 % 2
            def ft(t, m1=m1, n=n, pi=pi, sgv=sg.v, svv=sv.v, gp=g_m[pi]):
                t.wait_ge(sg.h, sgv)
                t.wait_ge(sv.h, svv)
                if gp:
                    t.wait_ge(gp[0], gp[1])
                for kc in range(8):
                    t.matmul(ps_m[pi][:, 0:n],
                             w1sb[:, m1 * 1024 + kc * 128:m1 * 1024 + (kc + 1) * 128],
                             lns.xn[:, kc * NB:kc * NB + n],
                             start=(kc == 0), stop=False)
                st.inc(t.matmul(ps_m[pi][:, 0:n], b1row[:, m1 * 128:(m1 + 1) * 128],
                                ones1[:, 0:n], start=False, stop=True), 1)
            b.sec("tensor", ft)
            g_h1[0] = (st.h, st.v)
            def fs(s, m1=m1, n=n, pi=pi, stv=st.v):
                s.wait_ge(st.h, stv)
                ss.inc(s.activation(h1_sb[:, m1 * NB:m1 * NB + n], ps_m[pi][:, 0:n],
                                    AF.Gelu), 1)
            b.sec("scalar", fs)
            g_m[pi] = ss.snap()
        for mi in range(8):
            pi = mi % 2
            def fx(sy, mi=mi, off=off, n=n, gx=g_x6):
                if gx:
                    sy.wait_ge(gx[0], gx[1])
                sd.inc(sy.dma_start(out=x6_nb[:, 0:n],
                                    in_=x6_hbm[mi * 128:(mi + 1) * 128, off:off + n]), 16)
            b.sec("sync", fx)
            def ft(t, mi=mi, n=n, pi=pi, sgv=sg.v, ssv=ss.v, gp=g_m[pi]):
                t.wait_ge(sg.h, sgv)
                t.wait_ge(ss.h, ssv)
                if gp:
                    t.wait_ge(gp[0], gp[1])
                for kc in range(32):
                    t.matmul(ps_m[pi][:, 0:n],
                             w2sb[:, mi * 4096 + kc * 128:mi * 4096 + (kc + 1) * 128],
                             h1_sb[:, kc * NB:kc * NB + n],
                             start=(kc == 0), stop=False)
                st.inc(t.matmul(ps_m[pi][:, 0:n], b2row[:, mi * 128:(mi + 1) * 128],
                                ones1[:, 0:n], start=False, stop=True), 1)
            b.sec("tensor", ft)
            g_h1[1] = (st.h, st.v)
            def fv(v, mi=mi, off=off, n=n, pi=pi, stv=st.v, sdv=sd.v):
                v.wait_ge(st.h, stv)
                v.wait_ge(sd.h, sdv)
                v.tensor_tensor(x6_nb[:, 0:n], x6_nb[:, 0:n], ps_m[pi][:, 0:n], Alu.add)
                sv.inc(v.tensor_tensor(x6_nb[:, 0:n], x6_nb[:, 0:n],
                                       lns.x_nb[:, mi * NB:mi * NB + n], Alu.add), 1)
            b.sec("vector", fv)
            g_m[pi] = sv.snap()
            def fgo(g, mi=mi, off=off, n=n, svv=sv.v):
                g.wait_ge(sv.h, svv)
                sg.inc(g.dma_start(out=x7_hbm[mi * 128:(mi + 1) * 128, off:off + n],
                                   in_=x6_nb[:, 0:n]), 16)
            b.sec("gpsimd", fgo)
            g_x6 = sg.snap()
        lns.guard = (st.h, st.v)
    b.fence()
    scopeE.close()


def build_la7_program(cfg):
    b = B(cfg)
    NCOL, TPC = cfg.NCOL, cfg.TPC
    x6c = b.dram_in("x6T_cols", (D, NCOL))
    W = {
        "ln1g_col": b.dram_in("ln1g_col", (128, 8)),
        "ln1b_col": b.dram_in("ln1b_col", (128, 8)),
        "ln2g_col": b.dram_in("ln2g_col", (128, 8)),
        "ln2b_col": b.dram_in("ln2b_col", (128, 8)),
        "A_col": b.dram_in("A_col", (128, 1)),
        "halo_gate": b.dram_in("halo_gate", (128, HALO)),
        "wmask": b.dram_in("wmask", (2, 4, 128, 256)),
        "l1_wq": b.dram_in("l1_wq", (8, 128, 8 * 128)),
        "l1_wk": b.dram_in("l1_wk", (8, 128, 8 * 128)),
        "l1_wv": b.dram_in("l1_wv", (D, D)),
        "l1_bqkv": b.dram_in("l1_bqkv", (1, 3 * D)),
        "l1_wo": b.dram_in("l1_wo", (8, 128, 8 * 128)),
        "l1_bo": b.dram_in("l1_bo", (1, D)),
        "l1_wg": b.dram_in("l1_wg", (8, 128, 8 * 128)),
        "l1_bg": b.dram_in("l1_bg", (1, D)),
        "l1_wd": b.dram_in("l1_wd", (1, 128, 8 * 128)),
        "l1_wd2": b.dram_in("l1_wd2", (1, 128, 8 * 128)),
        "l1_bu": b.dram_in("l1_bu", (1, SDIM)),
        "l1_cw": b.dram_in("l1_cw", (SDIM, D)),
        "l1_w1": b.dram_in("l1_w1", (32, 128, 8 * 128)),
        "l1_b1": b.dram_in("l1_b1", (1, 4096)),
        "l1_w2": b.dram_in("l1_w2", (8, 128, 32 * 128)),
        "l1_b2": b.dram_in("l1_b2", (1, D)),
    }
    x7c = b.dram_out("x7T_own", (D, TPC))
    b.start()
    emit_l1_attn_b(b, x6c, x7c, W, NCOL, TPC)
    def fo(g):
        for dn in ("gdma", "sdma"):
            sm = b.sem(dn)
            g.wait_ge(sm.h, sm.v)
    b.sec("gpsimd", fo)
    return b.finish()


def prep_la7_inputs(inputs, cfg, core):
    q = core % 4
    wqkv = _np(inputs["L1_attn_b_W_qkv"])
    wug = _np(inputs["L1_attn_b_W_ug"])
    mask = np.zeros((2, 4, 128, 256), np.float32)
    qi = np.arange(256)[None, :]
    for kc in range(4):
        p = np.arange(128)[:, None]
        if kc < 2:
            mask[0, kc] = ((kc * 128 + p) > qi).astype(np.float32)
        else:
            mask[0, kc] = (((kc - 2) * 128 + p) <= qi).astype(np.float32)
    mask[1] = mask[0]
    if q == 0:
        mask[1, 0] = 0.0
        mask[1, 1] = 0.0
    hg = np.ones((128, HALO), np.float32) if q > 0 else np.zeros((128, HALO), np.float32)
    col = lambda a: np.ascontiguousarray(_np(a).reshape(8, 128).T)
    return {
        "ln1g_col": col(inputs["L1_attn_b_ln1_g"]),
        "ln1b_col": col(inputs["L1_attn_b_ln1_b"]),
        "ln2g_col": col(inputs["L1_attn_b_ln2_g"]),
        "ln2b_col": col(inputs["L1_attn_b_ln2_b"]),
        "A_col": _np(inputs["L1_attn_b_A"]).reshape(128, 1),
        "halo_gate": hg,
        "wmask": mask,
        "l1_wq": _blockify_lhs(wqkv[:, 0:D]),
        "l1_wk": _blockify_lhs(wqkv[:, D:2 * D]),
        "l1_wv": np.ascontiguousarray(wqkv[:, 2 * D:]),
        "l1_bqkv": np.zeros((1, 3 * D), np.float32),
        "l1_wo": _blockify_lhs(_np(inputs["L1_attn_b_W_O"])),
        "l1_bo": _np(inputs["L1_attn_b_b_O"]).reshape(1, D),
        "l1_wg": _blockify_lhs(wug[:, SDIM:]),
        "l1_bg": _np(inputs["L1_attn_b_b_ug"])[SDIM:].reshape(1, D).copy(),
        "l1_wd": _blockify_lhs(_np(inputs["L1_attn_b_B_w"])),
        "l1_wd2": _blockify_lhs(wug[:, 0:SDIM]),
        "l1_bu": _np(inputs["L1_attn_b_b_ug"])[0:SDIM].reshape(1, SDIM).copy(),
        "l1_cw": _np(inputs["L1_attn_b_C_w"]),
        "l1_w1": _blockify_lhs(_np(inputs["L1_attn_b_mlp_W1"])),
        "l1_b1": _np(inputs["L1_attn_b_mlp_b1"]).reshape(1, 4096),
        "l1_w2": _blockify_lhs(_np(inputs["L1_attn_b_mlp_W2"])),
        "l1_b2": _np(inputs["L1_attn_b_mlp_b2"]).reshape(1, D),
    }



class Cfg:
    def __init__(self, TPC=2048):
        assert TPC % WINDOW == 0
        self.TPC = TPC
        self.Tb = 4 * TPC
        self.NCOL = TPC + EX
        self.NB = 256



def build_la1_program(cfg, dbg_stop=None):
    b = B(cfg)
    x0T = b.dram_in("x0T_full", (D, cfg.Tb))
    hT = b.dram_out("hT_full", (D, cfg.Tb))
    b.start()
    emit_la1(b, x0T, hT, dbg_stop=dbg_stop)
    return b.finish()


def prep_la1_inputs(inputs, batch):
    ig_b = _np(inputs["L0_attn_a_ig_b"]).reshape(1, D)
    ag_b = _np(inputs["L0_attn_a_ag_b"]).reshape(1, D)
    return {
        "ig_w": _np(inputs["L0_attn_a_ig_w"]),
        "ag_w": _np(inputs["L0_attn_a_ag_w"]),
        "ig_bias_row": ig_b,
        "ag_bias_row": ag_b,
        "a_param_col": np.ascontiguousarray(_np(inputs["L0_attn_a_a_param"]).reshape(8, 128).T),
    }


def _run(nc, in_maps):
    return run_bass_kernel_spmd(nc, in_maps, core_ids=list(range(8)))


def build_la8_program(cfg):
    b = B(cfg)
    TPC = cfg.TPC
    xsrc = b.dram_in("x7T_own", (D, TPC))
    selblob = b.dram_in("sel_blob", (8, 8, 128, 1024))
    selb = b.dram_in("selb_rows", (8, 1, D))
    compblob = b.dram_in("comp_blob", (8, 8, 128, 1024))
    compb = b.dram_in("compb_rows", (8, 1, D))
    xdst = b.dram_out("x8T_own", (D, TPC))
    b.start()
    emit_l1_ffn(b, xsrc, xdst, selblob, selb, compblob, compb, TPC)
    def fo(g):
        for dn in ("gdma", "sdma"):
            sm = b.sem(dn)
            g.wait_ge(sm.h, sm.v)
    b.sec("gpsimd", fo)
    return b.finish()


class APView:
    def __init__(self, ap, axis, shift):
        self.ap, self.axis, self.shift = ap, axis, shift

    def __getitem__(self, idx):
        if not isinstance(idx, tuple):
            idx = (idx,)
        idx = list(idx)
        i = idx[self.axis]
        if isinstance(i, slice):
            idx[self.axis] = slice(i.start + self.shift, i.stop + self.shift)
        else:
            idx[self.axis] = i + self.shift
        return self.ap[tuple(idx)]


def emit_la2_stage(b, x0c, hc, x2c, qTo, khp_out, vhg_out, cosin, sh, kv_only=False):
    TPC, NB = b.cfg.TPC, b.cfg.NB
    ncol = TPC if kv_only else b.cfg.NCOL
    emit_resadd_hbm(b, x2c, x0c, hc, ncol)
    b.fence()
    emit_ffn0_stage(b, x2c, x2c, sh["cpblob"], sh["w1blob"], sh["w2blob"], ncol)
    lns = LnStream(b, NB)
    blks = blk_list(ncol, NB)
    scope = ExitStack()
    qw = mk_qkv_state(b, scope, sh["wq"], sh["wk"], sh["wv"], sh["bq"], sh["bk"],
                      sh["bv"], cosin, sh["perm"], qTo, khp_out, vhg_out, ncol)
    b.fence()
    lns.guard = None
    for (off, n) in blks:
        lns.block(x2c, ncol, off, n)
        emit_qkv_block(b, lns, qw, off, n, TPC, ncol, do_q=not kv_only)
        lns.guard = b.sem("gdma").snap()
    b.fence()
    scope.close()


def build_fused_program(cfg):
    b = B(cfg)
    NCOL, TPC, NB, Tb = cfg.NCOL, cfg.TPC, cfg.NB, cfg.Tb
    nkt_q = TPC // 128
    x0T_full = b.dram_in("x0T_full", (D, Tb))
    x0c_s = [b.dram_in(f"x0c_s{s}", (D, NCOL if s == 0 else TPC)) for s in range(4)]
    cosin_s = [b.dram_in(f"cosin_s{s}", (2, 128, NCOL if s == 0 else TPC), BF16)
               for s in range(4)]
    sh = {
        "w1blob": b.dram_in("w1blob", (128, 32 * 128), BF16),
        "w2blob": b.dram_in("w2blob", (128, 128 * 128), BF16),
        "cpblob": b.dram_in("cpblob", (8, 128, 128 * 128), BF16),
        "wq": b.dram_in("wq_blob", (8, 128, 8 * 128), BF16),
        "wk": b.dram_in("wk_blob", (8, 128, 8 * 128), BF16),
        "wv": b.dram_in("wv_nat", (D, D), BF16),
        "bq": b.dram_in("bq_row", (1, D), BF16),
        "bk": b.dram_in("bk_row", (1, D), BF16),
        "bv": b.dram_in("bv_row", (1, D), BF16),
        "perm": b.dram_in("rope_perm", (128, 128), BF16),
    }
    wout = b.dram_in("wout_blob", (8, 128, 8 * 128), BF16)
    bout = b.dram_in("bout_row", (1, D), BF16)
    kT = b.dram_in("l1a_kT", (D, KLEN), BF16)
    vn = b.dram_in("l1a_v", (KLEN, D), BF16)
    selblob = b.dram_in("sel_blob", (8, 8, 128, 1024), BF16)
    selb = b.dram_in("selb_rows", (8, 1, D), BF16)
    compblob = b.dram_in("comp_blob", (8, 8, 128, 1024), BF16)
    compb = b.dram_in("compb_rows", (8, 1, D), BF16)
    W7 = {
        "ln1g_col": b.dram_in("ln1g_col", (128, 8)),
        "ln1b_col": b.dram_in("ln1b_col", (128, 8)),
        "ln2g_col": b.dram_in("ln2g_col", (128, 8)),
        "ln2b_col": b.dram_in("ln2b_col", (128, 8)),
        "A_col": b.dram_in("A_col", (128, 1)),
        "halo_gate": b.dram_in("halo_gate", (128, HALO)),
        "wmask": b.dram_in("wmask", (2, 4, 128, 256), BF16),
        "l1_wq": b.dram_in("l1_wq", (8, 128, 8 * 128), BF16),
        "l1_wk": b.dram_in("l1_wk", (8, 128, 8 * 128), BF16),
        "l1_wv": b.dram_in("l1_wv", (D, D), BF16),
        "l1_bqkv": b.dram_in("l1_bqkv", (1, 3 * D), BF16),
        "l1_wo": b.dram_in("l1_wo", (8, 128, 8 * 128), BF16),
        "l1_bo": b.dram_in("l1_bo", (1, D), BF16),
        "l1_wg": b.dram_in("l1_wg", (8, 128, 8 * 128), BF16),
        "l1_bg": b.dram_in("l1_bg", (1, D), BF16),
        "l1_wd": b.dram_in("l1_wd", (1, 128, 8 * 128), BF16),
        "l1_wd2": b.dram_in("l1_wd2", (1, 128, 8 * 128), BF16),
        "l1_bu": b.dram_in("l1_bu", (1, SDIM), BF16),
        "l1_cw": b.dram_in("l1_cw", (SDIM, D), BF16),
        "l1_w1": b.dram_in("l1_w1", (32, 128, 8 * 128), BF16),
        "l1_b1": b.dram_in("l1_b1", (1, 4096), BF16),
        "l1_w2": b.dram_in("l1_w2", (8, 128, 32 * 128), BF16),
        "l1_b2": b.dram_in("l1_b2", (1, D), BF16),
    }
    hT_full = b.dram_tmp("f_hT", (D, Tb))
    hc_s = [b.dram_tmp(f"f_hc{s}", (D, NCOL if s == 0 else TPC)) for s in range(4)]
    x2c_s = [b.dram_tmp(f"f_x2c{s}", (D, NCOL if s == 0 else TPC)) for s in range(4)]
    qT_s = [b.dram_tmp("f_qT0", (D, NCOL), BF16)]
    khp_full = b.dram_tmp("f_khp", (8, 128, Tb), BF16)
    vhg_full = b.dram_tmp("f_vhg", (4, 128, Tb // 128, 260), BF16)
    x3c = b.dram_tmp("f_x3c", (D, NCOL))
    x4c = b.dram_tmp("f_x4c", (D, NCOL))
    x5c = b.dram_tmp("f_x5c", (D, NCOL))
    x6c = b.dram_tmp("f_x6c", (D, NCOL))
    x7c = b.dram_tmp("f_x7c", (D, TPC))
    x8o = b.dram_out("x8T_own", (D, TPC))
    b.start()
    with b.stage():
        emit_la1(b, x0T_full, hT_full)
    b.fence()
    sg = b.sem("gdma")
    n_gather = 0
    def fgath(g):
        for arm in g.switch_core_id(n=128):
            q = arm.logical % 8 % 4
            order = [q] + [r for r in range(4) if r != q]
            for s, r in enumerate(order):
                dst = hc_s[s]
                r0 = r * TPC
                g.dma_start(out=dst[:, 0:TPC],
                            in_=hT_full[:, r0:r0 + TPC]).then_inc(sg.h, 16)
                if s == 0:
                    h0 = r0 - HALO if r > 0 else 0
                    g.dma_start(out=dst[:, TPC:TPC + HALO],
                                in_=hT_full[:, h0:h0 + HALO]).then_inc(sg.h, 16)
                    for j in range(4):
                        g.dma_start(out=dst[:, TPC + HALO + j:TPC + HALO + j + 1],
                                    in_=hT_full[:, j * TPC:j * TPC + 1],
                                    allow_slow_non_contiguous=True).then_inc(sg.h, 16)
    b.sec("gpsimd", fgath)
    sg.v += 9 * 16
    b.fence()
    for s in range(4):
        with b.stage():
            emit_la2_stage(b, x0c_s[s], hc_s[s], x2c_s[s], qT_s[0] if s == 0 else None,
                           APView(khp_full, 2, s * TPC),
                           APView(vhg_full, 2, s * nkt_q),
                           cosin_s[s], sh, kv_only=(s > 0))
        b.fence()
    with b.stage():
        emit_l0_attention(b, qT_s[0], khp_full, vhg_full, wout, bout,
                          x2c_s[0], x3c, NCOL, Tb)
    b.fence()
    with b.stage():
        emit_ffn0_stage(b, x3c, x4c, sh["cpblob"], sh["w1blob"], sh["w2blob"], NCOL)
    b.fence()
    with b.stage():
        emit_l1_attn_a(b, x4c, x5c, kT, vn, NCOL, TPC)
    b.fence()
    with b.stage():
        emit_l1_ffn(b, x5c, x6c, selblob, selb, compblob, compb, NCOL)
    b.fence()
    with b.stage():
        emit_l1_attn_b(b, x6c, x7c, W7, NCOL, TPC)
    b.fence()
    with b.stage():
        emit_l1_ffn(b, x7c, x8o, selblob, selb, compblob, compb, TPC)
    def fo(g):
        for dn in ("gdma", "sdma"):
            sm = b.sem(dn)
            g.wait_ge(sm.h, sm.v)
    b.sec("gpsimd", fo)
    return b.finish()


def prep_fused_inputs(inputs, cfg, core):
    b_ = core // 4
    q = core % 4
    order = [q] + [r for r in range(4) if r != q]
    x0 = _np(inputs["L0_attn_a_x"])[:, :cfg.Tb]
    x0T = np.ascontiguousarray(x0[b_].T)
    m = {}
    m.update(prep_la1_inputs(inputs, b_))
    m["x0T_full"] = x0T
    la2b = prep_la2_inputs(inputs, cfg, 0)
    for k2 in ("w1blob", "w2blob", "cpblob", "wq_blob", "wk_blob", "wv_nat",
               "bq_row", "bk_row", "bv_row", "rope_perm"):
        m[k2] = la2b[k2]
    cos2, sin2, _ = _rope_tables(cfg)
    for s, r in enumerate(order):
        if s == 0:
            cols = _col_index(cfg, r)
        else:
            cols = np.arange(r * cfg.TPC, (r + 1) * cfg.TPC)
        m[f"x0c_s{s}"] = np.ascontiguousarray(x0T[:, cols])
        m[f"cosin_s{s}"] = np.stack([np.ascontiguousarray(cos2[:, cols]),
                                     np.ascontiguousarray(sin2[:, cols])])
    m.update(prep_la3_inputs(inputs, cfg))
    m["l1a_kT"] = np.ascontiguousarray(_np(inputs["L1_attn_a_k"])[0].T)
    m["l1a_v"] = np.ascontiguousarray(_np(inputs["L1_attn_a_v"])[0])
    m.update(prep_l1ffn_inputs(inputs, cfg))
    m.update(prep_la7_inputs(inputs, cfg, core))
    for k in _BF16_KEYS:
        m[k] = np.ascontiguousarray(m[k]).astype(ml_dtypes.bfloat16)
    return m


_BF16_KEYS = (
    "cosin_s0", "cosin_s1", "cosin_s2", "cosin_s3",
    "w1blob", "w2blob", "cpblob", "wq_blob", "wk_blob", "wv_nat",
    "bq_row", "bk_row", "bv_row", "rope_perm",
    "wout_blob", "bout_row", "l1a_kT", "l1a_v",
    "sel_blob", "selb_rows", "comp_blob", "compb_rows",
    "wmask", "l1_wq", "l1_wk", "l1_wv", "l1_bqkv", "l1_wo", "l1_bo",
    "l1_wg", "l1_bg", "l1_wd", "l1_wd2", "l1_bu", "l1_cw",
    "l1_w1", "l1_b1", "l1_w2", "l1_b2",
)


def kernel(**inputs):
    cfg = Cfg(2048)
    nc = build_fused_program(cfg)
    ins = [prep_fused_inputs(inputs, cfg, c) for c in range(8)]
    r = _run(nc, ins)
    out = np.empty((2, cfg.Tb, D), np.float32)
    for c in range(8):
        b_, qq = c // 4, c % 4
        out[b_, qq * cfg.TPC:(qq + 1) * cfg.TPC, :] = np.asarray(r.results[c]["x8T_own"]).T
    return out


def _kernel_unfused(**inputs):
    TPC = 2048
    cfg = Cfg(TPC)
    NCOL, Tb = cfg.NCOL, cfg.Tb
    x0 = _np(inputs["L0_attn_a_x"])
    cols = [_col_index(cfg, c) for c in range(8)]

    nc = build_la1_program(cfg)
    base1 = prep_la1_inputs(inputs, 0)
    x0T = [np.ascontiguousarray(x0[b_].T) for b_ in range(2)]
    ins = [dict(base1, x0T_full=x0T[c // 4]) for c in range(8)]
    r1 = _run(nc, ins)
    hT = [np.asarray(r1.results[0]["hT_full"]), np.asarray(r1.results[4]["hT_full"])]

    nc = build_la2_program(cfg)
    ins = []
    for c in range(8):
        m = prep_la2_inputs(inputs, cfg, c)
        m["x0T_cols"] = np.ascontiguousarray(x0T[c // 4][:, cols[c]])
        m["hT_cols"] = np.ascontiguousarray(hT[c // 4][:, cols[c]])
        ins.append(m)
    r2 = _run(nc, ins)
    x2c = [np.asarray(r2.results[c]["x2T_cols"]) for c in range(8)]
    qTc = [np.asarray(r2.results[c]["qT_cols"]) for c in range(8)]
    khp_full = [np.concatenate([np.asarray(r2.results[b_ * 4 + qq]["khp_own"])
                                for qq in range(4)], axis=2) for b_ in range(2)]
    vhg_full = [np.concatenate([np.asarray(r2.results[b_ * 4 + qq]["vhg_own"])
                                for qq in range(4)], axis=2) for b_ in range(2)]

    nc = build_la3_program(cfg)
    base3 = prep_la3_inputs(inputs, cfg)
    ins = []
    for c in range(8):
        m = dict(base3)
        m["qT_cols"] = qTc[c]
        m["khp_full"] = khp_full[c // 4]
        m["vhg_full"] = vhg_full[c // 4]
        m["x2T_cols"] = x2c[c]
        ins.append(m)
    r3 = _run(nc, ins)
    x3c = [np.asarray(r3.results[c]["x3T_cols"]) for c in range(8)]

    nc = build_la4_program(cfg)
    base2 = prep_la2_inputs(inputs, cfg, 0)
    wkeys = ("w1blob", "w2blob", "cpblob")
    ins = [dict({k: base2[k] for k in wkeys}, x3T_cols=x3c[c]) for c in range(8)]
    r4 = _run(nc, ins)
    x4c = [np.asarray(r4.results[c]["x4T_cols"]) for c in range(8)]

    nc = build_la5_program(cfg)
    kT = np.ascontiguousarray(_np(inputs["L1_attn_a_k"])[0].T)
    vn = np.ascontiguousarray(_np(inputs["L1_attn_a_v"])[0])
    ins = [{"l1a_kT": kT, "l1a_v": vn, "x4T_cols": x4c[c]} for c in range(8)]
    r5 = _run(nc, ins)
    x5c = [np.asarray(r5.results[c]["x5T_cols"]) for c in range(8)]

    nc = build_la6_program(cfg)
    basef = prep_l1ffn_inputs(inputs, cfg)
    ins = [dict(basef, x5T_cols=x5c[c]) for c in range(8)]
    r6 = _run(nc, ins)
    x6c = [np.asarray(r6.results[c]["x6T_cols"]) for c in range(8)]

    nc = build_la7_program(cfg)
    ins = []
    for c in range(8):
        m = prep_la7_inputs(inputs, cfg, c)
        m["x6T_cols"] = x6c[c]
        ins.append(m)
    r7 = _run(nc, ins)
    x7c = [np.asarray(r7.results[c]["x7T_own"]) for c in range(8)]

    nc = build_la8_program(cfg)
    ins = [dict(basef, x7T_own=x7c[c]) for c in range(8)]
    r8 = _run(nc, ins)

    out = np.empty((2, Tb, D), np.float32)
    for c in range(8):
        b_, q = c // 4, c % 4
        out[b_, q * TPC:(q + 1) * TPC, :] = np.asarray(r8.results[c]["x8T_own"]).T
    return out


# version 25
# speedup vs baseline: 1.0036x; 1.0036x over previous
import numpy as np
import ml_dtypes
import concourse.bass as bass
import concourse.mybir as mybir
from concourse.bass_utils import run_bass_kernel_spmd
from concourse.alu_op_type import AluOpType
from contextlib import ExitStack, contextmanager

F32 = mybir.dt.float32
BF16 = mybir.dt.bfloat16
AF = mybir.ActivationFunctionType
AX = mybir.AxisListType
Alu = AluOpType

D = 1024
HALO = 256
NSPEC = 4
EX = HALO + NSPEC
DH = 64
WINDOW = 256
SDIM = 128
KLEN = 4096
EPS = 1e-5
LN2EPS = 1e-5


def _blockify_lhs(W):
    K, M = W.shape
    assert K % 128 == 0 and M % 128 == 0
    return np.ascontiguousarray(
        W.reshape(K // 128, 128, M // 128, 128).transpose(2, 1, 0, 3)
    ).reshape(M // 128, 128, (K // 128) * 128)


def _np(x):
    return np.ascontiguousarray(np.asarray(x), dtype=np.float32)


def blk_list(ncol, NB):
    out = []
    off = 0
    while off < ncol:
        n = min(NB, ncol - off)
        rem = ncol - off - n
        if 0 < rem < 64:
            tot = n + rem
            n1 = (tot // 2 + 1) // 2 * 2
            out.append((off, n1))
            out.append((off + n1, tot - n1))
            off = ncol
            continue
        out.append((off, n))
        off += n
    return out



class Sem:
    __slots__ = ("h", "v")
    def __init__(self, h):
        self.h = h
        self.v = 0
    def inc(self, instr, n=1):
        instr.then_inc(self.h, n)
        self.v += n
        return instr
    def snap(self):
        return (self.h, self.v)


class B:
    def __init__(self, cfg):
        self.cfg = cfg
        self.nc = bass.Bass("TRN2", target_bir_lowering=False, detect_race_conditions=False)
        self.stack = ExitStack()
        self._names = {}
        self.sems = {}
        self.block = None
        self._consts = {}
        self._stage = None

    def _uname(self, base):
        i = self._names.get(base, 0)
        self._names[base] = i + 1
        return f"{base}_{i}" if i else base

    def sem(self, name):
        if name not in self.sems:
            self.sems[name] = Sem(self.stack.enter_context(
                self.nc.semaphore(self._uname("sm_" + name))))
        return self.sems[name]

    def sbuf(self, name, shape, dtype, scope=None, persist=False):
        if persist:
            return self.stack.enter_context(
                self.nc.sbuf_tensor(self._uname(name), list(shape), dtype))
        tgt = scope or self._stage or self.stack
        kw = {"side": "right"} if tgt is not self.stack else {}
        return tgt.enter_context(
            self.nc.sbuf_tensor(self._uname(name), list(shape), dtype, **kw))

    def psum(self, name, shape, dtype=F32, scope=None):
        shape = list(shape)
        elt = 4 if dtype == F32 else 2
        per_bank = 2048 // elt
        if shape[-1] % per_bank:
            shape[-1] = ((shape[-1] // per_bank) + 1) * per_bank
        t = (scope or self._stage or self.stack).enter_context(
            self.nc.psum_tensor(self._uname(name), shape, dtype))
        return t

    @contextmanager
    def stage(self):
        prev = self._stage
        es = ExitStack()
        self._stage = es
        try:
            yield es
        finally:
            self._stage = prev
            es.close()

    def dram_in(self, name, shape, dtype=F32):
        return self.nc.dram_tensor(name, list(shape), dtype, kind="ExternalInput").ap()

    def dram_out(self, name, shape, dtype=F32):
        return self.nc.dram_tensor(name, list(shape), dtype, kind="ExternalOutput").ap()

    def dram_tmp(self, name, shape, dtype=F32):
        return self.nc.dram_tensor(self._uname(name), list(shape), dtype).ap()

    def start(self):
        self.block = self.stack.enter_context(self.nc.Block())
        return self

    def sec(self, eng, fn):
        getattr(self.block, eng)(fn)

    def _fence_dummy(self):
        if "fd" not in self._consts:
            t = self.sbuf("fence_dummy", (128, 8), F32, persist=True)
            sm = self.sem("v")
            def f(v):
                sm.inc(v.memset(t[:, :], 0.0), 1)
                v.wait_ge(sm.h, sm.v)
            self.sec("vector", f)
            self._consts["fd"] = t
            self._consts["fd_init"] = sm.snap()
        return self._consts["fd"]

    def fence(self):
        fd = self._fence_dummy()
        init = self._consts["fd_init"]
        fs = {e: self.sem("fence_" + e) for e in ("vector", "scalar", "gpsimd")}
        def fv(v):
            fs["vector"].inc(v.tensor_copy(fd[:, 0:1], fd[:, 1:2]), 1)
        self.sec("vector", fv)
        def fsc(sc):
            sc.wait_ge(init[0], init[1])
            fs["scalar"].inc(sc.copy(fd[:, 2:3], fd[:, 3:4]), 1)
        self.sec("scalar", fsc)
        def fg(g):
            g.wait_ge(init[0], init[1])
            fs["gpsimd"].inc(g.tensor_copy(fd[:, 4:5], fd[:, 5:6]), 1)
        self.sec("gpsimd", fg)
        targets = [s.snap() for s in fs.values()]
        for dn in ("sdma", "gdma", "t"):
            if dn in self.sems:
                targets.append(self.sems[dn].snap())
        for e in ("tensor", "vector", "scalar", "gpsimd", "sync"):
            def mk(targets):
                def f(eng):
                    for h, v in targets:
                        if v:
                            eng.wait_ge(h, v)
                return f
            self.sec(e, mk(targets))

    def finish(self):
        self.stack.close()
        return self.nc

    def const_col(self, value):
        key = ("col", float(value))
        if key not in self._consts:
            t = self.sbuf(f"c_col", (128, 1), F32, persist=True)
            sm = self.sem("v")
            def f(v):
                sm.inc(v.memset(t[:, :], float(value)), 1)
                v.wait_ge(sm.h, sm.v)
            self.sec("vector", f)
            self._consts[key] = t
        return self._consts[key]

    def const_ones(self, p, n, dtype=BF16):
        key = ("ones", p, n, dtype)
        if key not in self._consts:
            t = self.sbuf(f"c_ones_{p}x{n}", (p, n), dtype, persist=True)
            sm = self.sem("v")
            def f(v):
                sm.inc(v.memset(t[:, :], 1.0), 1)
                v.wait_ge(sm.h, sm.v)
            self.sec("vector", f)
            self._consts[key] = t
        return self._consts[key]



class LnScratch:
    def __init__(self, b, NB):
        self.xb = b.sbuf("ln_xb", (128, 8 * NB), BF16)
        self.mu = b.sbuf("ln_mu", (128, NB), F32)
        self.rstd = b.sbuf("ln_rstd", (128, NB), F32)
        self.ps_s = b.psum("ln_ps_s", (128, NB))
        self.ps_q = b.psum("ln_ps_q", (128, NB))
        self.guard = None


def emit_ln(b, xT_ap_fn, ncol, xnT, sc, gain_col=None, bias_col=None):
    NB = b.cfg.NB
    ones = b.const_ones(128, 128)
    sv, ss, st = b.sem("v"), b.sem("s"), b.sem("t")
    nblk = (ncol + NB - 1) // NB
    for ib in range(nblk):
        off = ib * NB
        n = min(NB, ncol - off)
        guard = sc.guard
        def f(s, off=off, n=n):
            for cc in range(8):
                i = s.copy(sc.xb[:, cc * NB:cc * NB + n], xT_ap_fn(cc, off, n))
            ss.inc(i, 1)
        b.sec("scalar", f)
        def f(t, n=n, g=guard, ssv=ss.v):
            t.wait_ge(ss.h, ssv)
            if g:
                t.wait_ge(g[0], g[1])
            for cc in range(8):
                i = t.matmul(sc.ps_s[:, 0:n], ones[:, :], sc.xb[:, cc * NB:cc * NB + n],
                             start=(cc == 0), stop=(cc == 7))
            st.inc(i, 1)
        b.sec("tensor", f)
        def f(s, off=off, n=n, stv=st.v):
            s.wait_ge(st.h, stv)
            for cc in range(8):
                i = s.activation(sc.xb[:, cc * NB:cc * NB + n], xT_ap_fn(cc, off, n),
                                 AF.Square)
            ss.inc(i, 1)
        b.sec("scalar", f)
        def f(t, n=n, ssv=ss.v):
            t.wait_ge(ss.h, ssv)
            for cc in range(8):
                i = t.matmul(sc.ps_q[:, 0:n], ones[:, :], sc.xb[:, cc * NB:cc * NB + n],
                             start=(cc == 0), stop=(cc == 7))
            st.inc(i, 1)
        b.sec("tensor", f)
        def f(v, n=n, stv=st.v):
            v.wait_ge(st.h, stv)
            v.tensor_scalar(sc.mu[:, 0:n], sc.ps_s[:, 0:n], 1.0 / D, None, Alu.mult)
            v.tensor_tensor(sc.rstd[:, 0:n], sc.mu[:, 0:n], sc.mu[:, 0:n], Alu.mult)
            sv.inc(v.scalar_tensor_tensor(sc.rstd[:, 0:n], sc.ps_q[:, 0:n], 1.0 / D,
                                          sc.rstd[:, 0:n], Alu.mult, Alu.subtract), 1)
        b.sec("vector", f)
        def f(s, n=n, svv=sv.v):
            s.wait_ge(sv.h, svv)
            s.activation(sc.rstd[:, 0:n], sc.rstd[:, 0:n], AF.Ln, bias=b.const_col(LN2EPS)[:, :])
            ss.inc(s.activation(sc.rstd[:, 0:n], sc.rstd[:, 0:n], AF.Exp, scale=-0.5), 1)
        b.sec("scalar", f)
        def f(v, off=off, n=n, ssv=ss.v):
            v.wait_ge(ss.h, ssv)
            for cc in range(8):
                xo = xnT[:, cc * ncol + off: cc * ncol + off + n]
                v.tensor_tensor(sc.xb[:, cc * NB:cc * NB + n], xT_ap_fn(cc, off, n),
                                sc.mu[:, 0:n], Alu.subtract)
                if gain_col is None:
                    i = v.tensor_tensor(xo, sc.xb[:, cc * NB:cc * NB + n],
                                        sc.rstd[:, 0:n], Alu.mult)
                else:
                    v.tensor_tensor(sc.xb[:, cc * NB:cc * NB + n],
                                    sc.xb[:, cc * NB:cc * NB + n],
                                    sc.rstd[:, 0:n], Alu.mult)
                    i = v.tensor_scalar(xo, sc.xb[:, cc * NB:cc * NB + n],
                                        gain_col[:, cc:cc + 1], bias_col[:, cc:cc + 1],
                                        Alu.mult, Alu.add)
            sv.inc(i, 1)
        b.sec("vector", f)
        sc.guard = sv.snap()



def emit_proj(b, wblob, nK, nM, rhs_fn, ncol, out_cb, bias_row=None, wname="w"):
    NB = b.cfg.NB
    nkc = nK // 128
    nmt = nM // 128
    wsb = b.sbuf(f"w_{wname}", (128, 2 * nkc * 128), BF16)
    sg, st = b.sem("gdma"), b.sem("t")
    brow = None
    if bias_row is not None:
        brow = b.sbuf(f"wb_{wname}", (1, nM), BF16)
        def f(g):
            sg.inc(g.dma_start(out=brow[:, :], in_=bias_row), 16)
        b.sec("gpsimd", f)
    ones1 = b.const_ones(1, NB)
    nblk = (ncol + NB - 1) // NB
    wguard = [None, None]
    psguard = [None, None]
    pst = [b.psum(f"ps_{wname}", (128, NB)) for _ in range(2)]
    for mi in range(nmt):
        slot = mi % 2
        def f(g, mi=mi, slot=slot, wg=wguard[slot]):
            if wg:
                g.wait_ge(wg[0], wg[1])
            sg.inc(g.dma_start(out=wsb[:, slot * nkc * 128:(slot + 1) * nkc * 128],
                               in_=wblob[mi:mi + 1, :]), 16)
        b.sec("gpsimd", f)
        for ib in range(nblk):
            off = ib * NB
            n = min(NB, ncol - off)
            pi = (mi * nblk + ib) % 2
            ps = pst[pi]
            def f(t, mi=mi, slot=slot, off=off, n=n, ps=ps,
                  pg=psguard[pi], sgv=sg.v):
                t.wait_ge(sg.h, sgv)
                if pg:
                    t.wait_ge(pg[0], pg[1])
                for kc in range(nkc):
                    i = t.matmul(ps[:, 0:n],
                                 wsb[:, slot * nkc * 128 + kc * 128:
                                     slot * nkc * 128 + (kc + 1) * 128],
                                 rhs_fn(kc, off, n),
                                 start=(kc == 0),
                                 stop=(kc == nkc - 1 and brow is None))
                if brow is not None:
                    i = t.matmul(ps[:, 0:n], brow[:, mi * 128:(mi + 1) * 128],
                                 ones1[:, 0:n], start=False, stop=True)
                st.inc(i, 1)
            b.sec("tensor", f)
            psguard[pi] = out_cb(mi, off, n, ps)
        wguard[slot] = (st.h, st.v)



def emit_la1(b, x0T, hT_out, dbg_stop=None):
    cfg = b.cfg
    Tb = cfg.Tb
    NB2 = 512
    nblk = Tb // NB2
    sv, ss, st = b.sem("v"), b.sem("s"), b.sem("t")
    sd, sg = b.sem("sdma"), b.sem("gdma")
    ones128 = b.const_ones(128, 128)
    ones128f = b.const_ones(128, 128, F32)

    igw = b.dram_in("ig_w", (8, 128, 128))
    agw = b.dram_in("ag_w", (8, 128, 128))
    igb = b.dram_in("ig_bias_row", (1, D))
    agb = b.dram_in("ag_bias_row", (1, D))
    apar = b.dram_in("a_param_col", (128, 8))

    onesf = b.sbuf("onesf", (128, Tb), BF16)
    m8sp = b.sbuf("m8sp", (128, 8), F32)
    gwsb = b.sbuf("gwsb", (128, 16 * 128), BF16)
    gbsb = b.sbuf("gbsb", (1, 2 * D), BF16)
    mean_row = b.sbuf("mean_row", (1, Tb), BF16)
    rstd_row = b.sbuf("rstd_row", (1, Tb), BF16)
    rowtmp = b.sbuf("rowtmp", (1, Tb), BF16)
    sp_t = b.sbuf("sp_t", (128, 8), F32)
    sp_r = b.sbuf("sp_r", (128, 8), F32)
    scope1 = ExitStack()
    xblk = b.sbuf("xblk", (128, 8 * 512), F32, scope=scope1)
    xbb = b.sbuf("xbb", (128, 8 * 512), BF16, scope=scope1)
    ps_s = b.psum("ps_stat", (128, 512), scope=scope1)
    ps_q = b.psum("ps_stat2", (128, 512), scope=scope1)
    ev_s = b.sbuf("ev_stat", (128, 512), F32, scope=scope1)
    ev_q = b.sbuf("ev_stat2", (128, 512), F32, scope=scope1)

    def f(g):
        for cc in range(8):
            sg.inc(g.dma_start(out=gwsb[:, cc * 128:(cc + 1) * 128], in_=igw[cc]), 16)
            sg.inc(g.dma_start(out=gwsb[:, (8 + cc) * 128:(9 + cc) * 128], in_=agw[cc]), 16)
        sg.inc(g.dma_start(out=gbsb[:, 0:D], in_=igb), 16)
        sg.inc(g.dma_start(out=gbsb[:, D:2 * D], in_=agb), 16)
    b.sec("gpsimd", f)
    def f(sy):
        sd.inc(sy.dma_start(out=m8sp[:, :], in_=apar), 16)
    b.sec("sync", f)
    def f(v):
        v.wait_ge(sd.h, sd.v)
        v.memset(onesf[:, :], 1.0)
        sv.inc(v.tensor_scalar(sp_r[:, :], m8sp[:, :], 0.0, None, Alu.max), 1)
    b.sec("vector", f)
    def f(s, svv=sv.v):
        s.wait_ge(sv.h, svv)
        s.activation(sp_t[:, :], m8sp[:, :], AF.Abs)
        s.activation(sp_t[:, :], sp_t[:, :], AF.Exp, scale=-1.0)
        ss.inc(s.activation(sp_t[:, :], sp_t[:, :], AF.Ln, bias=b.const_col(1.0)[:, :]), 1)
    b.sec("scalar", f)
    def f(v, ssv=ss.v):
        v.wait_ge(ss.h, ssv)
        v.tensor_tensor(sp_t[:, :], sp_t[:, :], sp_r[:, :], Alu.add)
        sv.inc(v.tensor_scalar(m8sp[:, :], sp_t[:, :], -8.0, None, Alu.mult), 1)
    b.sec("vector", f)
    b.fence()
    if dbg_stop == 0:
        def f(g):
            sg.inc(g.dma_start(out=hT_out[0:128, :], in_=onesf[:, :]), 16)
            g.wait_ge(sg.h, sg.v)
        b.sec("gpsimd", f)
        return

    _n_iters = 1 if dbg_stop == 13 else Tb // 512
    SD1, SS1, ST1, SV1 = sd.v, ss.v, st.v, sv.v
    def f(sy, SS1=SS1, ssv=ss.v):
        sy.wait_ge(ss.h, ssv)
        for ib in range(_n_iters):
            off = ib * 512
            if ib >= 1:
                sy.wait_ge(ss.h, SS1 + 2 * ib)
            for cc in range(8):
                sd.inc(sy.dma_start(out=xblk[:, cc * 512:(cc + 1) * 512],
                                    in_=x0T[cc * 128:(cc + 1) * 128, off:off + 512]), 16)
    b.sec("sync", f)
    def f(s, SD1=SD1, ST1=ST1, stv=st.v):
        s.wait_ge(st.h, stv)
        for ib in range(_n_iters):
            s.wait_ge(sd.h, SD1 + 128 * (ib + 1))
            if ib >= 1:
                s.wait_ge(st.h, ST1 + 2 * ib)
            s.copy(xbb[:, :], xblk[:, :])
            ss.inc(s.activation(xblk[:, :], xblk[:, :], AF.Square), 1)
            s.wait_ge(st.h, ST1 + 2 * ib + 1)
            ss.inc(s.copy(xbb[:, :], xblk[:, :]), 1)
    b.sec("scalar", f)
    def f(t, SS1=SS1, SV1=SV1, svv=sv.v):
        t.wait_ge(sv.h, svv)
        for ib in range(_n_iters):
            t.wait_ge(ss.h, SS1 + 2 * ib + 1)
            if ib >= 1:
                t.wait_ge(sv.h, SV1 + ib)
            for cc in range(8):
                i = t.matmul(ps_s[:, :], ones128[:, :], xbb[:, cc * 512:(cc + 1) * 512],
                             start=(cc == 0), stop=(cc == 7))
            st.inc(i, 1)
            t.wait_ge(ss.h, SS1 + 2 * ib + 2)
            for cc in range(8):
                i = t.matmul(ps_q[:, :], ones128[:, :], xbb[:, cc * 512:(cc + 1) * 512],
                             start=(cc == 0), stop=(cc == 7))
            st.inc(i, 1)
    b.sec("tensor", f)
    def f(v, ST1=ST1):
        for ib in range(_n_iters):
            off = ib * 512
            v.wait_ge(st.h, ST1 + 2 * ib + 2)
            v.tensor_copy(ev_s[:, :], ps_s[:, :])
            v.tensor_copy(ev_q[:, :], ps_q[:, :])
            v.tensor_copy(mean_row[0:1, off:off + 512], ev_s[0:1, :])
            sv.inc(v.tensor_copy(rstd_row[0:1, off:off + 512], ev_q[0:1, :]), 1)
    b.sec("vector", f)
    b.fence()
    def f(v):
        with b.nc.allow_low_precision(reason="bf16 LN stat rows"):
            sv.inc(v.tensor_scalar(mean_row[0:1, :], mean_row[0:1, :], 1.0 / D, None,
                                   Alu.mult), 1)
    b.sec("vector", f)
    def f(s, svv=sv.v):
        s.wait_ge(sv.h, svv)
        ss.inc(s.activation(rowtmp[0:1, :], mean_row[0:1, :], AF.Square), 1)
    b.sec("scalar", f)
    def f(v, ssv=ss.v):
        v.wait_ge(ss.h, ssv)
        with b.nc.allow_low_precision(reason="bf16 LN stat rows"):
            sv.inc(v.scalar_tensor_tensor(rstd_row[0:1, :], rstd_row[0:1, :], 1.0 / D,
                                          rowtmp[0:1, :], Alu.mult, Alu.subtract), 1)
    b.sec("vector", f)
    def f(s, svv=sv.v):
        s.wait_ge(sv.h, svv)
        s.activation(rstd_row[0:1, :], rstd_row[0:1, :], AF.Ln, bias=b.const_col(EPS)[0:1, :])
        ss.inc(s.activation(rstd_row[0:1, :], rstd_row[0:1, :], AF.Exp, scale=-0.5), 1)
    b.sec("scalar", f)
    b.fence()
    if dbg_stop == 1 or (dbg_stop or 0) >= 10:
        def f(g):
            g.wait_ge(sg.h, sg.v)
        b.sec("gpsimd", f)
        scope1.close()
        return

    scope1.close()
    bA = b.sbuf("bufA", (128, Tb), F32)
    bX = b.sbuf("bufX", (128, Tb), F32)
    bP = b.sbuf("bufP", (128, Tb), F32)
    xnb = [b.sbuf("p2_xnb", (128, NB2), BF16) for _ in range(2)]
    gx = [b.sbuf("p2_gx", (128, NB2), F32) for _ in range(2)]
    tt1 = [b.sbuf("p2_t1", (128, NB2), F32) for _ in range(2)]
    tt2 = [b.sbuf("p2_t2", (128, NB2), F32) for _ in range(2)]
    xin = [b.sbuf("p2_x", (128, NB2), F32) for _ in range(2)]
    ps_r = [b.psum("ps_rep", (128, NB2)) for _ in range(2)]
    ps_r2 = [b.psum("ps_rep2", (128, NB2)) for _ in range(2)]
    ps_g = [b.psum("ps_g", (128, NB2)) for _ in range(2)]
    ps_g2 = [b.psum("ps_g2", (128, NB2)) for _ in range(2)]
    onesb = b.const_ones(1, NB2)
    g_f1 = [None, None]
    g_sig = [None, None]
    g_fin = [None, None]
    for cc in range(8):
        for pr in range(nblk // 2):
            offs = [(2 * pr) * NB2, (2 * pr + 1) * NB2]
            def f(sy, cc=cc, offs=offs, g0=g_f1[0], g1=g_f1[1]):
                for p, (off, g) in enumerate(zip(offs, (g0, g1))):
                    if g:
                        sy.wait_ge(g[0], g[1])
                    sd.inc(sy.dma_start(out=xin[p][:, :],
                                        in_=x0T[cc * 128:(cc + 1) * 128,
                                                off:off + NB2]), 16)
            b.sec("sync", f)
            def f(t, offs=offs, g0=g_f1[0], g1=g_f1[1]):
                for p, (off, g) in enumerate(zip(offs, (g0, g1))):
                    if g:
                        t.wait_ge(g[0], g[1])
                    t.matmul(ps_r[p][:, :], ones128[0:1, :],
                             mean_row[0:1, off:off + NB2], start=True, stop=True)
                    i = t.matmul(ps_r2[p][:, :], ones128[0:1, :],
                                 rstd_row[0:1, off:off + NB2], start=True, stop=True)
                    st.inc(i, 1)
            b.sec("tensor", f)
            def f(v, sdv=sd.v, stv=st.v):
                v.wait_ge(sd.h, sdv)
                v.wait_ge(st.h, stv)
                for p in range(2):
                    v.tensor_tensor(tt1[p][:, :], xin[p][:, :], ps_r[p][:, :],
                                    Alu.subtract)
                    i = v.tensor_tensor(xnb[p][:, :], tt1[p][:, :], ps_r2[p][:, :],
                                        Alu.mult)
                    sv.inc(i, 1)
            b.sec("vector", f)
            g_f1[0] = (sv.h, sv.v - 1)
            g_f1[1] = sv.snap()
            def f(t, cc=cc, svv=sv.v, g0=g_sig[0], g1=g_sig[1]):
                t.wait_ge(sv.h, svv)
                for p, g in enumerate((g0, g1)):
                    if g:
                        t.wait_ge(g[0], g[1])
                    t.matmul(ps_g[p][:, :], gwsb[:, cc * 128:(cc + 1) * 128],
                             xnb[p][:, :], start=True, stop=False)
                    t.matmul(ps_g[p][:, :], gbsb[:, cc * 128:(cc + 1) * 128],
                             onesb[:, 0:NB2], start=False, stop=True)
                    t.matmul(ps_g2[p][:, :], gwsb[:, (8 + cc) * 128:(9 + cc) * 128],
                             xnb[p][:, :], start=True, stop=False)
                    i = t.matmul(ps_g2[p][:, :], gbsb[:, D + cc * 128:D + (cc + 1) * 128],
                                 onesb[:, 0:NB2], start=False, stop=True)
                    st.inc(i, 1)
            b.sec("tensor", f)
            st_g = st.v
            ss_b = ss.v
            sv_b = sv.v
            def fsc(s, stg=st_g, g0=g_fin[0], g1=g_fin[1], offs=offs, svb=sv_b):
                for p, g in enumerate((g0, g1)):
                    s.wait_ge(st.h, stg - 1 + p)
                    if g:
                        s.wait_ge(g[0], g[1])
                    s.activation(gx[p][:, :], ps_g[p][:, :], AF.Sigmoid)
                    ss.inc(s.activation(tt1[p][:, :], ps_g2[p][:, :], AF.Sigmoid), 1)
                for p in range(2):
                    s.wait_ge(sv.h, svb + 1 + p)
                    ss.inc(s.activation(tt2[p][:, :], bA[:, offs[p]:offs[p] + NB2],
                                        AF.Exp, scale=2.0), 1)
                for p in range(2):
                    s.wait_ge(sv.h, svb + 3 + p)
                    s.activation(tt2[p][:, :], tt2[p][:, :], AF.Ln)
                    ss.inc(s.activation(tt2[p][:, :], tt2[p][:, :], AF.Exp, scale=0.5), 1)
            b.sec("scalar", fsc)
            g_sig[0] = (ss.h, ss_b + 1)
            g_sig[1] = (ss.h, ss_b + 2)
            def fv1(v, cc=cc, offs=offs, ssb=ss_b):
                for p in range(2):
                    v.wait_ge(ss.h, ssb + 1 + p)
                    sv.inc(v.tensor_scalar(bA[:, offs[p]:offs[p] + NB2], tt1[p][:, :],
                                           m8sp[:, cc:cc + 1], None, Alu.mult), 1)
            b.sec("vector", fv1)
            def fv2(v, ssb=ss_b):
                for p in range(2):
                    v.wait_ge(ss.h, ssb + 3 + p)
                    v.tensor_scalar(tt2[p][:, :], tt2[p][:, :], -1.0, 1.0,
                                    Alu.mult, Alu.add)
                    sv.inc(v.tensor_scalar(tt2[p][:, :], tt2[p][:, :], 1e-7, None,
                                           Alu.max), 1)
            b.sec("vector", fv2)
            def fv3(v, offs=offs, ssb=ss_b):
                for p in range(2):
                    v.wait_ge(ss.h, ssb + 5 + p)
                    v.tensor_tensor(tt1[p][:, :], xnb[p][:, :], gx[p][:, :], Alu.mult)
                    sv.inc(v.tensor_tensor(bX[:, offs[p]:offs[p] + NB2], tt1[p][:, :],
                                           tt2[p][:, :], Alu.mult), 1)
            b.sec("vector", fv3)
            g_fin[0] = (sv.h, sv.v - 1)
            g_fin[1] = sv.snap()
        if dbg_stop == 2:
            def f(v, cc=cc, svv=sv.v):
                v.wait_ge(sv.h, svv)
                sv.inc(v.tensor_copy(bP[:, :], bX[:, :]), 1)
            b.sec("vector", f)
            def f(g, cc=cc, svv=sv.v):
                g.wait_ge(sv.h, svv)
                sg.inc(g.dma_start(out=hT_out[cc * 128:(cc + 1) * 128, :], in_=bP[:, :]), 16)
            b.sec("gpsimd", f)
            continue
        def f(v, cc=cc, svv=sv.v):
            v.wait_ge(sv.h, svv)
            v.tensor_tensor_scan(bP[:, :], onesf[:, :], bA[:, :], 0.0,
                                 Alu.mult, Alu.add)
            sv.inc(v.tensor_scalar(bA[:, 1:Tb], bP[:, 0:Tb - 1], -16.11809565095832,
                                   None, Alu.max), 1)
            sv.inc(v.memset(bA[:, 0:1], 0.0), 1)
        b.sec("vector", f)
        def f(s, svv=sv.v):
            s.wait_ge(sv.h, svv)
            ss.inc(s.activation(bP[:, :], bA[:, :], AF.Exp, scale=-1.0), 1)
        b.sec("scalar", f)
        def f(v, ssv=ss.v):
            v.wait_ge(ss.h, ssv)
            v.tensor_tensor(bP[:, :], bX[:, :], bP[:, :], Alu.mult)
            sv.inc(v.tensor_tensor_scan(bX[:, :], onesf[:, :], bP[:, :], 0.0,
                                        Alu.mult, Alu.add), 1)
        b.sec("vector", f)
        def f(s, svv=sv.v):
            s.wait_ge(sv.h, svv)
            ss.inc(s.activation(bA[:, :], bA[:, :], AF.Exp), 1)
        b.sec("scalar", f)
        def f(v, ssv=ss.v):
            v.wait_ge(ss.h, ssv)
            sv.inc(v.tensor_tensor(bP[:, :], bA[:, :], bX[:, :], Alu.mult), 1)
        b.sec("vector", f)
        def f(g, cc=cc, svv=sv.v):
            g.wait_ge(sv.h, svv)
            sg.inc(g.dma_start(out=hT_out[cc * 128:(cc + 1) * 128, :], in_=bP[:, :]), 16)
        b.sec("gpsimd", f)
    def f(g):
        g.wait_ge(sg.h, sg.v)
    b.sec("gpsimd", f)
    b.fence()





def emit_resadd_hbm(b, dst, srcA, srcB, ncol):
    scope = ExitStack()
    sv, sd, sg = b.sem("v"), b.sem("sdma"), b.sem("gdma")
    ta = [b.sbuf("ra_a", (128, ncol), F32, scope=scope) for _ in range(2)]
    tb_ = [b.sbuf("ra_b", (128, ncol), F32, scope=scope) for _ in range(2)]
    SD0, SV0, SG0 = sd.v, sv.v, sg.v
    def fl(sy, svv=sv.v, sgv=sg.v, SG0=SG0):
        sy.wait_ge(sv.h, svv)
        sy.wait_ge(sg.h, sgv)
        for cc in range(8):
            p = cc % 2
            if cc >= 2:
                sy.wait_ge(sg.h, SG0 + 16 * (cc - 1))
            sd.inc(sy.dma_start(out=ta[p][:, :], in_=srcA[cc * 128:(cc + 1) * 128, :]), 16)
            sd.inc(sy.dma_start(out=tb_[p][:, :], in_=srcB[cc * 128:(cc + 1) * 128, :]), 16)
    b.sec("sync", fl)
    def fv(v, SD0=SD0):
        for cc in range(8):
            p = cc % 2
            v.wait_ge(sd.h, SD0 + 32 * (cc + 1))
            sv.inc(v.tensor_tensor(ta[p][:, :], ta[p][:, :], tb_[p][:, :], Alu.add), 1)
    b.sec("vector", fv)
    def fg(g, SV0=SV0):
        for cc in range(8):
            p = cc % 2
            g.wait_ge(sv.h, SV0 + cc + 1)
            sg.inc(g.dma_start(out=dst[cc * 128:(cc + 1) * 128, :], in_=ta[p][:, :]), 16)
    b.sec("gpsimd", fg)
    b.fence()
    scope.close()


class LnStream:
    def __init__(self, b, NB, nbuf=2):
        self.b = b
        self.NB = NB
        self.nbuf = nbuf
        self.x_nbs = [b.sbuf("lns_x", (128, 8 * NB), F32) for _ in range(nbuf)]
        self.xbs = [b.sbuf("lns_xb", (128, 8 * NB), BF16) for _ in range(nbuf)]
        self.xns = [b.sbuf("lns_xn", (128, 8 * NB), BF16) for _ in range(nbuf)]
        self.mus = [b.sbuf("lns_mu", (128, NB), F32) for _ in range(nbuf)]
        self.rstds = [b.sbuf("lns_rstd", (128, NB), F32) for _ in range(nbuf)]
        self.ps_sq = [b.psum("lns_ps_sq", (128, 2 * NB)) for _ in range(nbuf)]
        self.x_nb, self.xb, self.xn = self.x_nbs[0], self.xbs[0], self.xns[0]
        self.guard = None
        self.guards = [None] * nbuf
        self.stats_done = [None] * nbuf
        self.ps_free = [None] * nbuf
        self.xnb_free = [None] * nbuf
        self.blkcnt = 0
        self.last_p = None

    def block(self, xcur, ncol, off, n, gain_col=None, bias_col=None, xn_dst=None):
        b, NB = self.b, self.NB
        ones = b.const_ones(128, 128)
        sv, ss, st, sd = b.sem("v"), b.sem("s"), b.sem("t"), b.sem("sdma")
        p = self.blkcnt % self.nbuf
        self.blkcnt += 1
        if self.last_p is not None:
            self.guards[self.last_p] = self.guard
        self.last_p = p
        self.x_nb, self.xb, self.xn = self.x_nbs[p], self.xbs[p], self.xns[p]
        mu, rstd = self.mus[p], self.rstds[p]
        self.mu, self.rstd = mu, rstd
        ps_s, ps_q = self.ps_sq[p][:, 0:NB], self.ps_sq[p][:, NB:2 * NB]
        g0 = self.guards[p]
        gnb = self.xnb_free[p]
        def fl(sy, off=off, n=n, g=g0, gnb=gnb):
            if g:
                sy.wait_ge(g[0], g[1])
            if gnb:
                sy.wait_ge(gnb[0], gnb[1])
            for cc in range(8):
                sd.inc(sy.dma_start(out=self.x_nb[:, cc * NB:cc * NB + n],
                                    in_=xcur[cc * 128:(cc + 1) * 128, off:off + n]), 16)
        b.sec("sync", fl)
        def fs(s, n=n, sdv=sd.v, xb=self.xb, x_nb=self.x_nb, gst=self.stats_done[p],
               gnb=gnb, g0=g0):
            s.wait_ge(sd.h, sdv)
            if gst:
                s.wait_ge(gst[0], gst[1])
            if gnb:
                s.wait_ge(gnb[0], gnb[1])
            if g0:
                s.wait_ge(g0[0], g0[1])
            for cc in range(8):
                i = s.copy(xb[:, cc * NB:cc * NB + n], x_nb[:, cc * NB:cc * NB + n])
            ss.inc(i, 1)
        b.sec("scalar", fs)
        def ft(t, n=n, ssv=ss.v, xb=self.xb, ps_s=ps_s, gpf=self.ps_free[p]):
            t.wait_ge(ss.h, ssv)
            if gpf:
                t.wait_ge(gpf[0], gpf[1])
            for cc in range(8):
                i = t.matmul(ps_s[:, 0:n], ones[:, :], xb[:, cc * NB:cc * NB + n],
                             start=(cc == 0), stop=(cc == 7))
            st.inc(i, 1)
        b.sec("tensor", ft)
        def fs(s, n=n, stv=st.v, xb=self.xb, x_nb=self.x_nb):
            s.wait_ge(st.h, stv)
            for cc in range(8):
                i = s.activation(xb[:, cc * NB:cc * NB + n],
                                 x_nb[:, cc * NB:cc * NB + n], AF.Square)
            ss.inc(i, 1)
        b.sec("scalar", fs)
        def ft(t, n=n, ssv=ss.v, xb=self.xb, ps_q=ps_q):
            t.wait_ge(ss.h, ssv)
            for cc in range(8):
                i = t.matmul(ps_q[:, 0:n], ones[:, :], xb[:, cc * NB:cc * NB + n],
                             start=(cc == 0), stop=(cc == 7))
            st.inc(i, 1)
        b.sec("tensor", ft)
        self.stats_done[p] = (st.h, st.v)
        def fv(v, n=n, stv=st.v, mu=mu, rstd=rstd, ps_s=ps_s, ps_q=ps_q):
            v.wait_ge(st.h, stv)
            v.tensor_scalar(mu[:, 0:n], ps_s[:, 0:n], 1.0 / D, None, Alu.mult)
            v.tensor_tensor(rstd[:, 0:n], mu[:, 0:n], mu[:, 0:n], Alu.mult)
            sv.inc(v.scalar_tensor_tensor(rstd[:, 0:n], ps_q[:, 0:n], 1.0 / D,
                                          rstd[:, 0:n], Alu.mult, Alu.subtract), 1)
        b.sec("vector", fv)
        self.ps_free[p] = (sv.h, sv.v)
        def fs(s, n=n, svv=sv.v, rstd=rstd):
            s.wait_ge(sv.h, svv)
            s.activation(rstd[:, 0:n], rstd[:, 0:n], AF.Ln,
                         bias=b.const_col(EPS)[:, :])
            ss.inc(s.activation(rstd[:, 0:n], rstd[:, 0:n], AF.Exp, scale=-0.5), 1)
        b.sec("scalar", fs)
        def fv(v, off=off, n=n, ssv=ss.v, gain_col=gain_col, bias_col=bias_col,
               xn_dst=xn_dst, xb=self.xb, x_nb=self.x_nb, xn=self.xn,
               mu=mu, rstd=rstd):
            v.wait_ge(ss.h, ssv)
            for cc in range(8):
                if xn_dst is None:
                    xo = xn[:, cc * NB:cc * NB + n]
                else:
                    xo = xn_dst[0][:, cc * xn_dst[1] + off:cc * xn_dst[1] + off + n]
                v.tensor_tensor(xb[:, cc * NB:cc * NB + n],
                                x_nb[:, cc * NB:cc * NB + n],
                                mu[:, 0:n], Alu.subtract)
                if gain_col is None:
                    i = v.tensor_tensor(xo, xb[:, cc * NB:cc * NB + n],
                                        rstd[:, 0:n], Alu.mult)
                else:
                    v.tensor_tensor(xb[:, cc * NB:cc * NB + n],
                                    xb[:, cc * NB:cc * NB + n],
                                    rstd[:, 0:n], Alu.mult)
                    i = v.tensor_scalar(xo, xb[:, cc * NB:cc * NB + n],
                                        gain_col[:, cc:cc + 1], bias_col[:, cc:cc + 1],
                                        Alu.mult, Alu.add)
            sv.inc(i, 1)
        b.sec("vector", fv)
        self.xnb_free[p] = (sv.h, sv.v)


def emit_ffn0_stage(b, xsrc, xdst, cpblob, w1blob, w2blob, ncol):
    NB = b.cfg.NB
    sv, ss, st, sg, sd = b.sem("v"), b.sem("s"), b.sem("t"), b.sem("gdma"), b.sem("sdma")
    scope = ExitStack()
    xn_hbm = b.dram_tmp("f0_xnh", (D, ncol), BF16)
    blks = blk_list(ncol, NB)
    lnscope = ExitStack()
    prev = b._stage
    b._stage = lnscope
    lns = LnStream(b, NB)
    b._stage = prev
    for (off, n) in blks:
        lns.block(xsrc, ncol, off, n)
        def fxo(g, off=off, n=n, svv=sv.v):
            g.wait_ge(sv.h, svv)
            for cc in range(8):
                sg.inc(g.dma_start(out=xn_hbm[cc * 128:(cc + 1) * 128, off:off + n],
                                   in_=lns.xn[:, cc * NB:cc * NB + n]), 16)
        b.sec("gpsimd", fxo)
        lns.guard = sg.snap()
    b.fence()
    lnscope.close()
    w = {
        "w1sb": b.sbuf("f0_w1", (128, 32 * 128), BF16, scope=scope),
        "w2sb": b.sbuf("f0_w2", (128, 128 * 128), BF16, scope=scope),
        "cpsb": [b.sbuf("f0_cp", (128, 128 * 128), BF16, scope=scope) for _ in range(2)],
        "ysb": b.sbuf("f0_y", (128, 32 * NB), BF16, scope=scope),
        "hsb": b.sbuf("f0_h", (128, 128 * NB), BF16, scope=scope),
        "x_t": b.sbuf("f0_x", (128, 8 * NB), F32, scope=scope),
        "xnb": b.sbuf("f0_xnb", (128, 8 * NB), BF16, scope=scope),
        "ps_y": [b.psum("f0_psy", (128, NB), scope=scope) for _ in range(2)],
        "ps_z": [b.psum("f0_psz", (128, 4 * NB), scope=scope) for _ in range(2)],
        "ps_o": b.psum("f0_pso", (128, NB), scope=scope),
    }
    def fl(g):
        sg.inc(g.dma_start(out=w["w1sb"][:, :], in_=w1blob), 16)
        sg.inc(g.dma_start(out=w["w2sb"][:, :], in_=w2blob), 16)
        sg.inc(g.dma_start(out=w["cpsb"][0][:, :], in_=cpblob[0]), 16)
    b.sec("gpsimd", fl)
    gy = None
    gz = [None, None]
    go = None
    gcp = [None, None]
    gx = None
    cpi = 1
    for ib, (off, n) in enumerate(blks):
        def fx(sy, off=off, n=n, g=gx, gyv=gy):
            if g:
                sy.wait_ge(g[0], g[1])
            if gyv:
                sy.wait_ge(gyv[0], gyv[1])
            for mi in range(8):
                sd.inc(sy.dma_start(out=w["x_t"][:, mi * NB:mi * NB + n],
                                    in_=xsrc[mi * 128:(mi + 1) * 128, off:off + n]), 16)
                sd.inc(sy.dma_start(out=w["xnb"][:, mi * NB:mi * NB + n],
                                    in_=xn_hbm[mi * 128:(mi + 1) * 128, off:off + n]), 16)
        b.sec("sync", fx)
        STY0 = st.v
        SVY0 = sv.v
        def fty(t, n=n, g=gy, sgv=sg.v, sdv=sd.v, SVY0=SVY0):
            t.wait_ge(sg.h, sgv)
            t.wait_ge(sd.h, sdv)
            if g:
                t.wait_ge(g[0], g[1])
            for yt in range(32):
                if yt >= 2:
                    t.wait_ge(sv.h, SVY0 + yt - 1)
                cc = yt % 8
                st.inc(t.matmul(w["ps_y"][yt % 2][:, 0:n],
                                w["w1sb"][:, yt * 128:(yt + 1) * 128],
                                w["xnb"][:, cc * NB:cc * NB + n],
                                start=True, stop=True), 1)
        b.sec("tensor", fty)
        def fvy(v, n=n, STY0=STY0):
            for yt in range(32):
                v.wait_ge(st.h, STY0 + yt + 1)
                sv.inc(v.tensor_copy(w["ysb"][:, yt * NB:yt * NB + n],
                                     w["ps_y"][yt % 2][:, 0:n]), 1)
        b.sec("vector", fvy)
        gy = sv.snap()
        ss_base = ss.v
        STZ0 = st.v
        def ftz(t, n=n, svv=sv.v, ss_base=ss_base, gz0=gz[0], gz1=gz[1]):
            t.wait_ge(sv.h, svv)
            for grp in range(32):
                pi = grp % 2
                if grp >= 2:
                    t.wait_ge(ss.h, ss_base + grp - 1)
                else:
                    g = (gz0, gz1)[pi]
                    if g:
                        t.wait_ge(g[0], g[1])
                for a in range(4):
                    kc = grp * 4 + a
                    hq, cc = kc // 32, (kc % 32) // 4
                    yt = hq * 8 + cc
                    i = t.matmul(w["ps_z"][pi][:, a * NB:a * NB + n],
                                 w["w2sb"][:, kc * 128:(kc + 1) * 128],
                                 w["ysb"][:, yt * NB:yt * NB + n], start=True, stop=True)
                st.inc(i, 1)
        b.sec("tensor", ftz)
        gel_jobs = [(STZ0 + grp + 1, grp, grp % 2) for grp in range(32)]
        def fs_all(s, jobs=gel_jobs, n=n):
            for (stv, grp, pi) in jobs:
                s.wait_ge(st.h, stv)
                if n == NB:
                    i = s.activation(w["hsb"][:, grp * 4 * NB:(grp + 1) * 4 * NB],
                                     w["ps_z"][pi][:, :], AF.Gelu)
                else:
                    for a in range(4):
                        kc = grp * 4 + a
                        i = s.activation(w["hsb"][:, kc * NB:kc * NB + n],
                                         w["ps_z"][pi][:, a * NB:a * NB + n], AF.Gelu)
                ss.inc(i, 1)
        b.sec("scalar", fs_all)
        gz[0] = (ss.h, ss_base + 31)
        gz[1] = (ss.h, ss_base + 32)
        for mi in range(8):
            use = mi % 2
            sg_use = sg.v
            pf = 1 - use
            def fg(g, mi=mi, pf=pf, wg=gcp[pf]):
                if wg:
                    g.wait_ge(wg[0], wg[1])
                sg.inc(g.dma_start(out=w["cpsb"][pf][:, :], in_=cpblob[(mi + 1) % 8]), 16)
            b.sec("gpsimd", fg)
            def ft(t, mi=mi, n=n, use=use, g=go, sgv=sg_use, ssv=ss.v, ssb=ss_base):
                t.wait_ge(sg.h, sgv)
                if mi != 0:
                    t.wait_ge(ss.h, ssv)
                if g:
                    t.wait_ge(g[0], g[1])
                for kc in range(128):
                    if mi == 0 and kc % 4 == 0:
                        t.wait_ge(ss.h, ssb + kc // 4 + 1)
                    i = t.matmul(w["ps_o"][:, 0:n],
                                 w["cpsb"][use][:, kc * 128:(kc + 1) * 128],
                                 w["hsb"][:, kc * NB:kc * NB + n],
                                 start=(kc == 0), stop=(kc == 127))
                st.inc(i, 1)
            b.sec("tensor", ft)
            gcp[use] = (st.h, st.v)
            def fv(v, mi=mi, n=n, stv=st.v, sdv=sd.v):
                v.wait_ge(st.h, stv)
                v.wait_ge(sd.h, sdv)
                sv.inc(v.tensor_tensor(w["x_t"][:, mi * NB:mi * NB + n],
                                       w["x_t"][:, mi * NB:mi * NB + n],
                                       w["ps_o"][:, 0:n], Alu.add), 1)
            b.sec("vector", fv)
            go = sv.snap()
            def fg(g, mi=mi, off=off, n=n, svv=sv.v):
                g.wait_ge(sv.h, svv)
                sg.inc(g.dma_start(out=xdst[mi * 128:(mi + 1) * 128, off:off + n],
                                   in_=w["x_t"][:, mi * NB:mi * NB + n]), 16)
            b.sec("gpsimd", fg)
        gx = sg.snap()
    b.fence()
    scope.close()


def emit_ffn0_block(b, lns, st_weights, xcur, ncol, off, n, dbgfence=0, xsrc=None):
    w = st_weights
    sv, ss, st, sg, sd = b.sem("v"), b.sem("s"), b.sem("t"), b.sem("gdma"), b.sem("sdma")
    NB = b.cfg.NB
    for yt in range(32):
        def ft(t, yt=yt, n=n, g=w["gy"][0], sgv=sg.v, svv=sv.v):
            t.wait_ge(sg.h, sgv)
            t.wait_ge(sv.h, svv)
            if g:
                t.wait_ge(g[0], g[1])
            cc = yt % 8
            st.inc(t.matmul(w["ps_y"][0][:, 0:n], w["w1sb"][:, yt * 128:(yt + 1) * 128],
                            lns.xn[:, cc * NB:cc * NB + n], start=True, stop=True), 1)
        b.sec("tensor", ft)
        def fv(v, yt=yt, n=n, stv=st.v):
            v.wait_ge(st.h, stv)
            sv.inc(v.tensor_copy(w["ysb"][:, yt * NB:yt * NB + n], w["ps_y"][0][:, 0:n]), 1)
        b.sec("vector", fv)
        w["gy"][0] = sv.snap()
    if dbgfence & 1:
        b.fence()
    for grp in range(32):
        pi = grp % 2
        def ft(t, grp=grp, n=n, pi=pi, g=w["gz"][pi], svv=sv.v):
            t.wait_ge(sv.h, svv)
            if g:
                t.wait_ge(g[0], g[1])
            for a in range(4):
                kc = grp * 4 + a
                hq, cc = kc // 32, (kc % 32) // 4
                yt = hq * 8 + cc
                i = t.matmul(w["ps_z"][pi][:, a * NB:a * NB + n],
                             w["w2sb"][:, kc * 128:(kc + 1) * 128],
                             w["ysb"][:, yt * NB:yt * NB + n], start=True, stop=True)
            st.inc(i, 1)
        b.sec("tensor", ft)
        def fs(s, grp=grp, n=n, pi=pi, stv=st.v):
            s.wait_ge(st.h, stv)
            if n == NB:
                i = s.activation(w["hsb"][:, grp * 4 * NB:(grp + 1) * 4 * NB],
                                 w["ps_z"][pi][:, :], AF.Gelu)
            else:
                for a in range(4):
                    kc = grp * 4 + a
                    i = s.activation(w["hsb"][:, kc * NB:kc * NB + n],
                                     w["ps_z"][pi][:, a * NB:a * NB + n], AF.Gelu)
            ss.inc(i, 1)
        b.sec("scalar", fs)
        w["gz"][pi] = ss.snap()
    if dbgfence & 2:
        b.fence()
    for mi in range(8):
        slot = w["cpi"]
        w["cpi"] = 1 - slot
        def fg(g, mi=mi, slot=slot, wg=w["gcp"][slot]):
            if wg:
                g.wait_ge(wg[0], wg[1])
            sg.inc(g.dma_start(out=w["cpsb"][slot][:, :], in_=w["cpblob"][mi]), 16)
        b.sec("gpsimd", fg)
        def ft(t, mi=mi, n=n, slot=slot, g=w["go"][0], sgv=sg.v, ssv=ss.v):
            t.wait_ge(sg.h, sgv)
            t.wait_ge(ss.h, ssv)
            if g:
                t.wait_ge(g[0], g[1])
            for kc in range(128):
                i = t.matmul(w["ps_o"][0][:, 0:n],
                             w["cpsb"][slot][:, kc * 128:(kc + 1) * 128],
                             w["hsb"][:, kc * NB:kc * NB + n],
                             start=(kc == 0), stop=(kc == 127))
            st.inc(i, 1)
        b.sec("tensor", ft)
        w["gcp"][slot] = (st.h, st.v)
        def fv(v, mi=mi, n=n, stv=st.v):
            v.wait_ge(st.h, stv)
            sv.inc(v.tensor_tensor(lns.x_nb[:, mi * NB:mi * NB + n],
                                   lns.x_nb[:, mi * NB:mi * NB + n],
                                   w["ps_o"][0][:, 0:n], Alu.add), 1)
        b.sec("vector", fv)
        w["go"][0] = sv.snap()
        def fg(g, mi=mi, off=off, n=n, svv=sv.v):
            g.wait_ge(sv.h, svv)
            sg.inc(g.dma_start(out=xcur[mi * 128:(mi + 1) * 128, off:off + n],
                               in_=lns.x_nb[:, mi * NB:mi * NB + n]), 16)
        b.sec("gpsimd", fg)


def mk_ffn0_state(b, cpblob, w1blob, w2blob, scope):
    NB = b.cfg.NB
    w = {
        "w1sb": b.sbuf("f0_w1", (128, 32 * 128), BF16, scope=scope),
        "w2sb": b.sbuf("f0_w2", (128, 128 * 128), BF16, scope=scope),
        "cpsb": [b.sbuf("f0_cp", (128, 128 * 128), BF16, scope=scope) for _ in range(2)],
        "ysb": b.sbuf("f0_y", (128, 32 * NB), BF16, scope=scope),
        "hsb": b.sbuf("f0_h", (128, 128 * NB), BF16, scope=scope),
        "ps_y": [b.psum("f0_psy", (128, NB), scope=scope)],
        "ps_z": [b.psum("f0_psz", (128, 4 * NB), scope=scope) for _ in range(2)],
        "ps_o": [b.psum("f0_pso", (128, NB), scope=scope)],
        "cpblob": cpblob,
        "gy": [None, None], "gz": [None, None], "go": [None, None],
        "gcp": [None, None], "cpi": 0,
    }
    sg = b.sem("gdma")
    def fl(g):
        sg.inc(g.dma_start(out=w["w1sb"][:, :], in_=w1blob[:, :]), 16)
        for half in range(4):
            sg.inc(g.dma_start(
                out=w["w2sb"][:, half * 32 * 128:(half + 1) * 32 * 128],
                in_=w2blob[:, half * 32 * 128:(half + 1) * 32 * 128]), 16)
    b.sec("gpsimd", fl)
    return w


def emit_qkv_block(b, lns, qw, off, n, tpc, ncol, do_q=True):
    sv, ss, st, sg, sd = b.sem("v"), b.sem("s"), b.sem("t"), b.sem("gdma"), b.sem("sdma")
    NB = b.cfg.NB
    ones1 = b.const_ones(1, NB)
    ones1x128 = b.const_ones(1, 128)
    for which in ("q", "k"):
        if which == "q" and not do_q:
            continue
        if which == "k" and off >= tpc:
            continue
        nn = n if which == "q" else min(n, tpc - off)
        wsb = qw["wqsb"] if which == "q" else qw["wksb"]
        bsb = qw["bqsb"] if which == "q" else qw["bksb"]
        pi0 = qw["pi"]
        qw["pi"] = (pi0 + 8) % 2
        STA0 = st.v
        SVB0 = sv.v
        def ftA(t, nn=nn, pi0=pi0, sgv=sg.v, wsb=wsb, bsb=bsb, svv=sv.v,
                gqk=tuple(qw["gqk"]), SVB0=SVB0):
            t.wait_ge(sg.h, sgv)
            t.wait_ge(sv.h, svv)
            for mi in range(8):
                pi = (pi0 + mi) % 2
                if mi >= 2:
                    t.wait_ge(sv.h, SVB0 + mi - 1)
                elif gqk[pi]:
                    t.wait_ge(gqk[pi][0], gqk[pi][1])
                ps = qw["ps_qk"][pi]
                for kc in range(8):
                    t.matmul(ps[:, 0:nn], wsb[:, (mi * 8 + kc) * 128:(mi * 8 + kc + 1) * 128],
                             lns.xn[:, kc * NB:kc * NB + nn], start=(kc == 0), stop=False)
                st.inc(t.matmul(ps[:, 0:nn], bsb[:, mi * 128:(mi + 1) * 128],
                                ones1[:, 0:nn], start=False, stop=True), 1)
        b.sec("tensor", ftA)
        def fvB(v, nn=nn, pi0=pi0, STA0=STA0, graw=qw["graw_t"]):
            if graw:
                v.wait_ge(graw[0], graw[1])
            for mi in range(8):
                v.wait_ge(st.h, STA0 + mi + 1)
                sv.inc(v.tensor_copy(qw["qraw_all"][:, mi * NB:mi * NB + nn],
                                     qw["ps_qk"][(pi0 + mi) % 2][:, 0:nn]), 1)
        b.sec("vector", fvB)
        qw["gqk"][(pi0 + 6) % 2] = (sv.h, SVB0 + 7)
        qw["gqk"][(pi0 + 7) % 2] = (sv.h, SVB0 + 8)
        STC0 = st.v
        def ftC(t, nn=nn, SVB0=SVB0, gpp=tuple(qw["gpp2"])):
            for mi in range(8):
                p2 = mi % 2
                t.wait_ge(sv.h, SVB0 + mi + 1)
                if mi >= 2:
                    t.wait_ge(sv.h, SVB0 + 8 + mi - 1)
                elif gpp[p2]:
                    t.wait_ge(gpp[p2][0], gpp[p2][1])
                st.inc(t.matmul(qw["ps_p"][p2][:, 0:nn], qw["permsb"][:, :],
                                qw["qraw_all"][:, mi * NB:mi * NB + nn],
                                start=True, stop=True), 1)
        b.sec("tensor", ftC)
        qw["graw_t"] = (st.h, st.v)
        def fvD(v, off=off, nn=nn, STC0=STC0, gout=qw["gout_t"]):
            if gout:
                v.wait_ge(gout[0], gout[1])
            for mi in range(8):
                p2 = mi % 2
                v.wait_ge(st.h, STC0 + mi + 1)
                v.tensor_tensor(qw["t1s"][:, 0:nn],
                                qw["qraw_all"][:, mi * NB:mi * NB + nn],
                                qw["cossb"][:, off:off + nn], Alu.mult)
                v.tensor_tensor(qw["qraw_all"][:, mi * NB:mi * NB + nn],
                                qw["ps_p"][p2][:, 0:nn],
                                qw["sinsb"][:, off:off + nn], Alu.mult)
                sv.inc(v.tensor_tensor(qw["orow_all"][:, mi * NB:mi * NB + nn],
                                       qw["t1s"][:, 0:nn],
                                       qw["qraw_all"][:, mi * NB:mi * NB + nn],
                                       Alu.add), 1)
        b.sec("vector", fvD)
        qw["gpp2"][0] = (sv.h, SVB0 + 8 + 7)
        qw["gpp2"][1] = (sv.h, SVB0 + 8 + 8)
        def fgE(g, off=off, nn=nn, which=which, SVD0=SVB0 + 8):
            for mi in range(8):
                g.wait_ge(sv.h, SVD0 + mi + 1)
                if which == "q":
                    sg.inc(g.dma_start(out=qw["qT_out"][mi * 128:(mi + 1) * 128, off:off + nn],
                                       in_=qw["orow_all"][:, mi * NB:mi * NB + nn]), 16)
                else:
                    sg.inc(g.dma_start(out=qw["khp_out"][mi, :, off:off + nn],
                                       in_=qw["orow_all"][:, mi * NB:mi * NB + nn]), 16)
        b.sec("gpsimd", fgE)
        qw["gout_t"] = sg.snap()
    if off < tpc:
        nn = min(n, tpc - off)
        assert nn % 128 == 0 or nn == n
        for to in range(0, nn, 128):
            tt = (off + to) // 128
            tp = tt % 2
            for vb in range(2):
                def ft(t, to=to, vb=vb, tp=tp, sgv=sg.v, svv=sv.v, g=qw["gvaug2"][tp]):
                    t.wait_ge(sg.h, sgv)
                    t.wait_ge(sv.h, svv)
                    if g:
                        t.wait_ge(g[0], g[1])
                    for kc in range(8):
                        t.matmul(qw["ps_v"][vb][:, :],
                                 lns.xn[:, kc * NB + to:kc * NB + to + 128],
                                 qw["wvsb"][:, kc * 1024 + vb * 512:kc * 1024 + (vb + 1) * 512],
                                 start=(kc == 0), stop=False)
                    st.inc(t.matmul(qw["ps_v"][vb][:, :], ones1x128[:, :],
                                    qw["bvsb"][:, vb * 512:(vb + 1) * 512],
                                    start=False, stop=True), 1)
                b.sec("tensor", ft)
                def fv(v, vb=vb, tp=tp, stv=st.v, g=qw["gvaug2"][tp]):
                    v.wait_ge(st.h, stv)
                    if g:
                        v.wait_ge(g[0], g[1])
                    sv.inc(v.tensor_copy(
                        qw["vaug"][tp][:, :].rearrange("p (h c) -> p h c", h=16)[:, vb * 8:(vb + 1) * 8, 0:64],
                        qw["ps_v"][vb][:, :].rearrange("p (h c) -> p h c", h=8)), 1)
                b.sec("vector", fv)
            def fg(g, tt=tt, tp=tp, svv=sv.v):
                g.wait_ge(sv.h, svv)
                for hg in range(4):
                    sg.inc(g.dma_start(out=qw["vhg_out"][hg, :, tt, :],
                                       in_=qw["vaug"][tp][:, hg * 260:(hg + 1) * 260]), 16)
            b.sec("gpsimd", fg)
            qw["gvaug2"][tp] = sg.snap()


def mk_qkv_state(b, scope, wq, wk, wv, bq, bk, bv, cosin, perm, qT_out, khp_out,
                 vhg_out, ncol):
    NB = b.cfg.NB
    sg, sv = b.sem("gdma"), b.sem("v")
    qw = {
        "wqsb": b.sbuf("qk_wq", (128, 64 * 128), BF16, scope=scope),
        "wksb": b.sbuf("qk_wk", (128, 64 * 128), BF16, scope=scope),
        "wvsb": b.sbuf("qk_wv", (128, 8 * 1024), BF16, scope=scope),
        "bqsb": b.sbuf("qk_bq", (1, D), BF16, scope=scope),
        "bksb": b.sbuf("qk_bk", (1, D), BF16, scope=scope),
        "bvsb": b.sbuf("qk_bv", (1, D), BF16, scope=scope),
        "permsb": b.sbuf("qk_perm", (128, 128), BF16, scope=scope),
        "cossb": b.sbuf("qk_cos", (128, ncol), BF16, scope=scope),
        "sinsb": b.sbuf("qk_sin", (128, ncol), BF16, scope=scope),
        "qraw_all": b.sbuf("qk_raw", (128, 8 * NB), BF16, scope=scope),
        "t1s": b.sbuf("qk_t1", (128, NB), BF16, scope=scope),
        "orow_all": b.sbuf("qk_orow", (128, 8 * NB), BF16, scope=scope),
        "vaug": [b.sbuf("qk_vaug", (128, 1040), BF16, scope=scope) for _ in range(2)],
        "ps_qk": [b.psum("qk_psqk", (128, NB), scope=scope) for _ in range(2)],
        "ps_p": [b.psum("qk_psp", (128, NB), scope=scope) for _ in range(2)],
        "ps_v": [b.psum("qk_psv", (128, 512), scope=scope) for _ in range(2)],
        "qT_out": qT_out, "khp_out": khp_out, "vhg_out": vhg_out,
        "gqk": [None, None], "graw_t": None, "gpp2": [None, None],
        "gout_t": None, "gvaug2": [None, None],
        "pi": 0,
    }
    def fl(g):
        for mi in range(8):
            sg.inc(g.dma_start(out=qw["wqsb"][:, mi * 8 * 128:(mi + 1) * 8 * 128],
                               in_=wq[mi]), 16)
            sg.inc(g.dma_start(out=qw["wksb"][:, mi * 8 * 128:(mi + 1) * 8 * 128],
                               in_=wk[mi]), 16)
        for kc in range(8):
            sg.inc(g.dma_start(out=qw["wvsb"][:, kc * 1024:(kc + 1) * 1024],
                               in_=wv[kc * 128:(kc + 1) * 128, :]), 16)
        sg.inc(g.dma_start(out=qw["bqsb"][:, :], in_=bq), 16)
        sg.inc(g.dma_start(out=qw["bksb"][:, :], in_=bk), 16)
        sg.inc(g.dma_start(out=qw["bvsb"][:, :], in_=bv), 16)
        sg.inc(g.dma_start(out=qw["permsb"][:, :], in_=perm[:, :]), 16)
        sg.inc(g.dma_start(out=qw["cossb"][:, :], in_=cosin[0]), 16)
        sg.inc(g.dma_start(out=qw["sinsb"][:, :], in_=cosin[1]), 16)
    b.sec("gpsimd", fl)
    def fv(v):
        v.memset(qw["vaug"][0][:, :], 1.0)
        sv.inc(v.memset(qw["vaug"][1][:, :], 1.0), 1)
    b.sec("vector", fv)
    return qw


def build_la2_program(cfg, dbg=None):
    b = B(cfg)
    NCOL, TPC, NB = cfg.NCOL, cfg.TPC, cfg.NB
    x0c = b.dram_in("x0T_cols", (D, NCOL))
    hc = b.dram_in("hT_cols", (D, NCOL))
    w1blob = b.dram_in("w1blob", (128, 32 * 128))
    w2blob = b.dram_in("w2blob", (128, 128 * 128))
    cpblob = b.dram_in("cpblob", (8, 128, 128 * 128))
    wq = b.dram_in("wq_blob", (8, 128, 8 * 128))
    wk = b.dram_in("wk_blob", (8, 128, 8 * 128))
    wv = b.dram_in("wv_nat", (D, D))
    bq = b.dram_in("bq_row", (1, D))
    bk = b.dram_in("bk_row", (1, D))
    bv = b.dram_in("bv_row", (1, D))
    cosin = b.dram_in("cosin", (2, 128, NCOL))
    perm = b.dram_in("rope_perm", (128, 128))
    x2c = b.dram_out("x2T_cols", (D, NCOL))
    qTo = b.dram_out("qT_cols", (D, NCOL), BF16)
    khp = b.dram_out("khp_own", (8, 128, TPC), BF16)
    vhg = b.dram_out("vhg_own", (4, 128, TPC // 128, 260), BF16)
    b.start()
    emit_resadd_hbm(b, x2c, x0c, hc, NCOL)
    b.fence()
    if dbg != "noffn":
        emit_ffn0_stage(b, x2c, x2c, cpblob, w1blob, w2blob, NCOL)
    lns = LnStream(b, NB)
    blks = blk_list(NCOL, NB)
    if dbg == "noqkv":
        def fo(g):
            sg = b.sem("gdma")
            g.wait_ge(sg.h, sg.v)
            sd = b.sem("sdma")
            g.wait_ge(sd.h, sd.v)
        b.sec("gpsimd", fo)
        return b.finish()
    scope = ExitStack()
    qw = mk_qkv_state(b, scope, wq, wk, wv, bq, bk, bv, cosin, perm, qTo, khp, vhg, NCOL)
    b.fence()
    lns.guard = None
    for ib, (off, n) in enumerate(blks):
        lns.block(x2c, NCOL, off, n)
        emit_qkv_block(b, lns, qw, off, n, TPC, NCOL)
        lns.guard = b.sem("gdma").snap()
    b.fence()
    scope.close()
    def fo(g):
        sg = b.sem("gdma")
        g.wait_ge(sg.h, sg.v)
        sd = b.sem("sdma")
        g.wait_ge(sd.h, sd.v)
    b.sec("gpsimd", fo)
    return b.finish()


def _rope_tables(cfg):
    inv = 1.0 / (10000.0 ** (np.arange(0, DH, 2, dtype=np.float64) / DH))
    pos = np.arange(cfg.Tb, dtype=np.float64)
    fr = pos[:, None] * inv[None, :]
    emb = np.concatenate([fr, fr], axis=1)
    cos = np.cos(emb).astype(np.float32).T
    sin = np.sin(emb).astype(np.float32).T
    cos2 = np.concatenate([cos, cos], axis=0)
    sin2 = np.concatenate([sin, sin], axis=0)
    perm = np.zeros((128, 128), np.float32)
    for base in (0, 64):
        for dd in range(32):
            perm[base + dd + 32, base + dd] = -1.0
            perm[base + dd, base + dd + 32] = 1.0
    return cos2, sin2, perm


def _col_index(cfg, core):
    q = core % 4
    q0 = q * cfg.TPC
    own = np.arange(q0, q0 + cfg.TPC)
    if q > 0:
        haloidx = np.arange(q0 - HALO, q0)
    else:
        haloidx = np.zeros(HALO, np.int64)
    spec = np.arange(4) * cfg.TPC
    return np.concatenate([own, haloidx, spec])


def prep_la2_inputs(inputs, cfg, core):
    rm = _np(inputs["L0_ffn_rail_maker"])
    ru = _np(inputs["L0_ffn_rail_up_proj"])
    w1 = np.zeros((4, 8, 128, 128), np.float32)
    for h in range(4):
        for cc in range(8):
            for rl in range(8):
                w1[h, cc, rl * 16:(rl + 1) * 16, rl * 16:(rl + 1) * 16] = rm[h, cc * 8 + rl]
    w1blob = np.ascontiguousarray(
        w1.reshape(32, 128, 128).transpose(1, 0, 2)).reshape(128, 32 * 128)
    w2 = np.zeros((4, 8, 4, 128, 128), np.float32)
    for h in range(4):
        for cc in range(8):
            for a in range(4):
                for rl in range(2):
                    r = cc * 8 + 2 * a + rl
                    w2[h, cc, a, (2 * a + rl) * 16:(2 * a + rl + 1) * 16,
                       rl * 64:(rl + 1) * 64] = ru[h, r].T
    w2blob = np.ascontiguousarray(
        w2.reshape(128, 128, 128).transpose(1, 0, 2)).reshape(128, 128 * 128)
    cpblob = _blockify_lhs(np.ascontiguousarray(_np(inputs["L0_ffn_w_cproj"]).T))
    wqkv = _np(inputs["L0_attn_b_W_qkv"])
    bqkv = _np(inputs["L0_attn_b_b_qkv"])
    cos2, sin2, perm = _rope_tables(cfg)
    cols = _col_index(cfg, core)
    cosin = np.stack([np.ascontiguousarray(cos2[:, cols]),
                      np.ascontiguousarray(sin2[:, cols])])
    return {
        "w1blob": w1blob, "w2blob": w2blob, "cpblob": cpblob,
        "wq_blob": _blockify_lhs(wqkv[:, 0:D]),
        "wk_blob": _blockify_lhs(wqkv[:, D:2 * D]),
        "wv_nat": np.ascontiguousarray(wqkv[:, 2 * D:3 * D]),
        "bq_row": bqkv[0:D].reshape(1, D).copy(),
        "bk_row": bqkv[D:2 * D].reshape(1, D).copy(),
        "bv_row": bqkv[2 * D:3 * D].reshape(1, D).copy(),
        "cosin": cosin, "rope_perm": perm,
    }





def emit_l0_attention(b, qT_hbm, khp, vhg, woutblob, bout_row, x2_hbm, x3_hbm,
                      ncol, Tb, dbg=False):
    cfg = b.cfg
    sv, ss, st, sg, sd = b.sem("v"), b.sem("s"), b.sem("t"), b.sem("gdma"), b.sem("sdma")
    scope = ExitStack()
    nkt = Tb // 128
    attnT_hbm = (b.dram_out("attn_dbg", (D, ncol), BF16) if dbg else b.dram_tmp("attnT_scratch", (D, ncol), BF16))
    qh = [b.sbuf("at_qh", (128, ncol), BF16, scope=scope) for _ in range(2)]
    kt = [b.sbuf("at_kt", (128, Tb), BF16, scope=scope) for _ in range(2)]
    vg_sb = b.sbuf("at_vg", (128, nkt * 260), BF16, scope=scope)
    P2 = [b.sbuf("at_p2", (128, 2 * 512), BF16, scope=scope) for _ in range(2)]
    rep_sb = b.sbuf("at_rep", (64, 512), F32, scope=scope)
    den_sb = b.sbuf("at_den", (1, 512), F32, scope=scope)
    ao_sb = b.sbuf("at_ao", (64, 512), BF16, scope=scope)
    ps_S2 = [b.psum("at_psS2", (128, 2 * 512), scope=scope) for _ in range(2)]
    ps_av = [b.psum("at_psav", (128, 512), scope=scope) for _ in range(2)]
    ps_rep = b.psum("at_psrep", (64, 512), scope=scope)
    ones64f = b.const_ones(1, 64, F32)
    qblks = blk_list(ncol, 512)
    g_q = None
    g_kt = [None, None]
    g_vg = None
    g_av = None
    g_S2 = [None, None]
    g_P2 = [None, None]
    for hg in range(4):
        def fgv(g, hg=hg, gv=g_vg):
            if gv:
                g.wait_ge(gv[0], gv[1])
            sg.inc(g.dma_start(out=vg_sb[:, :], in_=vhg[hg]), 16)
        b.sec("gpsimd", fgv)
        for hp in range(2):
            hgp = hg * 2 + hp
            hhA, hhB = 2 * hp, 2 * hp + 1
            pp = hgp % 2
            def fk(sy, hgp=hgp, pp=pp, gk=g_kt[hgp % 2]):
                if gk:
                    sy.wait_ge(gk[0], gk[1])
                sd.inc(sy.dma_start(out=kt[pp][:, :], in_=khp[hgp]), 16)
                sd.inc(sy.dma_start(out=qh[pp][:, :],
                                    in_=qT_hbm[hgp * 128:(hgp + 1) * 128, :]), 16)
            b.sec("sync", fk)
            for (qo, nq) in qblks:
                ng = nkt
                ss_base = ss.v
                exp_jobs = []
                for g2 in range(ng + 1):
                    pi = g2 % 2
                    if g2 < ng:
                        gS = (ss.h, ss_base + g2 - 1) if g2 >= 2 else g_S2[pi]
                        def ftS(t, qo=qo, nq=nq, g2=g2, pi=pi,
                                sdv=sd.v, gS=gS):
                            t.wait_ge(sd.h, sdv)
                            if gS:
                                t.wait_ge(gS[0], gS[1])
                            for par in range(2):
                                i = t.matmul(ps_S2[pi][:, par * 512:par * 512 + nq],
                                             kt[pp][par * 64:par * 64 + 64,
                                                g2 * 128:(g2 + 1) * 128],
                                             qh[pp][par * 64:par * 64 + 64, qo:qo + nq],
                                             start=True, stop=True)
                            st.inc(i, 1)
                        b.sec("tensor", ftS)
                        exp_jobs.append((st.v, g_P2[pi], pi))
                    if g2 >= 1:
                        kc = g2 - 1
                        pa = kc % 2
                        def ftA(t, hhA=hhA, hhB=hhB, nq=nq, kc=kc, pa=pa,
                                ssv=ss_base + kc + 1,
                                sgv=sg.v, ga=g_av):
                            t.wait_ge(ss.h, ssv)
                            t.wait_ge(sg.h, sgv)
                            if kc == 0 and ga:
                                t.wait_ge(ga[0], ga[1])
                            for pj, hx in ((0, hhA), (1, hhB)):
                                i = t.matmul(ps_av[pj][0:65, 0:nq],
                                             vg_sb[:, kc * 260 + hx * 65:
                                                   kc * 260 + (hx + 1) * 65],
                                             P2[pa][:, pj * 512:pj * 512 + nq],
                                             start=(kc == 0), stop=(kc == nkt - 1))
                            st.inc(i, 1)
                        b.sec("tensor", ftA)
                        g_P2[pa] = (st.h, st.v)
                def fsE_all(s, jobs=exp_jobs, nq=nq):
                    for (stv, gP, pi) in jobs:
                        s.wait_ge(st.h, stv)
                        if gP:
                            s.wait_ge(gP[0], gP[1])
                        if nq == 512:
                            i = s.activation(P2[pi][:, :], ps_S2[pi][:, :],
                                             AF.Exp, scale=0.125)
                        else:
                            for j in range(2):
                                i = s.activation(P2[pi][:, j * 512:j * 512 + nq],
                                                 ps_S2[pi][:, j * 512:j * 512 + nq],
                                                 AF.Exp, scale=0.125)
                        ss.inc(i, 1)
                b.sec("scalar", fsE_all)
                g_S2[0] = (ss.h, ss_base + ng - 1)
                g_S2[1] = (ss.h, ss_base + ng)
                SVT = sv.v
                STT = st.v
                SG0 = sg.v
                def fvT(v, nq=nq, stv=st.v, sgv=sg.v, STT=STT, SG0=SG0):
                    v.wait_ge(st.h, stv)
                    v.wait_ge(sg.h, sgv)
                    for pj in range(2):
                        v.tensor_copy(den_sb[0:1, 0:nq], ps_av[pj][64:65, 0:nq])
                        sv.inc(v.reciprocal(den_sb[0:1, 0:nq], den_sb[0:1, 0:nq]), 1)
                        v.wait_ge(st.h, STT + pj + 1)
                        if pj == 1:
                            v.wait_ge(sg.h, SG0 + 16)
                        v.tensor_copy(rep_sb[:, 0:nq], ps_rep[:, 0:nq])
                        sv.inc(v.tensor_tensor(ao_sb[:, 0:nq], ps_av[pj][0:64, 0:nq],
                                               rep_sb[:, 0:nq], Alu.mult), 1)
                b.sec("vector", fvT)
                def ftT(t, nq=nq, SVT=SVT):
                    for pj in range(2):
                        t.wait_ge(sv.h, SVT + 2 * pj + 1)
                        st.inc(t.matmul(ps_rep[:, 0:nq], ones64f[:, :],
                                        den_sb[0:1, 0:nq], start=True, stop=True), 1)
                b.sec("tensor", ftT)
                def fgT(g, hgp=hgp, qo=qo, nq=nq, SVT=SVT):
                    for par in range(2):
                        g.wait_ge(sv.h, SVT + 2 * par + 2)
                        sg.inc(g.dma_start(
                            out=attnT_hbm[hgp * 128 + par * 64:
                                          hgp * 128 + par * 64 + 64, qo:qo + nq],
                            in_=ao_sb[:, 0:nq]), 16)
                b.sec("gpsimd", fgT)
                g_av = sv.snap()
            g_kt[hgp % 2] = (st.h, st.v)
        g_vg = (st.h, st.v)
    b.fence()
    scope.close()
    scope = ExitStack()
    wsb = b.sbuf("wo_w", (128, 64 * 128), BF16, scope=scope)
    bsb = b.sbuf("wo_b", (1, D), BF16, scope=scope)
    a_nb = b.sbuf("wo_a", (128, 8 * 512), BF16, scope=scope)
    x_nb = b.sbuf("wo_x", (128, 8 * 512), F32, scope=scope)
    ps_o = [b.psum("wo_ps", (128, 512), scope=scope) for _ in range(2)]
    ones1 = b.const_ones(1, 512)
    def fl(g):
        sg.inc(g.dma_start(out=bsb[:, :], in_=bout_row), 16)
        for mi in range(8):
            sg.inc(g.dma_start(out=wsb[:, mi * 1024:(mi + 1) * 1024],
                               in_=woutblob[mi]), 16)
    b.sec("gpsimd", fl)
    g_a = None
    g_x = None
    g_o = [None, None]
    for (qo, nq) in qblks:
        SD0 = sd.v
        def fla(sy, qo=qo, nq=nq, ga=g_a, gx=g_x):
            if ga:
                sy.wait_ge(ga[0], ga[1])
            if gx:
                sy.wait_ge(gx[0], gx[1])
            for kc in range(8):
                sd.inc(sy.dma_start(out=a_nb[:, kc * 512:kc * 512 + nq],
                                    in_=attnT_hbm[kc * 128:(kc + 1) * 128, qo:qo + nq]), 16)
            for mi in range(8):
                sd.inc(sy.dma_start(out=x_nb[:, mi * 512:mi * 512 + nq],
                                    in_=x2_hbm[mi * 128:(mi + 1) * 128, qo:qo + nq]), 16)
        b.sec("sync", fla)
        ST0 = st.v
        SV0 = sv.v
        def ft(t, nq=nq, sdv=sd.v, sgv=sg.v, gO=tuple(g_o), SV0=SV0):
            t.wait_ge(sd.h, sdv)
            t.wait_ge(sg.h, sgv)
            for mi in range(8):
                pi = mi % 2
                if mi >= 2:
                    t.wait_ge(sv.h, SV0 + mi - 1)
                elif gO[pi]:
                    t.wait_ge(gO[pi][0], gO[pi][1])
                for kc in range(8):
                    t.matmul(ps_o[pi][:, 0:nq], wsb[:, mi * 1024 + kc * 128:mi * 1024 + (kc + 1) * 128],
                             a_nb[:, kc * 512:kc * 512 + nq],
                             start=(kc == 0), stop=False)
                st.inc(t.matmul(ps_o[pi][:, 0:nq], bsb[:, mi * 128:(mi + 1) * 128],
                                ones1[:, 0:nq], start=False, stop=True), 1)
        b.sec("tensor", ft)
        def fv(v, nq=nq, ST0=ST0):
            for mi in range(8):
                v.wait_ge(st.h, ST0 + mi + 1)
                sv.inc(v.tensor_tensor(x_nb[:, mi * 512:mi * 512 + nq],
                                       x_nb[:, mi * 512:mi * 512 + nq],
                                       ps_o[mi % 2][:, 0:nq], Alu.add), 1)
        b.sec("vector", fv)
        g_o[0] = (sv.h, SV0 + 7)
        g_o[1] = (sv.h, SV0 + 8)
        def fgo(g, qo=qo, nq=nq, SV0=SV0):
            for mi in range(8):
                g.wait_ge(sv.h, SV0 + mi + 1)
                sg.inc(g.dma_start(out=x3_hbm[mi * 128:(mi + 1) * 128, qo:qo + nq],
                                   in_=x_nb[:, mi * 512:mi * 512 + nq]), 16)
        b.sec("gpsimd", fgo)
        g_x = sg.snap()
        g_a = (st.h, st.v)
    b.fence()
    scope.close()


def build_la4_program(cfg):
    b = B(cfg)
    NCOL, NB = cfg.NCOL, cfg.NB
    x3c = b.dram_in("x3T_cols", (D, NCOL))
    w1blob = b.dram_in("w1blob", (128, 32 * 128))
    w2blob = b.dram_in("w2blob", (128, 128 * 128))
    cpblob = b.dram_in("cpblob", (8, 128, 128 * 128))
    x4c = b.dram_out("x4T_cols", (D, NCOL))
    b.start()
    emit_ffn0_stage(b, x3c, x4c, cpblob, w1blob, w2blob, NCOL)
    def fo(g):
        for dn in ("gdma", "sdma"):
            sm = b.sem(dn)
            g.wait_ge(sm.h, sm.v)
    b.sec("gpsimd", fo)
    return b.finish()


def build_la3_program(cfg, dbg=False):
    b = B(cfg)
    NCOL, Tb = cfg.NCOL, cfg.Tb
    qT = b.dram_in("qT_cols", (D, NCOL), BF16)
    khp = b.dram_in("khp_full", (8, 128, Tb), BF16)
    vhg = b.dram_in("vhg_full", (4, 128, Tb // 128, 260), BF16)
    wout = b.dram_in("wout_blob", (8, 128, 8 * 128))
    bout = b.dram_in("bout_row", (1, D))
    x2c = b.dram_in("x2T_cols", (D, NCOL))
    x3c = b.dram_out("x3T_cols", (D, NCOL))
    b.start()
    emit_l0_attention(b, qT, khp, vhg, wout, bout, x2c, x3c, NCOL, Tb, dbg=dbg)
    def fo(g):
        for dn in ("gdma", "sdma"):
            sm = b.sem(dn)
            g.wait_ge(sm.h, sm.v)
    b.sec("gpsimd", fo)
    return b.finish()


def prep_la3_inputs(inputs, cfg):
    return {
        "wout_blob": _blockify_lhs(_np(inputs["L0_attn_b_W_out"])),
        "bout_row": _np(inputs["L0_attn_b_b_out"]).reshape(1, D),
    }





def emit_l1_attn_a(b, x4_hbm, x5_hbm, kT_in, v_in, ncol, tpc, dbg=False):
    cfg = b.cfg
    NB = cfg.NB
    sv, ss, st, sg, sd = b.sem("v"), b.sem("s"), b.sem("t"), b.sem("gdma"), b.sem("sdma")
    scope = ExitStack()
    lns = LnStream(b, NB, nbuf=1)
    kt_sb = b.sbuf("aa_kt", (128, 8 * KLEN), BF16, scope=scope)
    v_sb = b.sbuf("aa_v", (128, 32 * 1024), BF16, scope=scope)
    S_sb = b.sbuf("aa_S", (128, KLEN), F32, scope=scope)
    P_sb = b.sbuf("aa_P", (128, KLEN), BF16, scope=scope)
    PT_sb = P_sb
    srep = b.sbuf("aa_srep", (128, KLEN), BF16, scope=scope)
    ident = b.sbuf("aa_id", (128, 128), BF16, scope=scope)
    nmx = b.sbuf("aa_nmx", (128, 1), F32, scope=scope)
    den = b.sbuf("aa_den", (128, 1), F32, scope=scope)
    x_t = b.sbuf("aa_x", (128, 8 * 128), F32, scope=scope)
    ps_S = [b.psum("aa_psS", (128, 512), scope=scope) for _ in range(2)]
    ps_T = [b.psum("aa_psT", (128, 1024), BF16, scope=scope) for _ in range(2)]
    ps_o = [b.psum("aa_pso", (128, 128), scope=scope) for _ in range(2)]
    scale4 = b.dram_tmp("aa_scale4", (4, D))
    ones128 = b.const_ones(128, 128)

    def fl(g):
        for kc in range(8):
            sg.inc(g.dma_start(out=kt_sb[:, kc * KLEN:(kc + 1) * KLEN],
                               in_=kT_in[kc * 128:(kc + 1) * 128, :]), 16)
        sg.inc(g.dma_start(out=v_sb[:, :].rearrange("p (t v) -> p t v", t=32),
                           in_=v_in[:, :].rearrange("(t p) v -> p t v", p=128)), 16)
        g.memset(ident[:, :], 1.0)
        b.sem("gc").inc(g.affine_select(ident[:, :], ident[:, :], [[-1, 128]],
                                        Alu.is_equal, 0.0, base=0, channel_multiplier=1), 1)
    b.sec("gpsimd", fl)
    sgc = b.sem("gc")

    blks = blk_list(ncol, NB)
    (off_l, n_l) = blks[-1]
    lns.block(x4_hbm, ncol, off_l, n_l)
    def fgs(g, off=off_l, n=n_l, svv=sv.v):
        g.wait_ge(sv.h, svv)
        for cc in range(8):
            sg.inc(g.dma_start(
                out=scale4[:, cc * 128:(cc + 1) * 128].rearrange("j p -> p j"),
                in_=lns.xn[:, cc * NB + n - 4:cc * NB + n],
                allow_slow_non_contiguous=True), 16)
    b.sec("gpsimd", fgs)
    iscope = ExitStack()
    srow = b.sbuf("aa_srow", (1, KLEN), BF16, scope=iscope)
    ps_r = ps_S[0]
    def fls(g2, sgv=sg.v):
        g2.wait_ge(sg.h, sgv)
        sg.inc(g2.dma_start(out=srow[:, :],
                            in_=scale4[:, :].rearrange("a b -> (a b)")[None, :]), 16)
    b.sec("gpsimd", fls)
    for j in range(8):
        def ft(t, j=j, svv=sv.v, sgv=sg.v):
            t.wait_ge(sg.h, sgv)
            t.wait_ge(sv.h, svv)
            st.inc(t.matmul(ps_r[:, :], ones128[0:1, :], srow[0:1, j * 512:(j + 1) * 512],
                            start=True, stop=True), 1)
        b.sec("tensor", ft)
        def fv(v, j=j, stv=st.v):
            v.wait_ge(st.h, stv)
            sv.inc(v.tensor_copy(srep[:, j * 512:(j + 1) * 512], ps_r[:, :]), 1)
        b.sec("vector", fv)
    b.fence()
    iscope.close()

    lns.guard = None
    g_S = [None, None]
    g_T = [None, None]
    g_o = [None, None]
    g_soft = None
    g_xnb = None
    for (off, n) in blks:
        lns.block(x4_hbm, ncol, off, n)
        subs = [(0, n)] if n <= 128 else ([(i * 128, 128) for i in range(n // 128)]
                                          if n % 128 == 0 else [(0, (n + 1) // 2), ((n + 1) // 2, n - (n + 1) // 2)])
        for (so, sn) in subs:
            STS0 = st.v
            SVS0 = sv.v
            def ftS(t, off=off, so=so, sn=sn, svv=sv.v, gS=tuple(g_S), SVS0=SVS0):
                t.wait_ge(sv.h, svv)
                for kb in range(8):
                    pi = kb % 2
                    if kb >= 2:
                        t.wait_ge(sv.h, SVS0 + kb - 1)
                    elif gS[pi]:
                        t.wait_ge(gS[pi][0], gS[pi][1])
                    for kc in range(8):
                        i = t.matmul(ps_S[pi][0:sn, 0:512],
                                     lns.xn[:, kc * NB + so:kc * NB + so + sn],
                                     kt_sb[:, kc * KLEN + kb * 512:kc * KLEN + (kb + 1) * 512],
                                     start=(kc == 0), stop=(kc == 7))
                    st.inc(i, 1)
            b.sec("tensor", ftS)
            def fvS(v, sn=sn, STS0=STS0, gs=g_soft):
                if gs:
                    v.wait_ge(gs[0], gs[1])
                for kb in range(8):
                    v.wait_ge(st.h, STS0 + kb + 1)
                    sv.inc(v.tensor_tensor(S_sb[0:sn, kb * 512:(kb + 1) * 512],
                                           ps_S[kb % 2][0:sn, 0:512],
                                           srep[0:sn, kb * 512:(kb + 1) * 512], Alu.mult), 1)
            b.sec("vector", fvS)
            g_S[0] = (sv.h, SVS0 + 7)
            g_S[1] = (sv.h, SVS0 + 8)
            def fvm(v, sn=sn, ssv=ss.v):
                v.wait_ge(ss.h, ssv)
                sv.inc(v.reduce_max(nmx[0:sn, :], S_sb[0:sn, :], axis=AX.X, negate=True), 1)
            b.sec("vector", fvm)
            def fse(s, sn=sn, svv=sv.v):
                s.wait_ge(sv.h, svv)
                ss.inc(s.activation(P_sb[0:sn, :], S_sb[0:sn, :], AF.Exp,
                                    bias=nmx[0:sn, :], accum_out=den[0:sn, :]), 1)
            b.sec("scalar", fse)
            def fvd(v, sn=sn, ssv=ss.v):
                v.wait_ge(ss.h, ssv)
                v.reciprocal(den[0:sn, :], den[0:sn, :])
                sv.inc(v.tensor_scalar(P_sb[0:sn, :], P_sb[0:sn, :], den[0:sn, :],
                                       None, Alu.mult), 1)
            b.sec("vector", fvd)
            g_soft = sv.snap()
            if dbg and off == 0 and so == 0:
                b.fence()
                sdbg = b.dram_out("S_dbg", (128, KLEN))
                pdbg = b.dram_out("P_dbg", (128, KLEN), BF16)
                ddbg = b.dram_out("den_dbg", (128, 1))
                srdbg = b.dram_out("srep_dbg", (128, KLEN), BF16)
                def fdbg(g, svv=sv.v):
                    g.wait_ge(sv.h, svv)
                    sg.inc(g.dma_start(out=sdbg[:, :], in_=S_sb[:, :]), 16)
                    sg.inc(g.dma_start(out=pdbg[:, :], in_=P_sb[:, :]), 16)
                    sg.inc(g.dma_start(out=ddbg[:, :], in_=den[:, :]), 16)
                    sg.inc(g.dma_start(out=srdbg[:, :], in_=srep[:, :]), 16)
                    g.wait_ge(sg.h, sg.v)
                b.sec("gpsimd", fdbg)
                b.fence()
            STT0 = st.v
            SVT0 = sv.v
            def ftT(t, sn=sn, svv=sv.v, gT=tuple(g_T), sgcv=sgc.v, SVT0=SVT0):
                t.wait_ge(sv.h, svv)
                t.wait_ge(sgc.h, sgcv)
                for kc in range(32):
                    pi = kc % 2
                    if kc >= 2:
                        t.wait_ge(sv.h, SVT0 + kc - 1)
                    elif gT[pi]:
                        t.wait_ge(gT[pi][0], gT[pi][1])
                    st.inc(t.transpose(ps_T[pi][0:128, 0:sn],
                                       P_sb[0:sn, kc * 128:(kc + 1) * 128],
                                       ident[0:sn, 0:sn]), 1)
            b.sec("tensor", ftT)
            def fvT(v, sn=sn, STT0=STT0):
                for kc in range(32):
                    v.wait_ge(st.h, STT0 + kc + 1)
                    sv.inc(v.tensor_copy(PT_sb[:, kc * 128:kc * 128 + sn],
                                         ps_T[kc % 2][:, 0:sn]), 1)
            b.sec("vector", fvT)
            g_T[0] = (sv.h, SVT0 + 31)
            g_T[1] = (sv.h, SVT0 + 32)
            STA0 = st.v
            SVA0 = SVT0 + 32
            SD0 = sd.v
            def ftA(t, sn=sn, SVA0=SVA0, gO=tuple(g_o)):
                t.wait_ge(sv.h, SVA0)
                for mi in range(8):
                    pi = mi % 2
                    if mi >= 2:
                        t.wait_ge(sv.h, SVA0 + mi - 1)
                    elif gO[pi]:
                        t.wait_ge(gO[pi][0], gO[pi][1])
                    for kc in range(32):
                        i = t.matmul(ps_o[pi][:, 0:sn],
                                     v_sb[:, kc * 1024 + mi * 128:kc * 1024 + (mi + 1) * 128],
                                     PT_sb[:, kc * 128:kc * 128 + sn],
                                     start=(kc == 0), stop=(kc == 31))
                    st.inc(i, 1)
            b.sec("tensor", ftA)
            def fx(sy, off=off, so=so, sn=sn, gx=g_xnb):
                if gx:
                    sy.wait_ge(gx[0], gx[1])
                for mi in range(8):
                    sd.inc(sy.dma_start(out=x_t[:, mi * 128:mi * 128 + sn],
                                        in_=x4_hbm[mi * 128:(mi + 1) * 128,
                                                   off + so:off + so + sn]), 16)
            b.sec("sync", fx)
            def fvA(v, sn=sn, STA0=STA0, SD0=SD0):
                for mi in range(8):
                    v.wait_ge(st.h, STA0 + mi + 1)
                    v.wait_ge(sd.h, SD0 + 16 * (mi + 1))
                    sv.inc(v.tensor_tensor(x_t[:, mi * 128:mi * 128 + sn],
                                           x_t[:, mi * 128:mi * 128 + sn],
                                           ps_o[mi % 2][:, 0:sn], Alu.add), 1)
            b.sec("vector", fvA)
            g_o[0] = (sv.h, SVA0 + 7)
            g_o[1] = (sv.h, SVA0 + 8)
            def fgo(g, off=off, so=so, sn=sn, SVA0=SVA0):
                for mi in range(8):
                    g.wait_ge(sv.h, SVA0 + mi + 1)
                    sg.inc(g.dma_start(out=x5_hbm[mi * 128:(mi + 1) * 128,
                                                  off + so:off + so + sn],
                                       in_=x_t[:, mi * 128:mi * 128 + sn]), 16)
            b.sec("gpsimd", fgo)
            g_xnb = sg.snap()
        lns.guard = (st.h, st.v)
    b.fence()
    scope.close()


def build_la5_program(cfg, dbg=False):
    b = B(cfg)
    NCOL, TPC = cfg.NCOL, cfg.TPC
    x4c = b.dram_in("x4T_cols", (D, NCOL))
    kT = b.dram_in("l1a_kT", (D, KLEN))
    vn = b.dram_in("l1a_v", (KLEN, D))
    x5c = b.dram_out("x5T_cols", (D, NCOL))
    b.start()
    emit_l1_attn_a(b, x4c, x5c, kT, vn, NCOL, TPC, dbg=dbg)
    def fo(g):
        for dn in ("gdma", "sdma"):
            sm = b.sem(dn)
            g.wait_ge(sm.h, sm.v)
    b.sec("gpsimd", fo)
    return b.finish()





def emit_l1_ffn(b, xsrc, xdst, selblob, selb_rows, compblob, compb_rows, ncol):
    cfg = b.cfg
    NB = cfg.NB
    sv, ss, st, sg, sd = b.sem("v"), b.sem("s"), b.sem("t"), b.sem("gdma"), b.sem("sdma")
    scope = ExitStack()
    lns = LnStream(b, NB, nbuf=1)
    xn_full = b.sbuf("lf_xn", (128, 8 * ncol), BF16, scope=scope)
    Lbuf = b.sbuf("lf_L", (128, 8 * ncol), F32, scope=scope)
    mbuf = b.sbuf("lf_m", (128, ncol), F32, scope=scope)
    msk = b.sbuf("lf_msk", (128, ncol), F32, scope=scope)
    acc = b.sbuf("lf_acc", (128, ncol), F32, scope=scope)
    x_t = b.sbuf("lf_x", (128, ncol), F32, scope=scope)
    wsb = b.sbuf("lf_w", (128, 2 * 1024), BF16, scope=scope)
    brow = b.sbuf("lf_b", (1, 2 * 128), BF16, scope=scope)
    ps = [b.psum("lf_ps", (128, 512), scope=scope) for _ in range(2)]
    ones1 = b.const_ones(1, 512)
    blks = blk_list(ncol, NB)
    for (off, n) in blks:
        lns.block(xsrc, ncol, off, n)
        def fv(v, off=off, n=n, g=lns.guard):
            for cc in range(8):
                i = v.tensor_copy(xn_full[:, cc * ncol + off:cc * ncol + off + n],
                                  lns.xn[:, cc * NB:cc * NB + n])
            sv.inc(i, 1)
        b.sec("vector", fv)
        lns.guard = sv.snap()
    b.fence()
    nbl2 = blk_list(ncol, 512)
    g_w = [None, None]
    g_ps = [None, None]
    g_sel = None
    for oc in range(8):
        for n_e in range(8):
            slot = n_e % 2
            def fg(g, oc=oc, n_e=n_e, slot=slot, gw=g_w[slot]):
                if gw:
                    g.wait_ge(gw[0], gw[1])
                sg.inc(g.dma_start(out=wsb[:, slot * 1024:(slot + 1) * 1024],
                                   in_=selblob[n_e, oc]), 16)
                sg.inc(g.dma_start(out=brow[:, slot * 128:(slot + 1) * 128],
                                   in_=selb_rows[n_e, 0:1, oc * 128:(oc + 1) * 128]), 16)
            b.sec("gpsimd", fg)
            for (qo, nq) in nbl2:
                pi = (n_e * len(nbl2) + len(nbl2) + nbl2.index((qo, nq))) % 2
                def ft(t, oc=oc, n_e=n_e, slot=slot, qo=qo, nq=nq, pi=pi,
                       sgv=sg.v, gp=g_ps[pi]):
                    t.wait_ge(sg.h, sgv)
                    if gp:
                        t.wait_ge(gp[0], gp[1])
                    for kc in range(8):
                        t.matmul(ps[pi][:, 0:nq],
                                 wsb[:, slot * 1024 + kc * 128:slot * 1024 + (kc + 1) * 128],
                                 xn_full[:, kc * ncol + qo:kc * ncol + qo + nq],
                                 start=(kc == 0), stop=False)
                    st.inc(t.matmul(ps[pi][:, 0:nq], brow[:, slot * 128:slot * 128 + 128],
                                    ones1[:, 0:nq], start=False, stop=True), 1)
                b.sec("tensor", ft)
                def fs(s, n_e=n_e, qo=qo, nq=nq, pi=pi, stv=st.v, gsl=g_sel):
                    s.wait_ge(st.h, stv)
                    if gsl:
                        s.wait_ge(gsl[0], gsl[1])
                    ss.inc(s.copy(Lbuf[:, n_e * ncol + qo:n_e * ncol + qo + nq],
                                  ps[pi][:, 0:nq]), 1)
                b.sec("scalar", fs)
                def fv(v, n_e=n_e, qo=qo, nq=nq, ssv=ss.v):
                    v.wait_ge(ss.h, ssv)
                    if n_e == 0:
                        i = v.tensor_copy(mbuf[:, qo:qo + nq],
                                          Lbuf[:, qo:qo + nq])
                    else:
                        i = v.tensor_tensor(mbuf[:, qo:qo + nq],
                                            mbuf[:, qo:qo + nq],
                                            Lbuf[:, n_e * ncol + qo:n_e * ncol + qo + nq],
                                            Alu.max)
                    sv.inc(i, 1)
                b.sec("vector", fv)
                g_ps[pi] = ss.snap()
            g_w[slot] = (st.h, st.v)
        for n_e in range(8):
            slot = n_e % 2
            def fg(g, oc=oc, n_e=n_e, slot=slot, gw=g_w[slot]):
                if gw:
                    g.wait_ge(gw[0], gw[1])
                sg.inc(g.dma_start(out=wsb[:, slot * 1024:(slot + 1) * 1024],
                                   in_=compblob[n_e, oc]), 16)
                sg.inc(g.dma_start(out=brow[:, slot * 128:(slot + 1) * 128],
                                   in_=compb_rows[n_e, 0:1, oc * 128:(oc + 1) * 128]), 16)
            b.sec("gpsimd", fg)
            for (qo, nq) in nbl2:
                pi = (n_e * len(nbl2) + nbl2.index((qo, nq))) % 2
                def ft(t, n_e=n_e, slot=slot, qo=qo, nq=nq, pi=pi, sgv=sg.v, gp=g_ps[pi]):
                    t.wait_ge(sg.h, sgv)
                    if gp:
                        t.wait_ge(gp[0], gp[1])
                    for kc in range(8):
                        t.matmul(ps[pi][:, 0:nq],
                                 wsb[:, slot * 1024 + kc * 128:slot * 1024 + (kc + 1) * 128],
                                 xn_full[:, kc * ncol + qo:kc * ncol + qo + nq],
                                 start=(kc == 0), stop=False)
                    st.inc(t.matmul(ps[pi][:, 0:nq], brow[:, slot * 128:slot * 128 + 128],
                                    ones1[:, 0:nq], start=False, stop=True), 1)
                b.sec("tensor", ft)
                def fv(v, n_e=n_e, qo=qo, nq=nq, pi=pi, stv=st.v, svv=sv.v):
                    v.wait_ge(st.h, stv)
                    v.tensor_tensor(msk[:, qo:qo + nq],
                                    Lbuf[:, n_e * ncol + qo:n_e * ncol + qo + nq],
                                    mbuf[:, qo:qo + nq], Alu.is_ge)
                    if n_e == 0:
                        i = v.tensor_tensor(acc[:, qo:qo + nq], msk[:, qo:qo + nq],
                                            ps[pi][:, 0:nq], Alu.mult)
                    else:
                        v.tensor_tensor(msk[:, qo:qo + nq], msk[:, qo:qo + nq],
                                        ps[pi][:, 0:nq], Alu.mult)
                        i = v.tensor_tensor(acc[:, qo:qo + nq], acc[:, qo:qo + nq],
                                            msk[:, qo:qo + nq], Alu.add)
                    sv.inc(i, 1)
                b.sec("vector", fv)
                g_ps[pi] = sv.snap()
            g_w[slot] = (st.h, st.v)
        def fx(sy, oc=oc, sgv=sg.v):
            sy.wait_ge(sg.h, sgv)
            sd.inc(sy.dma_start(out=x_t[:, :],
                                in_=xsrc[oc * 128:(oc + 1) * 128, :]), 16)
        b.sec("sync", fx)
        def fv(v, oc=oc, sdv=sd.v, svv=sv.v):
            v.wait_ge(sd.h, sdv)
            sv.inc(v.tensor_tensor(x_t[:, :], x_t[:, :], acc[:, :], Alu.add), 1)
        b.sec("vector", fv)
        def fgo(g, oc=oc, svv=sv.v):
            g.wait_ge(sv.h, svv)
            sg.inc(g.dma_start(out=xdst[oc * 128:(oc + 1) * 128, :], in_=x_t[:, :]), 16)
        b.sec("gpsimd", fgo)
        g_sel = sv.snap()
    b.fence()
    scope.close()


def build_la6_program(cfg, inname="x5T_cols", outname="x6T_cols"):
    b = B(cfg)
    NCOL = cfg.NCOL
    xsrc = b.dram_in(inname, (D, NCOL))
    selblob = b.dram_in("sel_blob", (8, 8, 128, 1024))
    selb = b.dram_in("selb_rows", (8, 1, D))
    compblob = b.dram_in("comp_blob", (8, 8, 128, 1024))
    compb = b.dram_in("compb_rows", (8, 1, D))
    xdst = b.dram_out(outname, (D, NCOL))
    b.start()
    emit_l1_ffn(b, xsrc, xdst, selblob, selb, compblob, compb, NCOL)
    def fo(g):
        for dn in ("gdma", "sdma"):
            sm = b.sem(dn)
            g.wait_ge(sm.h, sm.v)
    b.sec("gpsimd", fo)
    return b.finish()


def prep_l1ffn_inputs(inputs, cfg):
    sel_w = _np(inputs["L1_ffn_sel_w"])
    sel_b = _np(inputs["L1_ffn_sel_b"])
    comp_w = _np(inputs["L1_ffn_comp_w"])
    comp_b = _np(inputs["L1_ffn_comp_b"])
    selT = sel_w.reshape(1024, 8, 1024).transpose(1, 2, 0)
    selblob = np.zeros((8, 8, 128, 1024), np.float32)
    compblob = np.zeros((8, 8, 128, 1024), np.float32)
    for n_e in range(8):
        bb = _blockify_lhs(np.ascontiguousarray(selT[n_e]))
        selblob[n_e] = bb
        cb = _blockify_lhs(np.ascontiguousarray(comp_w[n_e].T))
        compblob[n_e] = cb
    selb_rows = np.ascontiguousarray(sel_b.reshape(1024, 8).T).reshape(8, 1, D)
    compb_rows = comp_b.reshape(8, 1, D).copy()
    return {"sel_blob": selblob, "selb_rows": selb_rows,
            "comp_blob": compblob, "compb_rows": compb_rows}





def emit_l1_attn_b(b, x6_hbm, x7_hbm, W, ncol, tpc):
    cfg = b.cfg
    NB = cfg.NB
    sv, ss, st, sg, sd = b.sem("v"), b.sem("s"), b.sem("t"), b.sem("gdma"), b.sem("sdma")
    sgc = b.sem("gc")
    nwin = tpc // WINDOW
    y0_hbm = b.dram_tmp("lb_y0", (D, ncol))
    xin1_hbm = b.dram_tmp("lb_xin1", (D, tpc))
    lns = LnStream(b, NB, nbuf=1)
    attnT_hbm = b.dram_tmp("lb_attnT", (D, tpc), BF16)
    ln1g = b.sbuf("lb_ln1g", (128, 8), F32)
    ln1b = b.sbuf("lb_ln1b", (128, 8), F32)
    ln2g = b.sbuf("lb_ln2g", (128, 8), F32)
    ln2b = b.sbuf("lb_ln2b", (128, 8), F32)
    acol = b.sbuf("lb_acol", (128, 1), F32)
    hgate = b.sbuf("lb_hgate", (128, HALO), F32)
    scopeCD = ExitStack()
    g_sb = b.sbuf("lb_g", (128, 8 * tpc), BF16, scope=scopeCD)
    states = b.sbuf("lb_states", (128, HALO + tpc), F32, scope=scopeCD)
    states_bf = b.sbuf("lb_statesbf", (128, tpc), BF16, scope=scopeCD)
    def fl(sy):
        for t, src in ((ln1g, W["ln1g_col"]), (ln1b, W["ln1b_col"]),
                       (ln2g, W["ln2g_col"]), (ln2b, W["ln2b_col"]),
                       (acol, W["A_col"]), (hgate, W["halo_gate"])):
            sd.inc(sy.dma_start(out=t[:, :], in_=src), 16)
    b.sec("sync", fl)
    b.fence()

    scopeA = ExitStack()
    xn_full = b.sbuf("lb_xn", (128, 8 * ncol), BF16, scope=scopeA)
    blks = blk_list(ncol, NB)
    for (off, n) in blks:
        lns.block(x6_hbm, ncol, off, n)
        def fv(v, off=off, n=n):
            for cc in range(8):
                v.tensor_tensor(lns.xb[:, cc * NB:cc * NB + n],
                                lns.x_nb[:, cc * NB:cc * NB + n],
                                lns.mu[:, 0:n], Alu.subtract)
                v.tensor_tensor(lns.xb[:, cc * NB:cc * NB + n],
                                lns.xb[:, cc * NB:cc * NB + n],
                                lns.rstd[:, 0:n], Alu.mult)
                i = v.tensor_copy(lns.x_nb[:, cc * NB:cc * NB + n],
                                  lns.xb[:, cc * NB:cc * NB + n])
            sv.inc(i, 1)
        b.sec("vector", fv)
        def fv2(v, off=off, n=n, svv=sv.v):
            for cc in range(8):
                i = v.tensor_scalar(xn_full[:, cc * ncol + off:cc * ncol + off + n],
                                    lns.x_nb[:, cc * NB:cc * NB + n],
                                    ln1g[:, cc:cc + 1], ln1b[:, cc:cc + 1],
                                    Alu.mult, Alu.add)
            sv.inc(i, 1)
        b.sec("vector", fv2)
        def fgo(g, off=off, n=n, svv=sv.v):
            g.wait_ge(sv.h, svv)
            for cc in range(8):
                sg.inc(g.dma_start(out=y0_hbm[cc * 128:(cc + 1) * 128, off:off + n],
                                   in_=lns.x_nb[:, cc * NB:cc * NB + n]), 16)
        b.sec("gpsimd", fgo)
        lns.guard = sg.snap()
    b.fence()

    scopeB = ExitStack()
    wqsb = b.sbuf("lb_wq", (128, 64 * 128), BF16, scope=scopeB)
    wksb = b.sbuf("lb_wk", (128, 64 * 128), BF16, scope=scopeB)
    wvsb = b.sbuf("lb_wv", (128, 8 * 1024), BF16, scope=scopeB)
    bqkvsb = b.sbuf("lb_bqkv", (1, 3 * D), BF16, scope=scopeB)
    masksb = b.sbuf("lb_mask", (128, 2 * 4 * 256), BF16, scope=scopeB)
    qwin = b.sbuf("lb_qwin", (128, 8 * 256), BF16, scope=scopeB)
    kwin = b.sbuf("lb_kwin", (128, 2 * 8 * 256), BF16, scope=scopeB)
    vwin = b.sbuf("lb_vwin", (128, 2 * 2 * 1040), BF16, scope=scopeB)
    P2 = [b.sbuf("lb_P2", (128, 1024), BF16, scope=scopeB) for _ in range(2)]
    den2 = [b.sbuf("lb_den2", (1, 256), F32, scope=scopeB) for _ in range(2)]
    rep2 = [b.sbuf("lb_rep2", (64, 256), F32, scope=scopeB) for _ in range(2)]
    ao2_sb = [b.sbuf("lb_ao2", (64, 256), BF16, scope=scopeB) for _ in range(2)]
    ps_qkv = [b.psum("lb_psqkv", (128, 512), scope=scopeB) for _ in range(2)]
    ps_S2 = [b.psum("lb_psS2", (128, 1024), scope=scopeB)]
    ps_av2 = [b.psum("lb_psav2", (128, 256), scope=scopeB) for _ in range(2)]
    ps_rep2 = ps_qkv[0]
    ones1 = b.const_ones(1, 512)
    ones64f = b.const_ones(1, 64, F32)
    def flw(g):
        for mi in range(8):
            sg.inc(g.dma_start(out=wqsb[:, mi * 8 * 128:(mi + 1) * 8 * 128],
                               in_=W["l1_wq"][mi]), 16)
            sg.inc(g.dma_start(out=wksb[:, mi * 8 * 128:(mi + 1) * 8 * 128],
                               in_=W["l1_wk"][mi]), 16)
        for kc in range(8):
            sg.inc(g.dma_start(out=wvsb[:, kc * 1024:(kc + 1) * 1024],
                               in_=W["l1_wv"][kc * 128:(kc + 1) * 128, :]), 16)
        sg.inc(g.dma_start(out=bqkvsb[:, :], in_=W["l1_bqkv"]), 16)
        wm = W["wmask"].rearrange("f k p c -> (f k) p c")
        for a in range(8):
            sg.inc(g.dma_start(out=masksb[:, a * 256:(a + 1) * 256], in_=wm[a]), 16)
    b.sec("gpsimd", flw)
    def fvv(v):
        sv.inc(v.memset(vwin[:, :], 1.0), 1)
    b.sec("vector", fvv)
    b.fence()

    gQKV = [None, None]
    g_headT = [None]
    g_headV = [None]
    qkv_done = [None]

    def qkv_window(col0, n, par, do_q):
        parts = [("k", wksb, D, kwin, par)]
        if do_q:
            parts.append(("q", wqsb, 0, qwin, par))
        for (which, wsb, boff, dst, par_) in parts:
            ST0 = st.v
            SV0 = sv.v
            def ft(t, col0=col0, n=n, wsb=wsb, boff=boff, SV0=SV0,
                   gq=tuple(gQKV), gh=g_headT[0], ghv=g_headV[0]):
                if gh:
                    t.wait_ge(gh[0], gh[1])
                if ghv:
                    t.wait_ge(ghv[0], ghv[1])
                for mi in range(8):
                    pi = mi % 2
                    if mi >= 2:
                        t.wait_ge(sv.h, SV0 + mi - 1)
                    elif gq[pi]:
                        t.wait_ge(gq[pi][0], gq[pi][1])
                    for kc in range(8):
                        t.matmul(ps_qkv[pi][:, 0:n],
                                 wsb[:, (mi * 8 + kc) * 128:(mi * 8 + kc + 1) * 128],
                                 xn_full[:, kc * ncol + col0:kc * ncol + col0 + n],
                                 start=(kc == 0), stop=False)
                    st.inc(t.matmul(ps_qkv[pi][:, 0:n],
                                    bqkvsb[:, boff + mi * 128:boff + (mi + 1) * 128],
                                    ones1[:, 0:n], start=False, stop=True), 1)
            b.sec("tensor", ft)
            def fv(v, n=n, dst=dst, par_=par_, ST0=ST0, iskw=(which == "k")):
                for mi in range(8):
                    o = (par_ * 8 + mi) * 256 if iskw else mi * 256
                    v.wait_ge(st.h, ST0 + mi + 1)
                    sv.inc(v.tensor_copy(dst[:, o:o + n],
                                         ps_qkv[mi % 2][:, 0:n]), 1)
            b.sec("vector", fv)
            gQKV[0] = (sv.h, SV0 + 7)
            gQKV[1] = (sv.h, SV0 + 8)
        ST0 = st.v
        SV0 = sv.v
        combos = [(tt, vb) for tt in range(2) for vb in range(2)]
        def ftv(t, col0=col0, SV0=SV0, gq=tuple(gQKV)):
            for j, (tt, vb) in enumerate(combos):
                pi = vb % 2
                if j >= 2:
                    t.wait_ge(sv.h, SV0 + j - 1)
                elif gq[pi]:
                    t.wait_ge(gq[pi][0], gq[pi][1])
                for kc in range(8):
                    t.matmul(ps_qkv[pi][:, :],
                             xn_full[:, kc * ncol + col0 + tt * 128:
                                     kc * ncol + col0 + (tt + 1) * 128],
                             wvsb[:, kc * 1024 + vb * 512:kc * 1024 + (vb + 1) * 512],
                             start=(kc == 0), stop=False)
                st.inc(t.matmul(ps_qkv[pi][:, :], ones1[0:1, 0:128],
                                bqkvsb[:, 2 * D + vb * 512:2 * D + (vb + 1) * 512],
                                start=False, stop=True), 1)
        b.sec("tensor", ftv)
        def fvv(v, par=par, ST0=ST0):
            for j, (tt, vb) in enumerate(combos):
                v.wait_ge(st.h, ST0 + j + 1)
                sv.inc(v.tensor_copy(
                    vwin[:, (par * 2 + tt) * 1040:(par * 2 + tt + 1) * 1040]
                    .rearrange("p (h c) -> p h c", h=16)[:, vb * 8:(vb + 1) * 8, 0:64],
                    ps_qkv[vb % 2][:, :].rearrange("p (h c) -> p h c", h=8)), 1)
        b.sec("vector", fvv)
        gQKV[0] = (sv.h, SV0 + 3)
        gQKV[1] = (sv.h, SV0 + 4)
        qkv_done[0] = sv.snap()

    qkv_window(tpc, HALO, 0, False)
    gS2W = None
    gAoW = None
    for w in range(nwin):
        par_cur = (w + 1) % 2
        par_prev = w % 2
        qkv_window(w * WINDOW, WINDOW, par_cur, True)
        first = (w == 0)
        moff = (4 * 256) if first else 0
        STB0 = st.v
        SSE0 = ss.v
        SVB0 = sv.v
        SGB0 = sg.v
        def ftH(t, par_cur=par_cur, par_prev=par_prev, qd=qkv_done[0],
                gS2W=gS2W, STB0=STB0, SSE0=SSE0, SVB0=SVB0):
            t.wait_ge(qd[0], qd[1])
            if gS2W:
                t.wait_ge(gS2W[0], gS2W[1])
            for h in range(17):
                if h < 16:
                    hp, hpar = h // 2, h % 2
                    if h >= 1:
                        t.wait_ge(ss.h, SSE0 + h)
                    for kc in range(4):
                        par = par_prev if kc < 2 else par_cur
                        koff = (par * 8 + hp) * 256 + (kc % 2) * 128
                        i = t.matmul(ps_S2[0][:, kc * 256:(kc + 1) * 256],
                                     kwin[hpar * 64:hpar * 64 + 64, koff:koff + 128],
                                     qwin[hpar * 64:hpar * 64 + 64,
                                          hp * 256:(hp + 1) * 256],
                                     start=True, stop=True)
                    st.inc(i, 1)
                if h >= 1:
                    j = h - 1
                    t.wait_ge(sv.h, SVB0 + 3 * j + 1)
                    for kc in range(4):
                        par = par_prev if kc < 2 else par_cur
                        vt = par * 2 + (kc % 2)
                        i = t.matmul(ps_av2[j % 2][0:65, 0:256],
                                     vwin[:, vt * 1040 + j * 65:vt * 1040 + j * 65 + 65],
                                     P2[j % 2][:, kc * 256:(kc + 1) * 256],
                                     start=(kc == 0), stop=(kc == 3))
                    st.inc(i, 1)
                    t.wait_ge(sv.h, SVB0 + 3 * j + 2)
                    st.inc(t.matmul(ps_rep2[0:64, 0:256], ones64f[0:1, 0:64],
                                    den2[j % 2][0:1, :], start=True, stop=True), 1)
        b.sec("tensor", ftH)
        def fsH(s, STB0=STB0):
            for h in range(16):
                s.wait_ge(st.h, STB0 + (1 if h == 0 else 3 * h - 1))
                ss.inc(s.activation(P2[h % 2][:, :], ps_S2[0][:, :],
                                    AF.Exp, scale=0.125), 1)
        b.sec("scalar", fsH)
        gS2W = (ss.h, SSE0 + 16)
        def fvH(v, moff=moff, STB0=STB0, SSE0=SSE0, SGB0=SGB0, gAoW=gAoW):
            for h in range(16):
                pa = h % 2
                v.wait_ge(ss.h, SSE0 + h + 1)
                for kc in range(4):
                    i = v.tensor_tensor(P2[pa][:, kc * 256:(kc + 1) * 256],
                                        P2[pa][:, kc * 256:(kc + 1) * 256],
                                        masksb[:, moff + kc * 256:moff + (kc + 1) * 256],
                                        Alu.mult)
                sv.inc(i, 1)
                v.wait_ge(st.h, STB0 + (3 * h + 3 if h <= 14 else 47))
                v.tensor_copy(den2[pa][0:1, :], ps_av2[pa][64:65, 0:256])
                sv.inc(v.reciprocal(den2[pa][0:1, :], den2[pa][0:1, :]), 1)
                v.wait_ge(st.h, STB0 + (3 * h + 4 if h <= 14 else 48))
                if h >= 2:
                    v.wait_ge(sg.h, SGB0 + 16 * (h - 1))
                elif gAoW:
                    v.wait_ge(gAoW[0], gAoW[1])
                v.tensor_copy(rep2[pa][:, :], ps_rep2[0:64, 0:256])
                sv.inc(v.tensor_tensor(ao2_sb[pa][:, :], ps_av2[pa][0:64, 0:256],
                                       rep2[pa][:, :], Alu.mult), 1)
        b.sec("vector", fvH)
        def fgH(g, w=w, SVB0=SVB0):
            for h in range(16):
                hp, hpar = h // 2, h % 2
                g.wait_ge(sv.h, SVB0 + 3 * h + 3)
                sg.inc(g.dma_start(
                    out=attnT_hbm[hp * 128 + hpar * 64:hp * 128 + hpar * 64 + 64,
                                  w * WINDOW:(w + 1) * WINDOW],
                    in_=ao2_sb[hpar][:, :]), 16)
        b.sec("gpsimd", fgH)
        gAoW = (sg.h, SGB0 + 16 * 16)
        g_headT[0] = (st.h, st.v)
        g_headV[0] = (sv.h, SVB0 + 3 * 15 + 3)
    b.fence()
    scopeB.close()

    scopeC = ExitStack()
    wgsb = b.sbuf("lb_wg", (128, 64 * 128), BF16, scope=scopeC)
    wdsb = b.sbuf("lb_wd", (128, 8 * 128), BF16, scope=scopeC)
    wd2sb = b.sbuf("lb_wd2", (128, 8 * 128), BF16, scope=scopeC)
    bgrow = b.sbuf("lb_bg", (1, D), BF16, scope=scopeC)
    burow = b.sbuf("lb_bu", (1, SDIM), BF16, scope=scopeC)
    drive = b.sbuf("lb_drive", (128, HALO + tpc), F32, scope=scopeC)
    onesbig = b.sbuf("lb_onesbig", (128, HALO + tpc), F32, scope=scopeC)
    ps_g = [b.psum("lb_psg", (128, 512), scope=scopeC) for _ in range(2)]
    def flc(g):
        for mi in range(8):
            sg.inc(g.dma_start(out=wgsb[:, mi * 8 * 128:(mi + 1) * 8 * 128],
                               in_=W["l1_wg"][mi]), 16)
        sg.inc(g.dma_start(out=wdsb[:, :], in_=W["l1_wd"][0]), 16)
        sg.inc(g.dma_start(out=wd2sb[:, :], in_=W["l1_wd2"][0]), 16)
        sg.inc(g.dma_start(out=bgrow[:, :], in_=W["l1_bg"]), 16)
        sg.inc(g.dma_start(out=burow[:, :], in_=W["l1_bu"]), 16)
    b.sec("gpsimd", flc)
    def fvc(v, sgv=sg.v):
        v.wait_ge(sg.h, sgv)
        sv.inc(v.memset(onesbig[:, :], 1.0), 1)
        i = v.tensor_tensor(wdsb[:, :], wdsb[:, :], wd2sb[:, :], Alu.add)
        sv.inc(i, 1)
    b.sec("vector", fvc)
    b.fence()
    nbl2 = blk_list(tpc, 512)
    combosG = [(mi, qo, nq) for mi in range(8) for (qo, nq) in nbl2]
    STC0 = st.v
    SSC0 = ss.v
    def ftG(t, combosG=combosG, SSC0=SSC0, ssv=ss.v):
        t.wait_ge(ss.h, ssv)
        for j, (mi, qo, nq) in enumerate(combosG):
            pi = j % 2
            if j >= 2:
                t.wait_ge(ss.h, SSC0 + j - 1)
            for kc in range(8):
                t.matmul(ps_g[pi][:, 0:nq],
                         wgsb[:, (mi * 8 + kc) * 128:(mi * 8 + kc + 1) * 128],
                         xn_full[:, kc * ncol + qo:kc * ncol + qo + nq],
                         start=(kc == 0), stop=False)
            st.inc(t.matmul(ps_g[pi][:, 0:nq], bgrow[:, mi * 128:(mi + 1) * 128],
                            ones1[:, 0:nq], start=False, stop=True), 1)
    b.sec("tensor", ftG)
    def fsG(s, combosG=combosG, STC0=STC0):
        for j, (mi, qo, nq) in enumerate(combosG):
            s.wait_ge(st.h, STC0 + j + 1)
            ss.inc(s.activation(g_sb[:, mi * tpc + qo:mi * tpc + qo + nq],
                                ps_g[j % 2][:, 0:nq], AF.Exp, scale=-1.0), 1)
    b.sec("scalar", fsG)
    def fvG(v, combosG=combosG, SSC0=SSC0):
        for j, (mi, qo, nq) in enumerate(combosG):
            v.wait_ge(ss.h, SSC0 + j + 1)
            sl = g_sb[:, mi * tpc + qo:mi * tpc + qo + nq]
            v.tensor_scalar(lns.xb[:, 0:nq], sl, 1.0, None, Alu.add)
            with b.nc.allow_low_precision(reason="sigmoid gate stored bf16"):
                sv.inc(v.reciprocal(sl, lns.xb[:, 0:nq]), 1)
    b.sec("vector", fvG)
    combosD = [(tpc, HALO, 0)] + [(w * 512, min(512, tpc - w * 512), HALO + w * 512)
                                  for w in range((tpc + 511) // 512)]
    STD0 = st.v
    SVD0 = sv.v
    nG = len(combosG)
    def ftD(t, combosD=combosD, SSC0=SSC0, nG=nG, SVD0=SVD0):
        t.wait_ge(ss.h, SSC0 + nG)
        for j, (c0, n, dof) in enumerate(combosD):
            pi = j % 2
            if j >= 2:
                t.wait_ge(sv.h, SVD0 + j - 1)
            for kc in range(8):
                t.matmul(ps_g[pi][:, 0:n],
                         wdsb[:, kc * 128:(kc + 1) * 128],
                         xn_full[:, kc * ncol + c0:kc * ncol + c0 + n],
                         start=(kc == 0), stop=False)
            st.inc(t.matmul(ps_g[pi][:, 0:n], burow[:, 0:SDIM],
                            ones1[:, 0:n], start=False, stop=True), 1)
    b.sec("tensor", ftD)
    def fvD2(v, combosD=combosD, STD0=STD0):
        for j, (c0, n, dof) in enumerate(combosD):
            v.wait_ge(st.h, STD0 + j + 1)
            sv.inc(v.tensor_copy(drive[:, dof:dof + n], ps_g[j % 2][:, 0:n]), 1)
    b.sec("vector", fvD2)
    def fvs(v, svv=sv.v):
        v.tensor_tensor(drive[:, 0:HALO], drive[:, 0:HALO], hgate[:, :], Alu.mult)
        v.tensor_scalar(onesbig[:, :], onesbig[:, :], acol[:, :], None, Alu.mult)
        sv.inc(v.tensor_tensor_scan(states[:, :], onesbig[:, :], drive[:, :], 0.0,
                                    Alu.mult, Alu.add), 1)
    b.sec("vector", fvs)
    states_bf = b.sbuf("lb_statesbf", (128, tpc), BF16, scope=scopeC)
    def fsc(s, svv=sv.v):
        s.wait_ge(sv.h, svv)
        ss.inc(s.copy(states_bf[:, :], states[:, HALO:]), 1)
    b.sec("scalar", fsc)
    b.fence()
    scopeC.close()
    scopeA.close()

    scopeD = ExitStack()
    wosb = b.sbuf("lb_wo", (128, 8 * 8 * 128), BF16, scope=scopeD)
    cwsb = b.sbuf("lb_cw", (128, D), BF16, scope=scopeD)
    borow = b.sbuf("lb_bo", (1, D), BF16, scope=scopeD)
    ao_sb = b.sbuf("lb_ao", (128, 512), F32, scope=scopeD)
    a_nb = b.sbuf("lb_anb", (128, 8 * 512), BF16, scope=scopeD)
    y0_nb = b.sbuf("lb_y0nb", (128, 8 * 512), F32, scope=scopeD)
    t2_sb = b.sbuf("lb_t2", (128, 512), F32, scope=scopeD)
    ps_o = [b.psum("lb_pso", (128, 512), scope=scopeD) for _ in range(2)]
    ps_y = [b.psum("lb_psy", (128, 512), scope=scopeD) for _ in range(2)]
    def fld(g):
        sg.inc(g.dma_start(out=cwsb[:, :], in_=W["l1_cw"]), 16)
        sg.inc(g.dma_start(out=borow[:, :], in_=W["l1_bo"]), 16)
        for mi in range(8):
            sg.inc(g.dma_start(out=wosb[:, mi * 1024:(mi + 1) * 1024],
                               in_=W["l1_wo"][mi]), 16)
    b.sec("gpsimd", fld)
    b.fence()
    g_buf = None
    g_anb = None
    for (qo, nq) in nbl2:
        SD0 = sd.v
        def fla(sy, qo=qo, nq=nq, ga=g_anb, gb=g_buf):
            if ga:
                sy.wait_ge(ga[0], ga[1])
            if gb:
                sy.wait_ge(gb[0], gb[1])
            for kc in range(8):
                sd.inc(sy.dma_start(out=a_nb[:, kc * 512:kc * 512 + nq],
                                    in_=attnT_hbm[kc * 128:(kc + 1) * 128, qo:qo + nq]), 16)
            for mi in range(8):
                sd.inc(sy.dma_start(out=y0_nb[:, mi * 512:mi * 512 + nq],
                                    in_=y0_hbm[mi * 128:(mi + 1) * 128, qo:qo + nq]), 16)
        b.sec("sync", fla)
        STD = st.v
        SVD = sv.v
        def ftDD(t, qo=qo, nq=nq, sgv=sg.v, sdv=sd.v, ssv=ss.v, SVD=SVD,
                 gps=(g_anb is not None)):
            t.wait_ge(sg.h, sgv)
            t.wait_ge(sd.h, sdv)
            t.wait_ge(ss.h, ssv)
            for mi in range(8):
                pi = mi % 2
                if mi >= 2:
                    t.wait_ge(sv.h, SVD + mi - 1)
                for kc in range(8):
                    t.matmul(ps_o[pi][:, 0:nq],
                             wosb[:, mi * 1024 + kc * 128:mi * 1024 + (kc + 1) * 128],
                             a_nb[:, kc * 512:kc * 512 + nq],
                             start=(kc == 0), stop=False)
                t.matmul(ps_o[pi][:, 0:nq], borow[:, mi * 128:(mi + 1) * 128],
                         ones1[:, 0:nq], start=False, stop=True)
                st.inc(t.matmul(ps_y[pi][:, 0:nq], cwsb[:, mi * 128:(mi + 1) * 128],
                                states_bf[:, qo:qo + nq], start=True, stop=True), 1)
        b.sec("tensor", ftDD)
        def fvDD(v, qo=qo, nq=nq, STD=STD):
            for mi in range(8):
                pi = mi % 2
                v.wait_ge(st.h, STD + mi + 1)
                v.tensor_copy(ao_sb[:, 0:nq], ps_o[pi][:, 0:nq])
                v.tensor_tensor(t2_sb[:, 0:nq], ao_sb[:, 0:nq], ps_y[pi][:, 0:nq],
                                Alu.subtract)
                v.tensor_tensor(t2_sb[:, 0:nq], t2_sb[:, 0:nq],
                                g_sb[:, mi * tpc + qo:mi * tpc + qo + nq], Alu.mult)
                v.tensor_tensor(y0_nb[:, mi * 512:mi * 512 + nq],
                                y0_nb[:, mi * 512:mi * 512 + nq], ps_y[pi][:, 0:nq],
                                Alu.add)
                sv.inc(v.tensor_tensor(y0_nb[:, mi * 512:mi * 512 + nq],
                                       y0_nb[:, mi * 512:mi * 512 + nq], t2_sb[:, 0:nq],
                                       Alu.add), 1)
        b.sec("vector", fvDD)
        def fgo(g, qo=qo, nq=nq, SVD=SVD):
            for mi in range(8):
                g.wait_ge(sv.h, SVD + mi + 1)
                sg.inc(g.dma_start(out=xin1_hbm[mi * 128:(mi + 1) * 128, qo:qo + nq],
                                   in_=y0_nb[:, mi * 512:mi * 512 + nq]), 16)
        b.sec("gpsimd", fgo)
        g_buf = sg.snap()
        g_anb = (st.h, st.v)
    b.fence()
    scopeD.close()
    scopeCD.close()

    scopeE = ExitStack()
    w1sb = b.sbuf("lb_w1", (128, 32 * 1024), BF16, scope=scopeE)
    w2sb = b.sbuf("lb_w2", (128, 8 * 4096), BF16, scope=scopeE)
    b1row = b.sbuf("lb_b1", (1, 4096), BF16, scope=scopeE)
    b2row = b.sbuf("lb_b2", (1, D), BF16, scope=scopeE)
    h1_sb = b.sbuf("lb_h1", (128, 32 * NB), BF16, scope=scopeE)
    x6_nb = b.sbuf("lb_x6nb", (128, NB), F32, scope=scopeE)
    ps_m = [b.psum("lb_psm", (128, NB), scope=scopeE) for _ in range(2)]
    def fle(g):
        sg.inc(g.dma_start(out=b1row[:, :], in_=W["l1_b1"]), 16)
        sg.inc(g.dma_start(out=b2row[:, :], in_=W["l1_b2"]), 16)
        for m1 in range(32):
            sg.inc(g.dma_start(out=w1sb[:, m1 * 1024:(m1 + 1) * 1024],
                               in_=W["l1_w1"][m1]), 16)
        for mi in range(8):
            sg.inc(g.dma_start(out=w2sb[:, mi * 4096:(mi + 1) * 4096],
                               in_=W["l1_w2"][mi]), 16)
    b.sec("gpsimd", fle)
    lns.guard = None
    g_h1 = [None, None]
    g_m = [None, None]
    g_x6 = None
    for (off, n) in blk_list(tpc, NB):
        lns.block(xin1_hbm, tpc, off, n, gain_col=ln2g, bias_col=ln2b)
        for m1 in range(32):
            pi = m1 % 2
            def ft(t, m1=m1, n=n, pi=pi, sgv=sg.v, svv=sv.v, gp=g_m[pi]):
                t.wait_ge(sg.h, sgv)
                t.wait_ge(sv.h, svv)
                if gp:
                    t.wait_ge(gp[0], gp[1])
                for kc in range(8):
                    t.matmul(ps_m[pi][:, 0:n],
                             w1sb[:, m1 * 1024 + kc * 128:m1 * 1024 + (kc + 1) * 128],
                             lns.xn[:, kc * NB:kc * NB + n],
                             start=(kc == 0), stop=False)
                st.inc(t.matmul(ps_m[pi][:, 0:n], b1row[:, m1 * 128:(m1 + 1) * 128],
                                ones1[:, 0:n], start=False, stop=True), 1)
            b.sec("tensor", ft)
            g_h1[0] = (st.h, st.v)
            def fs(s, m1=m1, n=n, pi=pi, stv=st.v):
                s.wait_ge(st.h, stv)
                ss.inc(s.activation(h1_sb[:, m1 * NB:m1 * NB + n], ps_m[pi][:, 0:n],
                                    AF.Gelu), 1)
            b.sec("scalar", fs)
            g_m[pi] = ss.snap()
        for mi in range(8):
            pi = mi % 2
            def fx(sy, mi=mi, off=off, n=n, gx=g_x6):
                if gx:
                    sy.wait_ge(gx[0], gx[1])
                sd.inc(sy.dma_start(out=x6_nb[:, 0:n],
                                    in_=x6_hbm[mi * 128:(mi + 1) * 128, off:off + n]), 16)
            b.sec("sync", fx)
            def ft(t, mi=mi, n=n, pi=pi, sgv=sg.v, ssv=ss.v, gp=g_m[pi]):
                t.wait_ge(sg.h, sgv)
                t.wait_ge(ss.h, ssv)
                if gp:
                    t.wait_ge(gp[0], gp[1])
                for kc in range(32):
                    t.matmul(ps_m[pi][:, 0:n],
                             w2sb[:, mi * 4096 + kc * 128:mi * 4096 + (kc + 1) * 128],
                             h1_sb[:, kc * NB:kc * NB + n],
                             start=(kc == 0), stop=False)
                st.inc(t.matmul(ps_m[pi][:, 0:n], b2row[:, mi * 128:(mi + 1) * 128],
                                ones1[:, 0:n], start=False, stop=True), 1)
            b.sec("tensor", ft)
            g_h1[1] = (st.h, st.v)
            def fv(v, mi=mi, off=off, n=n, pi=pi, stv=st.v, sdv=sd.v):
                v.wait_ge(st.h, stv)
                v.wait_ge(sd.h, sdv)
                v.tensor_tensor(x6_nb[:, 0:n], x6_nb[:, 0:n], ps_m[pi][:, 0:n], Alu.add)
                sv.inc(v.tensor_tensor(x6_nb[:, 0:n], x6_nb[:, 0:n],
                                       lns.x_nb[:, mi * NB:mi * NB + n], Alu.add), 1)
            b.sec("vector", fv)
            g_m[pi] = sv.snap()
            def fgo(g, mi=mi, off=off, n=n, svv=sv.v):
                g.wait_ge(sv.h, svv)
                sg.inc(g.dma_start(out=x7_hbm[mi * 128:(mi + 1) * 128, off:off + n],
                                   in_=x6_nb[:, 0:n]), 16)
            b.sec("gpsimd", fgo)
            g_x6 = sg.snap()
        lns.guard = (st.h, st.v)
    b.fence()
    scopeE.close()


def build_la7_program(cfg):
    b = B(cfg)
    NCOL, TPC = cfg.NCOL, cfg.TPC
    x6c = b.dram_in("x6T_cols", (D, NCOL))
    W = {
        "ln1g_col": b.dram_in("ln1g_col", (128, 8)),
        "ln1b_col": b.dram_in("ln1b_col", (128, 8)),
        "ln2g_col": b.dram_in("ln2g_col", (128, 8)),
        "ln2b_col": b.dram_in("ln2b_col", (128, 8)),
        "A_col": b.dram_in("A_col", (128, 1)),
        "halo_gate": b.dram_in("halo_gate", (128, HALO)),
        "wmask": b.dram_in("wmask", (2, 4, 128, 256)),
        "l1_wq": b.dram_in("l1_wq", (8, 128, 8 * 128)),
        "l1_wk": b.dram_in("l1_wk", (8, 128, 8 * 128)),
        "l1_wv": b.dram_in("l1_wv", (D, D)),
        "l1_bqkv": b.dram_in("l1_bqkv", (1, 3 * D)),
        "l1_wo": b.dram_in("l1_wo", (8, 128, 8 * 128)),
        "l1_bo": b.dram_in("l1_bo", (1, D)),
        "l1_wg": b.dram_in("l1_wg", (8, 128, 8 * 128)),
        "l1_bg": b.dram_in("l1_bg", (1, D)),
        "l1_wd": b.dram_in("l1_wd", (1, 128, 8 * 128)),
        "l1_wd2": b.dram_in("l1_wd2", (1, 128, 8 * 128)),
        "l1_bu": b.dram_in("l1_bu", (1, SDIM)),
        "l1_cw": b.dram_in("l1_cw", (SDIM, D)),
        "l1_w1": b.dram_in("l1_w1", (32, 128, 8 * 128)),
        "l1_b1": b.dram_in("l1_b1", (1, 4096)),
        "l1_w2": b.dram_in("l1_w2", (8, 128, 32 * 128)),
        "l1_b2": b.dram_in("l1_b2", (1, D)),
    }
    x7c = b.dram_out("x7T_own", (D, TPC))
    b.start()
    emit_l1_attn_b(b, x6c, x7c, W, NCOL, TPC)
    def fo(g):
        for dn in ("gdma", "sdma"):
            sm = b.sem(dn)
            g.wait_ge(sm.h, sm.v)
    b.sec("gpsimd", fo)
    return b.finish()


def prep_la7_inputs(inputs, cfg, core):
    q = core % 4
    wqkv = _np(inputs["L1_attn_b_W_qkv"])
    wug = _np(inputs["L1_attn_b_W_ug"])
    mask = np.zeros((2, 4, 128, 256), np.float32)
    qi = np.arange(256)[None, :]
    for kc in range(4):
        p = np.arange(128)[:, None]
        if kc < 2:
            mask[0, kc] = ((kc * 128 + p) > qi).astype(np.float32)
        else:
            mask[0, kc] = (((kc - 2) * 128 + p) <= qi).astype(np.float32)
    mask[1] = mask[0]
    if q == 0:
        mask[1, 0] = 0.0
        mask[1, 1] = 0.0
    hg = np.ones((128, HALO), np.float32) if q > 0 else np.zeros((128, HALO), np.float32)
    col = lambda a: np.ascontiguousarray(_np(a).reshape(8, 128).T)
    return {
        "ln1g_col": col(inputs["L1_attn_b_ln1_g"]),
        "ln1b_col": col(inputs["L1_attn_b_ln1_b"]),
        "ln2g_col": col(inputs["L1_attn_b_ln2_g"]),
        "ln2b_col": col(inputs["L1_attn_b_ln2_b"]),
        "A_col": _np(inputs["L1_attn_b_A"]).reshape(128, 1),
        "halo_gate": hg,
        "wmask": mask,
        "l1_wq": _blockify_lhs(wqkv[:, 0:D]),
        "l1_wk": _blockify_lhs(wqkv[:, D:2 * D]),
        "l1_wv": np.ascontiguousarray(wqkv[:, 2 * D:]),
        "l1_bqkv": np.zeros((1, 3 * D), np.float32),
        "l1_wo": _blockify_lhs(_np(inputs["L1_attn_b_W_O"])),
        "l1_bo": _np(inputs["L1_attn_b_b_O"]).reshape(1, D),
        "l1_wg": _blockify_lhs(wug[:, SDIM:]),
        "l1_bg": _np(inputs["L1_attn_b_b_ug"])[SDIM:].reshape(1, D).copy(),
        "l1_wd": _blockify_lhs(_np(inputs["L1_attn_b_B_w"])),
        "l1_wd2": _blockify_lhs(wug[:, 0:SDIM]),
        "l1_bu": _np(inputs["L1_attn_b_b_ug"])[0:SDIM].reshape(1, SDIM).copy(),
        "l1_cw": _np(inputs["L1_attn_b_C_w"]),
        "l1_w1": _blockify_lhs(_np(inputs["L1_attn_b_mlp_W1"])),
        "l1_b1": _np(inputs["L1_attn_b_mlp_b1"]).reshape(1, 4096),
        "l1_w2": _blockify_lhs(_np(inputs["L1_attn_b_mlp_W2"])),
        "l1_b2": _np(inputs["L1_attn_b_mlp_b2"]).reshape(1, D),
    }



class Cfg:
    def __init__(self, TPC=2048):
        assert TPC % WINDOW == 0
        self.TPC = TPC
        self.Tb = 4 * TPC
        self.NCOL = TPC + EX
        self.NB = 256



def build_la1_program(cfg, dbg_stop=None):
    b = B(cfg)
    x0T = b.dram_in("x0T_full", (D, cfg.Tb))
    hT = b.dram_out("hT_full", (D, cfg.Tb))
    b.start()
    emit_la1(b, x0T, hT, dbg_stop=dbg_stop)
    return b.finish()


def prep_la1_inputs(inputs, batch):
    ig_b = _np(inputs["L0_attn_a_ig_b"]).reshape(1, D)
    ag_b = _np(inputs["L0_attn_a_ag_b"]).reshape(1, D)
    return {
        "ig_w": _np(inputs["L0_attn_a_ig_w"]),
        "ag_w": _np(inputs["L0_attn_a_ag_w"]),
        "ig_bias_row": ig_b,
        "ag_bias_row": ag_b,
        "a_param_col": np.ascontiguousarray(_np(inputs["L0_attn_a_a_param"]).reshape(8, 128).T),
    }


def _run(nc, in_maps):
    return run_bass_kernel_spmd(nc, in_maps, core_ids=list(range(8)))


def build_la8_program(cfg):
    b = B(cfg)
    TPC = cfg.TPC
    xsrc = b.dram_in("x7T_own", (D, TPC))
    selblob = b.dram_in("sel_blob", (8, 8, 128, 1024))
    selb = b.dram_in("selb_rows", (8, 1, D))
    compblob = b.dram_in("comp_blob", (8, 8, 128, 1024))
    compb = b.dram_in("compb_rows", (8, 1, D))
    xdst = b.dram_out("x8T_own", (D, TPC))
    b.start()
    emit_l1_ffn(b, xsrc, xdst, selblob, selb, compblob, compb, TPC)
    def fo(g):
        for dn in ("gdma", "sdma"):
            sm = b.sem(dn)
            g.wait_ge(sm.h, sm.v)
    b.sec("gpsimd", fo)
    return b.finish()


class APView:
    def __init__(self, ap, axis, shift):
        self.ap, self.axis, self.shift = ap, axis, shift

    def __getitem__(self, idx):
        if not isinstance(idx, tuple):
            idx = (idx,)
        idx = list(idx)
        i = idx[self.axis]
        if isinstance(i, slice):
            idx[self.axis] = slice(i.start + self.shift, i.stop + self.shift)
        else:
            idx[self.axis] = i + self.shift
        return self.ap[tuple(idx)]


def emit_la2_stage(b, x0c, hc, x2c, qTo, khp_out, vhg_out, cosin, sh, kv_only=False):
    TPC, NB = b.cfg.TPC, b.cfg.NB
    ncol = TPC if kv_only else b.cfg.NCOL
    emit_resadd_hbm(b, x2c, x0c, hc, ncol)
    b.fence()
    emit_ffn0_stage(b, x2c, x2c, sh["cpblob"], sh["w1blob"], sh["w2blob"], ncol)
    lns = LnStream(b, NB)
    blks = blk_list(ncol, NB)
    scope = ExitStack()
    qw = mk_qkv_state(b, scope, sh["wq"], sh["wk"], sh["wv"], sh["bq"], sh["bk"],
                      sh["bv"], cosin, sh["perm"], qTo, khp_out, vhg_out, ncol)
    b.fence()
    lns.guard = None
    for (off, n) in blks:
        lns.block(x2c, ncol, off, n)
        emit_qkv_block(b, lns, qw, off, n, TPC, ncol, do_q=not kv_only)
        lns.guard = b.sem("gdma").snap()
    b.fence()
    scope.close()


def build_fused_program(cfg):
    b = B(cfg)
    NCOL, TPC, NB, Tb = cfg.NCOL, cfg.TPC, cfg.NB, cfg.Tb
    nkt_q = TPC // 128
    x0T_full = b.dram_in("x0T_full", (D, Tb))
    x0c_s = [b.dram_in(f"x0c_s{s}", (D, NCOL if s == 0 else TPC)) for s in range(4)]
    cosin_s = [b.dram_in(f"cosin_s{s}", (2, 128, NCOL if s == 0 else TPC), BF16)
               for s in range(4)]
    sh = {
        "w1blob": b.dram_in("w1blob", (128, 32 * 128), BF16),
        "w2blob": b.dram_in("w2blob", (128, 128 * 128), BF16),
        "cpblob": b.dram_in("cpblob", (8, 128, 128 * 128), BF16),
        "wq": b.dram_in("wq_blob", (8, 128, 8 * 128), BF16),
        "wk": b.dram_in("wk_blob", (8, 128, 8 * 128), BF16),
        "wv": b.dram_in("wv_nat", (D, D), BF16),
        "bq": b.dram_in("bq_row", (1, D), BF16),
        "bk": b.dram_in("bk_row", (1, D), BF16),
        "bv": b.dram_in("bv_row", (1, D), BF16),
        "perm": b.dram_in("rope_perm", (128, 128), BF16),
    }
    wout = b.dram_in("wout_blob", (8, 128, 8 * 128), BF16)
    bout = b.dram_in("bout_row", (1, D), BF16)
    kT = b.dram_in("l1a_kT", (D, KLEN), BF16)
    vn = b.dram_in("l1a_v", (KLEN, D), BF16)
    selblob = b.dram_in("sel_blob", (8, 8, 128, 1024), BF16)
    selb = b.dram_in("selb_rows", (8, 1, D), BF16)
    compblob = b.dram_in("comp_blob", (8, 8, 128, 1024), BF16)
    compb = b.dram_in("compb_rows", (8, 1, D), BF16)
    W7 = {
        "ln1g_col": b.dram_in("ln1g_col", (128, 8)),
        "ln1b_col": b.dram_in("ln1b_col", (128, 8)),
        "ln2g_col": b.dram_in("ln2g_col", (128, 8)),
        "ln2b_col": b.dram_in("ln2b_col", (128, 8)),
        "A_col": b.dram_in("A_col", (128, 1)),
        "halo_gate": b.dram_in("halo_gate", (128, HALO)),
        "wmask": b.dram_in("wmask", (2, 4, 128, 256), BF16),
        "l1_wq": b.dram_in("l1_wq", (8, 128, 8 * 128), BF16),
        "l1_wk": b.dram_in("l1_wk", (8, 128, 8 * 128), BF16),
        "l1_wv": b.dram_in("l1_wv", (D, D), BF16),
        "l1_bqkv": b.dram_in("l1_bqkv", (1, 3 * D), BF16),
        "l1_wo": b.dram_in("l1_wo", (8, 128, 8 * 128), BF16),
        "l1_bo": b.dram_in("l1_bo", (1, D), BF16),
        "l1_wg": b.dram_in("l1_wg", (8, 128, 8 * 128), BF16),
        "l1_bg": b.dram_in("l1_bg", (1, D), BF16),
        "l1_wd": b.dram_in("l1_wd", (1, 128, 8 * 128), BF16),
        "l1_wd2": b.dram_in("l1_wd2", (1, 128, 8 * 128), BF16),
        "l1_bu": b.dram_in("l1_bu", (1, SDIM), BF16),
        "l1_cw": b.dram_in("l1_cw", (SDIM, D), BF16),
        "l1_w1": b.dram_in("l1_w1", (32, 128, 8 * 128), BF16),
        "l1_b1": b.dram_in("l1_b1", (1, 4096), BF16),
        "l1_w2": b.dram_in("l1_w2", (8, 128, 32 * 128), BF16),
        "l1_b2": b.dram_in("l1_b2", (1, D), BF16),
    }
    hT_full = b.dram_tmp("f_hT", (D, Tb))
    hc_s = [b.dram_tmp(f"f_hc{s}", (D, NCOL if s == 0 else TPC)) for s in range(4)]
    x2c_s = [b.dram_tmp(f"f_x2c{s}", (D, NCOL if s == 0 else TPC)) for s in range(4)]
    qT_s = [b.dram_tmp("f_qT0", (D, NCOL), BF16)]
    khp_full = b.dram_tmp("f_khp", (8, 128, Tb), BF16)
    vhg_full = b.dram_tmp("f_vhg", (4, 128, Tb // 128, 260), BF16)
    x3c = b.dram_tmp("f_x3c", (D, NCOL))
    x4c = b.dram_tmp("f_x4c", (D, NCOL))
    x5c = b.dram_tmp("f_x5c", (D, NCOL))
    x6c = b.dram_tmp("f_x6c", (D, NCOL))
    x7c = b.dram_tmp("f_x7c", (D, TPC))
    x8o = b.dram_out("x8T_own", (D, TPC))
    b.start()
    with b.stage():
        emit_la1(b, x0T_full, hT_full)
    b.fence()
    sg = b.sem("gdma")
    n_gather = 0
    def fgath(g):
        for arm in g.switch_core_id(n=128):
            q = arm.logical % 8 % 4
            order = [q] + [r for r in range(4) if r != q]
            for s, r in enumerate(order):
                dst = hc_s[s]
                r0 = r * TPC
                g.dma_start(out=dst[:, 0:TPC],
                            in_=hT_full[:, r0:r0 + TPC]).then_inc(sg.h, 16)
                if s == 0:
                    h0 = r0 - HALO if r > 0 else 0
                    g.dma_start(out=dst[:, TPC:TPC + HALO],
                                in_=hT_full[:, h0:h0 + HALO]).then_inc(sg.h, 16)
                    for j in range(4):
                        g.dma_start(out=dst[:, TPC + HALO + j:TPC + HALO + j + 1],
                                    in_=hT_full[:, j * TPC:j * TPC + 1],
                                    allow_slow_non_contiguous=True).then_inc(sg.h, 16)
    b.sec("gpsimd", fgath)
    sg.v += 9 * 16
    b.fence()
    for s in range(4):
        with b.stage():
            emit_la2_stage(b, x0c_s[s], hc_s[s], x2c_s[s], qT_s[0] if s == 0 else None,
                           APView(khp_full, 2, s * TPC),
                           APView(vhg_full, 2, s * nkt_q),
                           cosin_s[s], sh, kv_only=(s > 0))
        b.fence()
    with b.stage():
        emit_l0_attention(b, qT_s[0], khp_full, vhg_full, wout, bout,
                          x2c_s[0], x3c, NCOL, Tb)
    b.fence()
    with b.stage():
        emit_ffn0_stage(b, x3c, x4c, sh["cpblob"], sh["w1blob"], sh["w2blob"], NCOL)
    b.fence()
    with b.stage():
        emit_l1_attn_a(b, x4c, x5c, kT, vn, NCOL, TPC)
    b.fence()
    with b.stage():
        emit_l1_ffn(b, x5c, x6c, selblob, selb, compblob, compb, NCOL)
    b.fence()
    with b.stage():
        emit_l1_attn_b(b, x6c, x7c, W7, NCOL, TPC)
    b.fence()
    with b.stage():
        emit_l1_ffn(b, x7c, x8o, selblob, selb, compblob, compb, TPC)
    def fo(g):
        for dn in ("gdma", "sdma"):
            sm = b.sem(dn)
            g.wait_ge(sm.h, sm.v)
    b.sec("gpsimd", fo)
    return b.finish()


def prep_fused_inputs(inputs, cfg, core):
    b_ = core // 4
    q = core % 4
    order = [q] + [r for r in range(4) if r != q]
    x0 = _np(inputs["L0_attn_a_x"])[:, :cfg.Tb]
    x0T = np.ascontiguousarray(x0[b_].T)
    m = {}
    m.update(prep_la1_inputs(inputs, b_))
    m["x0T_full"] = x0T
    la2b = prep_la2_inputs(inputs, cfg, 0)
    for k2 in ("w1blob", "w2blob", "cpblob", "wq_blob", "wk_blob", "wv_nat",
               "bq_row", "bk_row", "bv_row", "rope_perm"):
        m[k2] = la2b[k2]
    cos2, sin2, _ = _rope_tables(cfg)
    for s, r in enumerate(order):
        if s == 0:
            cols = _col_index(cfg, r)
        else:
            cols = np.arange(r * cfg.TPC, (r + 1) * cfg.TPC)
        m[f"x0c_s{s}"] = np.ascontiguousarray(x0T[:, cols])
        m[f"cosin_s{s}"] = np.stack([np.ascontiguousarray(cos2[:, cols]),
                                     np.ascontiguousarray(sin2[:, cols])])
    m.update(prep_la3_inputs(inputs, cfg))
    m["l1a_kT"] = np.ascontiguousarray(_np(inputs["L1_attn_a_k"])[0].T)
    m["l1a_v"] = np.ascontiguousarray(_np(inputs["L1_attn_a_v"])[0])
    m.update(prep_l1ffn_inputs(inputs, cfg))
    m.update(prep_la7_inputs(inputs, cfg, core))
    for k in _BF16_KEYS:
        m[k] = np.ascontiguousarray(m[k]).astype(ml_dtypes.bfloat16)
    return m


_BF16_KEYS = (
    "cosin_s0", "cosin_s1", "cosin_s2", "cosin_s3",
    "w1blob", "w2blob", "cpblob", "wq_blob", "wk_blob", "wv_nat",
    "bq_row", "bk_row", "bv_row", "rope_perm",
    "wout_blob", "bout_row", "l1a_kT", "l1a_v",
    "sel_blob", "selb_rows", "comp_blob", "compb_rows",
    "wmask", "l1_wq", "l1_wk", "l1_wv", "l1_bqkv", "l1_wo", "l1_bo",
    "l1_wg", "l1_bg", "l1_wd", "l1_wd2", "l1_bu", "l1_cw",
    "l1_w1", "l1_b1", "l1_w2", "l1_b2",
)


def kernel(**inputs):
    cfg = Cfg(2048)
    nc = build_fused_program(cfg)
    ins = [prep_fused_inputs(inputs, cfg, c) for c in range(8)]
    r = _run(nc, ins)
    out = np.empty((2, cfg.Tb, D), np.float32)
    for c in range(8):
        b_, qq = c // 4, c % 4
        out[b_, qq * cfg.TPC:(qq + 1) * cfg.TPC, :] = np.asarray(r.results[c]["x8T_own"]).T
    return out


def _kernel_unfused(**inputs):
    TPC = 2048
    cfg = Cfg(TPC)
    NCOL, Tb = cfg.NCOL, cfg.Tb
    x0 = _np(inputs["L0_attn_a_x"])
    cols = [_col_index(cfg, c) for c in range(8)]

    nc = build_la1_program(cfg)
    base1 = prep_la1_inputs(inputs, 0)
    x0T = [np.ascontiguousarray(x0[b_].T) for b_ in range(2)]
    ins = [dict(base1, x0T_full=x0T[c // 4]) for c in range(8)]
    r1 = _run(nc, ins)
    hT = [np.asarray(r1.results[0]["hT_full"]), np.asarray(r1.results[4]["hT_full"])]

    nc = build_la2_program(cfg)
    ins = []
    for c in range(8):
        m = prep_la2_inputs(inputs, cfg, c)
        m["x0T_cols"] = np.ascontiguousarray(x0T[c // 4][:, cols[c]])
        m["hT_cols"] = np.ascontiguousarray(hT[c // 4][:, cols[c]])
        ins.append(m)
    r2 = _run(nc, ins)
    x2c = [np.asarray(r2.results[c]["x2T_cols"]) for c in range(8)]
    qTc = [np.asarray(r2.results[c]["qT_cols"]) for c in range(8)]
    khp_full = [np.concatenate([np.asarray(r2.results[b_ * 4 + qq]["khp_own"])
                                for qq in range(4)], axis=2) for b_ in range(2)]
    vhg_full = [np.concatenate([np.asarray(r2.results[b_ * 4 + qq]["vhg_own"])
                                for qq in range(4)], axis=2) for b_ in range(2)]

    nc = build_la3_program(cfg)
    base3 = prep_la3_inputs(inputs, cfg)
    ins = []
    for c in range(8):
        m = dict(base3)
        m["qT_cols"] = qTc[c]
        m["khp_full"] = khp_full[c // 4]
        m["vhg_full"] = vhg_full[c // 4]
        m["x2T_cols"] = x2c[c]
        ins.append(m)
    r3 = _run(nc, ins)
    x3c = [np.asarray(r3.results[c]["x3T_cols"]) for c in range(8)]

    nc = build_la4_program(cfg)
    base2 = prep_la2_inputs(inputs, cfg, 0)
    wkeys = ("w1blob", "w2blob", "cpblob")
    ins = [dict({k: base2[k] for k in wkeys}, x3T_cols=x3c[c]) for c in range(8)]
    r4 = _run(nc, ins)
    x4c = [np.asarray(r4.results[c]["x4T_cols"]) for c in range(8)]

    nc = build_la5_program(cfg)
    kT = np.ascontiguousarray(_np(inputs["L1_attn_a_k"])[0].T)
    vn = np.ascontiguousarray(_np(inputs["L1_attn_a_v"])[0])
    ins = [{"l1a_kT": kT, "l1a_v": vn, "x4T_cols": x4c[c]} for c in range(8)]
    r5 = _run(nc, ins)
    x5c = [np.asarray(r5.results[c]["x5T_cols"]) for c in range(8)]

    nc = build_la6_program(cfg)
    basef = prep_l1ffn_inputs(inputs, cfg)
    ins = [dict(basef, x5T_cols=x5c[c]) for c in range(8)]
    r6 = _run(nc, ins)
    x6c = [np.asarray(r6.results[c]["x6T_cols"]) for c in range(8)]

    nc = build_la7_program(cfg)
    ins = []
    for c in range(8):
        m = prep_la7_inputs(inputs, cfg, c)
        m["x6T_cols"] = x6c[c]
        ins.append(m)
    r7 = _run(nc, ins)
    x7c = [np.asarray(r7.results[c]["x7T_own"]) for c in range(8)]

    nc = build_la8_program(cfg)
    ins = [dict(basef, x7T_own=x7c[c]) for c in range(8)]
    r8 = _run(nc, ins)

    out = np.empty((2, Tb, D), np.float32)
    for c in range(8):
        b_, q = c // 4, c % 4
        out[b_, q * TPC:(q + 1) * TPC, :] = np.asarray(r8.results[c]["x8T_own"]).T
    return out
